# version 10
# baseline (speedup 1.0000x reference)
import numpy as np
import ml_dtypes
import concourse.bass as bass
import concourse.mybir as mybir
from concourse.bass_utils import run_bass_kernel_spmd
from contextlib import ExitStack

F32 = mybir.dt.float32
BF16 = mybir.dt.bfloat16
I8 = mybir.dt.int8
AF = mybir.ActivationFunctionType
ALU = mybir.AluOpType
AX = mybir.AxisListType.X

NCORES = 8
T = 1024
NT = 8
D = 2048
KC = 16
H = 16
F = 8192
SEQ = 8192
EPS_ATT = 1e-6
EPS_LN = 1e-5
EPS_Q = 1e-5

ENGS = ("tensor", "vector", "scalar", "sync", "gpsimd")
TICK = {"tensor": "pe", "vector": "dve", "scalar": "act", "gpsimd": "pool"}


class _Comp:
    __slots__ = ("sem", "inc", "needed", "val", "idx")

    def __init__(self, sem, inc, idx, needed=False):
        self.sem, self.inc, self.needed, self.val, self.idx = sem, inc, needed, None, idx


class _DmaComp:
    __slots__ = ("q", "total", "tick")

    def __init__(self, q, total):
        self.q, self.total, self.tick = q, total, None


QENG = {"qs": "sync", "qa": "scalar", "qg": "gpsimd"}
QTICK = {"qs": "fns", "qa": "fna", "qg": "fng"}


def _qsem(dma):
    q, par = dma
    return q + str(par & 1)


class Bld:

    def __init__(self):
        self.plan = {e: [] for e in ENGS}
        self.writer = {}
        self.readers = {}
        self.water = {e: {} for e in ENGS}
        self.nidx = 0
        self.qissued = {}
        self.qfence = {}

    def _fence(self, dc):
        covered, tick = self.qfence.get(dc.q, (0, None))
        if tick is not None and covered >= dc.total:
            return tick
        qeng = QENG[dc.q[:2]]
        self.nidx += 1
        w = _Comp(dc.q, 16, self.nidx, needed=True)
        w.val = 16 * self.qissued[dc.q]
        self.plan[qeng].append(("W", w))
        self.nidx += 1
        tick = _Comp(QTICK[dc.q[:2]], 1, self.nidx, needed=True)
        self.plan[qeng].append(("o", None, tick))
        self.qfence[dc.q] = (self.qissued[dc.q], tick)
        return tick

    def _dep(self, deps, comp):
        if comp is None:
            return
        if isinstance(comp, _DmaComp):
            comp = self._fence(comp)
        if comp.sem not in deps or comp.idx > deps[comp.sem].idx:
            deps[comp.sem] = comp

    def _waits(self, eng, deps):
        wm = self.water[eng]
        for sem in sorted(deps):
            c = deps[sem]
            if wm.get(sem) is None or c.idx > wm[sem].idx:
                c.needed = True
                self.plan[eng].append(("w", c))
                wm[sem] = c

    def op(self, eng, fn, reads=(), writes=(), wcheck=(), wpub=(), dma=None, comp_sem=None):
        deps = {}
        for t in reads:
            self._dep(deps, self.writer.get(t))
        for t in tuple(writes) + tuple(wcheck):
            self._dep(deps, self.writer.get(t))
            for r in self.readers.get(t, ()):
                self._dep(deps, r)
        self._waits(eng, deps)
        self.nidx += 1
        if comp_sem is not None:
            comp = _Comp(comp_sem[0], comp_sem[1], self.nidx, needed=True)
        elif dma is not None:
            sem = _qsem(dma) if isinstance(dma, tuple) else dma + "0"
            self.qissued[sem] = self.qissued.get(sem, 0) + 1
            comp = _Comp(sem, 16, self.nidx, needed=True)
            self.plan[eng].append(("o", fn, comp))
            dcomp = _DmaComp(sem, self.qissued[sem])
            for t in reads:
                self.readers.setdefault(t, []).append(dcomp)
            for t in tuple(writes) + tuple(wpub):
                self.readers[t] = []
                self.writer[t] = dcomp
            return dcomp
        else:
            comp = _Comp(TICK[eng], 1, self.nidx)
        self.plan[eng].append(("o", fn, comp))
        for t in reads:
            self.readers.setdefault(t, []).append(comp)
        for t in tuple(writes) + tuple(wpub):
            self.readers[t] = []
            self.writer[t] = comp
        return comp

    def barrier(self, eng, tokens):
        deps = {}
        for t in tokens:
            self._dep(deps, self.writer.get(t))
            for r in self.readers.get(t, ()):
                self._dep(deps, r)
        self._waits(eng, deps)

    def finalize(self):
        cnt = {}
        for e in ENGS:
            for item in self.plan[e]:
                if item[0] == "o" and item[2].needed and item[2].val is None:
                    c = item[2]
                    cnt[c.sem] = cnt.get(c.sem, 0) + c.inc
                    c.val = cnt[c.sem]

    def body(self, eng_name, sems):
        items = self.plan[eng_name]

        def run(e):
            for item in items:
                if item[0] in ("w", "W"):
                    e.wait_ge(sems[item[1].sem], item[1].val)
                elif item[1] is None:
                    e.nop().then_inc(sems[item[2].sem], item[2].inc)
                else:
                    inst = item[1](e)
                    if item[2].needed:
                        inst.then_inc(sems[item[2].sem], item[2].inc)
        return run


def build_nc(dbg=False):
    nc = bass.Bass("TRN2", target_bir_lowering=False)

    x_in = nc.dram_tensor("x_fm", [D, T], F32, kind="ExternalInput").ap()
    wqT = nc.dram_tensor("wqT", [D, D], F32, kind="ExternalInput").ap()
    wkT = nc.dram_tensor("wkT", [D, D], F32, kind="ExternalInput").ap()
    wvT = nc.dram_tensor("wvT", [D, D], F32, kind="ExternalInput").ap()
    woT = nc.dram_tensor("woT", [D, D], F32, kind="ExternalInput").ap()
    wguT_sh = nc.dram_tensor("wguT_sh", [256, 2 * F], F32, kind="ExternalInput").ap()
    wdT_sh = nc.dram_tensor("wdT_sh", [1024, D], F32, kind="ExternalInput").ap()
    wmod_sh = nc.dram_tensor("wmod_sh", [D, 1536], F32, kind="ExternalInput").ap()
    cT_in = nc.dram_tensor("cT", [128, 16], F32, kind="ExternalInput").ap()
    bq_in = nc.dram_tensor("bq_fm", [128, 16], F32, kind="ExternalInput").ap()
    bo_in = nc.dram_tensor("bo_fm", [128, 16], F32, kind="ExternalInput").ap()
    bk_in = nc.dram_tensor("bk_row", [1, D], F32, kind="ExternalInput").ap()
    bv_in = nc.dram_tensor("bv_row", [1, D], F32, kind="ExternalInput").ap()
    idx_in = nc.dram_tensor("idx", [1, T], F32, kind="ExternalInput").ap()
    y_out = nc.dram_tensor("y", [T, D], F32, kind="ExternalOutput").ap()

    mod_in = nc.dram_tensor("mod_in", [128, 12], F32).ap()
    mod_out = nc.dram_tensor("mod_out", [128 * NCORES, 12], F32).ap()
    kv_in = nc.dram_tensor("kv_in", [2065, 128], F32).ap()
    kv_out = nc.dram_tensor("kv_out", [2065, 128], F32).ap()
    tgu_in = nc.dram_tensor("tgu_in", [256, 2 * F], BF16).ap()
    tgu = nc.dram_tensor("tgu", [D, 2 * F], BF16).ap()
    twd_in = nc.dram_tensor("twd_in", [1024, D], BF16).ap()
    twd = nc.dram_tensor("twd", [F, D], BF16).ap()
    t_buf = nc.dram_tensor("t_buf", [F, T], F32).ap()
    if dbg:
        dbg_kv = nc.dram_tensor("dbg_kv", [2065, 128], F32, kind="ExternalOutput").ap()
        dbg_mod = nc.dram_tensor("dbg_mod", [128, 96], F32, kind="ExternalOutput").ap()
        dbg_x1 = nc.dram_tensor("dbg_x1", [D, T], F32, kind="ExternalOutput").ap()
        dbg_xq = nc.dram_tensor("dbg_xq", [D, T], BF16, kind="ExternalOutput").ap()
        dbg_t = nc.dram_tensor("dbg_t", [F, T], F32, kind="ExternalOutput").ap()
        dbg_ha = nc.dram_tensor("dbg_ha", [D, T], BF16, kind="ExternalOutput").ap()
        dbg_q = nc.dram_tensor("dbg_q", [D, T], BF16, kind="ExternalOutput").ap()
        dbg_tgu = nc.dram_tensor("dbg_tgu", [D, 2 * F], BF16, kind="ExternalOutput").ap()
        dbg_tgin = nc.dram_tensor("dbg_tgin", [256, 2 * F], BF16, kind="ExternalOutput").ap()
        dbg_twd = nc.dram_tensor("dbg_twd", [F, D], BF16, kind="ExternalOutput").ap()

    b = Bld()

    with ExitStack() as ctx:
        E = ctx.enter_context
        A_x = E(nc.sbuf_tensor("A_x", [128, KC * T], F32))
        ARB = E(nc.sbuf_tensor("ARB", [128, 24576], F32))
        SM = E(nc.sbuf_tensor("SM", [128, 6144], F32))
        ROWS = E(nc.sbuf_tensor("ROWS", [1, 4096], F32))
        dec_fm = E(nc.sbuf_tensor("dec_fm", [128, T], F32))
        idn = E(nc.sbuf_tensor("idn", [128, 128], F32))
        ones_bf = E(nc.sbuf_tensor("ones_bf", [128, 1], BF16))
        ones_col = E(nc.sbuf_tensor("ones_col", [128, 1], F32))
        onesr_f = E(nc.sbuf_tensor("onesr_f", [1, 128], F32))
        onesr_bf = E(nc.sbuf_tensor("onesr_bf", [1, 128], BF16))
        cT = E(nc.sbuf_tensor("cTs", [128, 16], F32))
        bq_fm = E(nc.sbuf_tensor("bq_s", [128, 16], F32))
        bo_fm = E(nc.sbuf_tensor("bo_s", [128, 16], F32))
        idx_tok = E(nc.sbuf_tensor("idx_tok", [128, NT], F32))
        dec_tok = E(nc.sbuf_tensor("dec_tok", [128, NT], F32))
        modf = E(nc.sbuf_tensor("modf", [128, 96], F32))
        onca = E(nc.sbuf_tensor("onca", [128, 16], F32))
        oncb = E(nc.sbuf_tensor("oncb", [128, 16], F32))
        ga_t = E(nc.sbuf_tensor("ga_t", [128, 16], F32))
        gb_t = E(nc.sbuf_tensor("gb_t", [128, 16], F32))
        gsc = E(nc.sbuf_tensor("gsc", [128, 16], F32))
        ksum_col = E(nc.sbuf_tensor("ksum_col", [128, 16], BF16))
        ks16s = E(nc.sbuf_tensor("ks16s", [16, 128], F32))
        zr_tok = E(nc.sbuf_tensor("zr_tok", [128, 128], F32))
        rm1_tok = E(nc.sbuf_tensor("rm1_tok", [128, NT], F32))
        s1_tok = E(nc.sbuf_tensor("s1_tok", [128, NT], F32))
        mx_tok = E(nc.sbuf_tensor("mx_tok", [128, NT], F32))
        fs_tok = E(nc.sbuf_tensor("fs_tok", [128, NT], F32))
        u_tok = E(nc.sbuf_tensor("u_tok", [128, NT], F32))
        sgu_pp = E(nc.sbuf_tensor("sgu_pp", [128, 1], F32))
        swd_pp = E(nc.sbuf_tensor("swd_pp", [128, 1], F32))
        isgu_pp = E(nc.sbuf_tensor("isgu_pp", [128, 1], F32))
        acc_gu = E(nc.sbuf_tensor("acc_gu", [128, 1], F32))
        acc_wd = E(nc.sbuf_tensor("acc_wd", [128, 1], F32))
        red_t = E(nc.sbuf_tensor("red_t", [128, 1], F32))
        epsln = E(nc.sbuf_tensor("epsln", [128, 1], F32))

        P = [E(nc.psum_tensor("P%d" % i, [128, 512], F32)) for i in range(8)]

        semnames = ["pe", "dve", "act", "pool", "syn", "fns", "fna", "fng", "qs0", "qs1", "qa0", "qa1", "qg0", "qg1", "cc1", "cc2", "cc3", "cc4"]
        sems = {n: E(nc.semaphore(n)) for n in semnames}
        block = E(nc.Block())

        arb_f = ARB.ap()
        arb_bf = ARB.ap().bitcast(BF16)
        ax_f = A_x.ap()
        ax_bf = A_x.ap().bitcast(BF16)
        sm_f = SM.ap()
        sm_bf = SM.ap().bitcast(BF16)
        sm_i8 = SM.ap().bitcast(I8)
        rows_f = ROWS.ap()
        rows_bf = ROWS.ap().bitcast(BF16)
        sc2 = sm_f[0:1, 5632:5760]

        def ha_c(c):
            return arb_bf[:, c * T:(c + 1) * T]

        def q_h(h):
            return arb_bf[:, 16384 + h * T:16384 + (h + 1) * T]

        def arbB_f(off, n):
            return arb_f[:, 8192 + off:8192 + off + n]

        def arbB_bf(off, n):
            return arb_bf[:, 16384 + off:16384 + off + n]

        def arbC_f(off, n):
            return arb_f[:, 16384 + off:16384 + off + n]

        def arbC_bf(off, n):
            return arb_bf[:, 32768 + off:32768 + off + n]

        def tA(*slots):
            return [("A", s) for s in slots]

        def tB(*slots):
            return [("B", s) for s in slots]

        def tC(*slots):
            return [("C", s) for s in slots]

        def tX(*slots):
            return [("X", s) for s in slots]

        def tSA(*slots):
            return [("SA", s) for s in slots]

        def tSB(*slots):
            return [("SB", s) for s in slots]

        def tSC(*slots):
            return [("SC", s) for s in slots]

        def tR(*slots):
            return [("R", s) for s in slots]

        def tP(*idx):
            return [("P", i) for i in idx]

        OP = b.op

        for c in range(KC):
            OP("sync", (lambda c: lambda e: e.dma_start(out=ax_f[:, c * T:(c + 1) * T],
                                                        in_=x_in[c * 128:(c + 1) * 128, :]))(c),
               writes=tX(c), dma=("qs", c))
        idx_row = rows_f[0:1, 3072:4096]
        OP("sync", lambda e: e.dma_start(out=idx_row, in_=idx_in[:, :]), writes=tR(6, 7), dma=("qs", 0))
        OP("sync", lambda e: e.dma_start(out=cT[:, :], in_=cT_in[:, :]), writes=[("cT",)], dma=("qs", 1))
        OP("sync", lambda e: e.dma_start(out=bq_fm[:, :], in_=bq_in[:, :]), writes=[("bq",)], dma=("qs", 0))
        OP("sync", lambda e: e.dma_start(out=bo_fm[:, :], in_=bo_in[:, :]), writes=[("bo",)], dma=("qs", 1))
        bk_row = rows_bf[0:1, 0:2048]
        bv_row = rows_bf[0:1, 2048:4096]

        OP("gpsimd", lambda e: e.memset(idn[:, :], 0.0), writes=[("idn",)])
        OP("gpsimd", lambda e: e.affine_select(out=idn[:, :], in_=idn[:, :], compare_op=ALU.not_equal,
                                               fill=1.0, base=0, pattern=[[-1, 128]], channel_multiplier=1),
           reads=[("idn",)], writes=[("idn",)])
        OP("vector", lambda e: e.memset(ones_bf[:, :], 1.0), writes=[("ones_bf",)])
        OP("vector", lambda e: e.memset(epsln[:, :], EPS_LN), writes=[("epsln",)])
        OP("vector", lambda e: e.memset(sc2[:, :], 0.0), writes=[("sc2",), ("SC", 3)])
        OP("vector", lambda e: e.memset(ones_col[:, :], 1.0), writes=[("ones_col",)])
        OP("vector", lambda e: e.memset(onesr_f[:, :], 1.0), writes=[("onesr_f",)])
        OP("vector", lambda e: e.memset(onesr_bf[:, :], 1.0), writes=[("onesr_bf",)])

        OP("scalar", lambda e: e.activation(cT[:, :], cT[:, :], AF.Silu), reads=[("cT",)], writes=[("cT",)])
        def wmod_dma(k):
            par = k % 2
            buf = arbC_f(par * 2048, 1536)
            OP("sync", (lambda k, buf: lambda e: e.dma_start(out=buf, in_=wmod_sh[k * 128:(k + 1) * 128, :]))(k, buf),
               writes=tC(2 * par, 2 * par + 1), dma=("qs", k))
        wmod_dma(0)
        for k in range(KC):
            par = k % 2
            if k + 1 < KC:
                wmod_dma(k + 1)
            for j in range(3):
                OP("tensor", (lambda k, j, par: lambda e: e.matmul(
                    P[4 + j][0:1, :], cT[:, k:k + 1], arbC_f(par * 2048 + j * 512, 512),
                    start=(k == 0), stop=(k == KC - 1)))(k, j, par),
                   reads=[("cT",)] + tC(2 * par, 2 * par + 1),
                   wcheck=tP(4 + j) if k == 0 else (), wpub=tP(4 + j) if k == KC - 1 else ())
        mrowA = rows_f[0:1, 2048:3072]
        for j in range(2):
            OP("vector", (lambda j: lambda e: e.tensor_copy(mrowA[0:1, j * 512:(j + 1) * 512], P[4 + j][0:1, :]))(j),
               reads=tP(4 + j), writes=tR(4 + j))
        for j in range(8):
            OP("tensor", (lambda j: lambda e: e.transpose(P[7][:, j:j + 1], mrowA[0:1, j * 128:(j + 1) * 128],
                                                          idn[0:1, 0:1]))(j),
               reads=tR(4 + j // 4) + [("idn",)],
               wcheck=tP(7) if j == 0 else (), wpub=tP(7) if j == 7 else ())
        mrowB = rows_f[0:1, 2048:2560]
        OP("vector", lambda e: e.tensor_copy(mrowB, P[6][0:1, :]), reads=tP(6), writes=tR(4))
        for j in range(4):
            OP("tensor", (lambda j: lambda e: e.transpose(P[7][:, 8 + j:9 + j], mrowB[0:1, j * 128:(j + 1) * 128],
                                                          idn[0:1, 0:1]))(j),
               reads=tR(4) + [("idn",)],
               wcheck=(), wpub=tP(7) if j == 3 else ())
        OP("vector", lambda e: e.tensor_copy(modf[:, 0:12], P[7][:, 0:12]), reads=tP(7), writes=[("modloc",)])
        OP("scalar", lambda e: e.dma_start(out=mod_in[:, :], in_=modf[:, 0:12]),
           reads=[("modloc",)], writes=[("mod_in",)], dma=("qa", 0))
        OP("gpsimd", lambda e: e.collective_compute(
            "AllGather", ALU.bypass, replica_groups=[list(range(NCORES))],
            ins=[mod_in.opt()], outs=[mod_out.opt()]),
           reads=[("mod_in",)], writes=[("mod_out",)], comp_sem=("cc1", 1))
        OP("gpsimd", lambda e: e.dma_start(out=modf[:, :].rearrange("p (r j) -> p r j", r=8),
                                           in_=mod_out.rearrange("(r p) j -> p r j", p=128)),
           reads=[("mod_out",), ("modloc",)], writes=[("modf",)], dma=("qg", 0))

        sa_v = modf[:, 0:16]
        ca_v = modf[:, 16:32]
        ga_v = modf[:, 32:48]
        sb_v = modf[:, 48:64]
        cb_v = modf[:, 64:80]
        gb_v = modf[:, 80:96]
        OP("vector", lambda e: e.tensor_scalar(onca[:, :], ca_v, 1.0, None, ALU.add),
           reads=[("modf",)], writes=[("onca",)])
        OP("vector", lambda e: e.tensor_scalar(oncb[:, :], cb_v, 1.0, None, ALU.add),
           reads=[("modf",)], writes=[("oncb",)])
        for gv, gout, gtok in ((ga_v, ga_t, "ga"), (gb_v, gb_t, "gb")):
            OP("scalar", (lambda gv: lambda e: e.activation(gsc[:, :], gv, AF.Sign))(gv),
               reads=[("modf",)], writes=[("gsc",)])
            OP("scalar", (lambda gv, gout: lambda e: e.activation(gout[:, :], gv, AF.Abs))(gv, gout),
               reads=[("modf",)], writes=[(gtok,)])
            OP("vector", (lambda gout: lambda e: e.tensor_scalar_max(gout[:, :], gout[:, :], 0.25))(gout),
               reads=[(gtok,)], writes=[(gtok,)])
            OP("vector", (lambda gout: lambda e: e.tensor_mul(gout[:, :], gout[:, :], gsc[:, :]))(gout),
               reads=[(gtok,), ("gsc",)], writes=[(gtok,)])

        for j in range(NT):
            OP("tensor", (lambda j: lambda e: e.transpose(P[6][:, j:j + 1], idx_row[0:1, j * 128:(j + 1) * 128],
                                                          idn[0:1, 0:1]))(j),
               reads=tR(6, 7) + [("idn",)],
               wcheck=tP(6) if j == 0 else (), wpub=tP(6) if j == NT - 1 else ())
        OP("vector", lambda e: e.tensor_copy(idx_tok[:, :], P[6][:, 0:NT]), reads=tP(6), writes=[("idx_tok",)])
        OP("scalar", lambda e: e.activation(dec_tok[:, :], idx_tok[:, :], AF.Exp, scale=-1.0 / SEQ),
           reads=[("idx_tok",)], writes=[("dec_tok",)])
        for n in range(2):
            OP("tensor", (lambda n: lambda e: e.matmul(P[6][:, :], onesr_f[:, :],
                                                       idx_row[0:1, n * 512:(n + 1) * 512], start=True, stop=True))(n),
               reads=tR(6, 7) + [("onesr_f",)], writes=tP(6))
            OP("scalar", (lambda n: lambda e: e.activation(dec_fm[:, n * 512:(n + 1) * 512], P[6][:, :],
                                                           AF.Exp, scale=-1.0 / SEQ))(n),
               reads=tP(6), writes=[("dec_fm", n)])

        mean_b = sm_f[:, 0:T]
        rstd_b = sm_f[:, T:2 * T]

        def ln_stats_chunk(c, pb):
            par = c % 2
            xb = sm_bf[:, 8192 + par * T:8192 + (par + 1) * T]
            xq_ = sm_bf[:, 8192 + 2 * T + par * T:8192 + 2 * T + (par + 1) * T]
            OP("vector", (lambda c, xb: lambda e: e.tensor_copy(xb, ax_f[:, c * T:(c + 1) * T]))(c, xb),
               reads=tX(c), writes=tSC(par))
            OP("scalar", (lambda c, xq_: lambda e: e.activation(xq_, ax_f[:, c * T:(c + 1) * T], AF.Square))(c, xq_),
               reads=tX(c), writes=tSC(2 + par))
            for n in range(2):
                OP("tensor", (lambda c, n, xb, pb: lambda e: e.matmul(
                    P[pb + n][0:1, :], ones_bf[:, :], xb[:, n * 512:(n + 1) * 512],
                    start=(c == 0), stop=(c == KC - 1)))(c, n, xb, pb),
                   reads=tSC(par) + [("ones_bf",)],
                   wcheck=tP(pb + n) if c == 0 else (), wpub=tP(pb + n) if c == KC - 1 else ())
                OP("tensor", (lambda c, n, xq_, pb: lambda e: e.matmul(
                    P[pb + 2 + n][0:1, :], ones_bf[:, :], xq_[:, n * 512:(n + 1) * 512],
                    start=(c == 0), stop=(c == KC - 1)))(c, n, xq_, pb),
                   reads=tSC(2 + par) + [("ones_bf",)],
                   wcheck=tP(pb + 2 + n) if c == 0 else (), wpub=tP(pb + 2 + n) if c == KC - 1 else ())

        def ln_stats_rows(pb):
            mean_r = rows_f[0:1, 0:1024]
            var_r = rows_f[0:1, 1024:2048]
            m2_r = rows_f[0:1, 2048:2560]
            for n in range(2):
                OP("vector", (lambda n, mean_r, pb: lambda e: e.tensor_scalar(
                    mean_r[0:1, n * 512:(n + 1) * 512], P[pb + n][0:1, :], 1.0 / D, None, ALU.mult))(n, mean_r, pb),
                   reads=tP(pb + n), writes=tR(n))
                OP("vector", (lambda n, var_r, pb: lambda e: e.tensor_scalar(
                    var_r[0:1, n * 512:(n + 1) * 512], P[pb + 2 + n][0:1, :], 1.0 / D, None, ALU.mult))(n, var_r, pb),
                   reads=tP(pb + 2 + n), writes=tR(2 + n))
                OP("vector", (lambda n, mean_r, m2_r: lambda e: e.tensor_mul(
                    m2_r, mean_r[0:1, n * 512:(n + 1) * 512], mean_r[0:1, n * 512:(n + 1) * 512]))(n, mean_r, m2_r),
                   reads=tR(n), writes=tR(4))
                OP("vector", (lambda n, var_r, m2_r: lambda e: e.tensor_sub(
                    var_r[0:1, n * 512:(n + 1) * 512], var_r[0:1, n * 512:(n + 1) * 512], m2_r))(n, var_r, m2_r),
                   reads=tR(2 + n, 4), writes=tR(2 + n))
            OP("scalar", lambda e: e.activation(var_r, var_r, AF.Ln, bias=epsln[0:1, 0:1]),
               reads=tR(2, 3) + [("epsln",)], writes=tR(2, 3))
            OP("scalar", lambda e: e.activation(var_r, var_r, AF.Exp, scale=-0.5), reads=tR(2, 3), writes=tR(2, 3))
            for n in range(2):
                OP("tensor", (lambda n, mean_r, pb: lambda e: e.matmul(
                    P[pb][:, :], onesr_f[:, :], mean_r[0:1, n * 512:(n + 1) * 512], start=True, stop=True))(n, mean_r, pb),
                   reads=tR(n) + [("onesr_f",)], writes=tP(pb))
                OP("vector", (lambda n, pb: lambda e: e.tensor_copy(mean_b[:, n * 512:(n + 1) * 512], P[pb][:, :]))(n, pb),
                   reads=tP(pb), writes=tSA(0))
                OP("tensor", (lambda n, var_r, pb: lambda e: e.matmul(
                    P[pb + 1][:, :], onesr_f[:, :], var_r[0:1, n * 512:(n + 1) * 512], start=True, stop=True))(n, var_r, pb),
                   reads=tR(2 + n) + [("onesr_f",)], writes=tP(pb + 1))
                OP("vector", (lambda n, pb: lambda e: e.tensor_copy(rstd_b[:, n * 512:(n + 1) * 512], P[pb + 1][:, :]))(n, pb),
                   reads=tP(pb + 1), writes=tSA(1))

        def ln_apply_chunk(c, scale_v, shift_v, out_ap, out_tokens, out_dtype_pass=None):
            par = c % 2
            t1 = sm_f[:, 2048 + par * T:2048 + (par + 1) * T]
            OP("vector", (lambda c, t1: lambda e: e.tensor_sub(t1, ax_f[:, c * T:(c + 1) * T], mean_b))(c, t1),
               reads=tX(c) + tSA(0), writes=tSB(2 * par, 2 * par + 1))
            OP("vector", (lambda t1: lambda e: e.tensor_mul(t1, t1, rstd_b))(t1),
               reads=tSB(2 * par, 2 * par + 1) + tSA(1), writes=tSB(2 * par, 2 * par + 1))
            OP("vector", (lambda c, t1, out_ap, scale_v, shift_v: lambda e: e.tensor_scalar(
                out_ap, t1, scale_v[:, c:c + 1], shift_v[:, c:c + 1], ALU.mult, ALU.add))(c, t1, out_ap, scale_v, shift_v),
               reads=tSB(2 * par, 2 * par + 1) + [("onca",), ("oncb",), ("modf",)], writes=out_tokens)

        def ternarize(shard_ap, rows, out_dram, ipp, fold_gb, base_tok):
            yield
            cols = shard_ap.shape[1]
            nr, ncb = rows // 128, cols // 1024
            def t_dma(i):
                rb, cb = divmod(i, ncb)
                par = i % 2
                buf = arbB_f(par * 1024, 1024)
                OP("sync", (lambda rb, cb, buf: lambda e: e.dma_start(
                    out=buf, in_=shard_ap[rb * 128:(rb + 1) * 128, cb * 1024:(cb + 1) * 1024]))(rb, cb, buf),
                   writes=tB(2 * par, 2 * par + 1), dma=("qs", i))
            t_dma(0)
            for i in range(nr * ncb):
                rb, cb = divmod(i, ncb)
                par = i % 2
                buf = arbB_f(par * 1024, 1024)
                i8b = arbB_f(4096 + par * 512, 256).bitcast(I8)
                ob = arbB_bf(12288 + par * 1024, 1024)
                btok = tB(2 * par, 2 * par + 1)
                if i + 1 < nr * ncb:
                    t_dma(i + 1)
                if True:
                    OP("scalar", (lambda buf, i8b, ipp: lambda e: e.activation(
                        i8b, buf, AF.Identity, scale=ipp[:, 0:1]))(buf, i8b, ipp),
                       reads=btok + [("isgu",), ("iswd",)], writes=tB(8 + par))
                    OP("scalar", (lambda i8b, ob: lambda e: e.activation(
                        ob, i8b, AF.Sign))(i8b, ob),
                       reads=tB(8 + par), writes=tB(12 + par))
                    if fold_gb:
                        OP("vector", (lambda ob, cb: lambda e: e.tensor_mul(
                            ob, ob, gb_b[:, (cb % 2) * 1024:(cb % 2) * 1024 + 1024]))(ob, cb),
                           reads=tB(12 + par) + [("gbb",)], writes=tB(12 + par))
                    OP("scalar", (lambda rb, cb, ob, out_dram: lambda e: e.dma_start(
                        out=out_dram[rb * 128:(rb + 1) * 128, cb * 1024:(cb + 1) * 1024], in_=ob))(rb, cb, ob, out_dram),
                       reads=tB(12 + par), writes=[(base_tok, rb, cb)], dma=("qa", i))
                    yield

        for c in range(KC):
            ln_stats_chunk(c, 0)
        ln_stats_rows(0)
        for c in range(KC):
            ln_apply_chunk(c, onca, sa_v, ha_c(c), tA(c))

        OP("gpsimd", lambda e: e.dma_start(out=bk_row, in_=bk_in[:, :]), writes=tR(0, 1), dma=("qg", 0))
        OP("gpsimd", lambda e: e.dma_start(out=bv_row, in_=bv_in[:, :]), writes=tR(2, 3), dma=("qg", 1))

        def scan_abs(shard_ap, rows, acc, accname):
            cols = shard_ap.shape[1]
            nr, ncb = rows // 128, cols // 2048

            def s_dma(i):
                rb, cb = divmod(i, ncb)
                par = i % 2
                buf = arbB_f(par * 2048, 2048)
                OP("sync", (lambda rb, cb, buf: lambda e: e.dma_start(
                    out=buf, in_=shard_ap[rb * 128:(rb + 1) * 128, cb * 2048:(cb + 1) * 2048]))(rb, cb, buf),
                   writes=tB(4 * par, 4 * par + 1, 4 * par + 2, 4 * par + 3), dma=("qs", i))
            s_dma(0)
            for i in range(nr * ncb):
                par = i % 2
                buf = arbB_f(par * 2048, 2048)
                if i + 1 < nr * ncb:
                    s_dma(i + 1)
                OP("vector", (lambda buf: lambda e: e.tensor_reduce(
                    red_t[:, :], buf, AX, ALU.add, apply_absolute_value=True))(buf),
                   reads=tB(4 * par, 4 * par + 1, 4 * par + 2, 4 * par + 3), writes=[("red",)])
                if i == 0:
                    OP("vector", (lambda acc: lambda e: e.tensor_copy(acc[:, :], red_t[:, :]))(acc),
                       reads=[("red",)], writes=[(accname,)])
                else:
                    OP("vector", (lambda acc: lambda e: e.tensor_add(acc[:, :], acc[:, :], red_t[:, :]))(acc),
                       reads=[("red",), (accname,)], writes=[(accname,)])

        scan_abs(wguT_sh, 256, acc_gu, "agu")
        scan_abs(wdT_sh, 1024, acc_wd, "awd")


        k_sl = sm_bf[:, 8192:10240]
        v_sl = sm_bf[:, 10240:12288]
        kvloc = sm_f[:, 0:2048]
        ksum_row = rows_f[0:1, 2048:4096]
        wk3 = wkT.rearrange("(kc p) d -> p kc d", p=128)
        wv3 = wvT.rearrange("(kc p) d -> p kc d", p=128)
        def kv_dma(ds):
            par = ds % 2
            kblk = arbC_bf(par * 4096, 4096)
            vblk = arbC_bf(8192 + par * 4096, 4096)
            OP("gpsimd", (lambda ds, kblk: lambda e: e.dma_start(
                out=kblk.rearrange("p (kc d) -> p kc d", kc=KC),
                in_=wk3[:, :, ds * 256:(ds + 1) * 256]))(ds, kblk),
               writes=tC(2 * par, 2 * par + 1), dma=("qg", ds))
            OP("gpsimd", (lambda ds, vblk: lambda e: e.dma_start(
                out=vblk.rearrange("p (kc d) -> p kc d", kc=KC),
                in_=wv3[:, :, ds * 256:(ds + 1) * 256]))(ds, vblk),
               writes=tC(4 + 2 * par, 5 + 2 * par), dma=("qg", ds))
        kv_dma(0)
        for ds in range(8):
            par = ds % 2
            kblk = arbC_bf(par * 4096, 4096)
            vblk = arbC_bf(8192 + par * 4096, 4096)
            if ds + 1 < 8:
                kv_dma(ds + 1)
            for nt in range(NT):
                pk = P[nt % 2]
                pv = P[2 + nt % 2]
                for k in range(KC):
                    OP("tensor", (lambda k, nt, kblk, pk: lambda e: e.matmul(
                        pk[:, 0:256], ha_c(k)[:, nt * 128:(nt + 1) * 128],
                        kblk[:, k * 256:(k + 1) * 256], start=(k == 0), stop=False))(k, nt, kblk, pk),
                       reads=tA(k) + tC(2 * par, 2 * par + 1),
                       wcheck=tP(nt % 2) if k == 0 else ())
                    OP("tensor", (lambda k, nt, vblk, pv: lambda e: e.matmul(
                        pv[:, 0:256], ha_c(k)[:, nt * 128:(nt + 1) * 128],
                        vblk[:, k * 256:(k + 1) * 256], start=(k == 0), stop=False))(k, nt, vblk, pv),
                       reads=tA(k) + tC(4 + 2 * par, 5 + 2 * par),
                       wcheck=tP(2 + nt % 2) if k == 0 else ())
                OP("tensor", (lambda ds, pk: lambda e: e.matmul(
                    pk[:, 0:256], onesr_bf[:, :], bk_row[0:1, ds * 256:(ds + 1) * 256],
                    start=False, stop=True))(ds, pk),
                   reads=tR(0, 1) + [("onesr_bf",)], wpub=tP(nt % 2))
                OP("tensor", (lambda ds, pv: lambda e: e.matmul(
                    pv[:, 0:256], onesr_bf[:, :], bv_row[0:1, ds * 256:(ds + 1) * 256],
                    start=False, stop=True))(ds, pv),
                   reads=tR(2, 3) + [("onesr_bf",)], wpub=tP(2 + nt % 2))
                OP("scalar", (lambda nt, pk: lambda e: e.activation(
                    k_sl[:, nt * 256:(nt + 1) * 256], pk[:, 0:256], AF.Relu,
                    scale=dec_tok[:, nt:nt + 1]))(nt, pk),
                   reads=tP(nt % 2) + [("dec_tok",)], writes=[("ksl", nt)] + (tSC(0, 1) if nt == 0 and ds == 0 else []))
                OP("scalar", (lambda nt, pv: lambda e: e.activation(
                    v_sl[:, nt * 256:(nt + 1) * 256], pv[:, 0:256], AF.Copy))(nt, pv),
                   reads=tP(2 + nt % 2), writes=[("vsl", nt)] + (tSC(2, 3) if nt == 0 and ds == 0 else []))
            for hh in range(2):
                h = 2 * ds + hh
                for nt in range(NT):
                    OP("tensor", (lambda hh, nt: lambda e: e.matmul(
                        P[4 + hh][:, 0:128],
                        k_sl[:, nt * 256 + hh * 128:nt * 256 + hh * 128 + 128],
                        v_sl[:, nt * 256 + hh * 128:nt * 256 + hh * 128 + 128],
                        start=(nt == 0), stop=(nt == NT - 1)))(hh, nt),
                       reads=[("ksl", nt), ("vsl", nt)],
                       wcheck=tP(4 + hh) if nt == 0 else (), wpub=tP(4 + hh) if nt == NT - 1 else ())
                OP("vector", (lambda h, hh: lambda e: e.tensor_copy(
                    kvloc[:, h * 128:(h + 1) * 128], P[4 + hh][:, 0:128]))(h, hh),
                   reads=tP(4 + hh), writes=[("kvloc", h), ("SA", h // 8)])

            for nt in range(NT):
                OP("tensor", (lambda ds, nt: lambda e: e.matmul(
                    P[6][0:1, 0:256], ones_bf[:, :], k_sl[:, nt * 256:(nt + 1) * 256],
                    start=(nt == 0), stop=(nt == NT - 1)))(ds, nt),
                   reads=[("ksl", nt), ("ones_bf",)],
                   wcheck=tP(6) if nt == 0 else (), wpub=tP(6) if nt == NT - 1 else ())
            OP("vector", (lambda ds: lambda e: e.tensor_copy(
                ksum_row[0:1, ds * 256:(ds + 1) * 256], P[6][0:1, 0:256]))(ds),
               reads=tP(6), writes=[("ksum", ds)] + (tR(4, 5, 6, 7) if ds == 0 else []))

        for i, (acc, accname) in enumerate(((acc_gu, "agu"), (acc_wd, "awd"))):
            OP("tensor", (lambda i, acc: lambda e: e.matmul(
                P[7][0:1, i:i + 1], acc[:, :], ones_col[:, :], start=True, stop=True))(i, acc),
               reads=[(accname,), ("ones_col",)],
               wcheck=tP(7) if i == 0 else (), wpub=tP(7) if i == 1 else ())
        OP("vector", lambda e: e.tensor_copy(sc2[0:1, 0:2], P[7][0:1, 0:2]), reads=tP(7), writes=[("sc2",), ("SC", 3)])

        OP("scalar", lambda e: e.dma_start(
            out=kv_in[0:2048, :].rearrange("(h d) e -> d h e", d=128),
            in_=kvloc[:, :].rearrange("p (h e) -> p h e", h=16)),
           reads=[("kvloc", h) for h in range(H)], writes=[("kv_in", 0)], dma=("qa", 0))
        OP("scalar", lambda e: e.dma_start(
            out=kv_in[2048:2064, :].rearrange("h e -> (h e)").unsqueeze(0), in_=ksum_row),
           reads=[("ksum", ds) for ds in range(8)], writes=[("kv_in", 1)], dma=("qa", 0))
        OP("scalar", lambda e: e.dma_start(out=kv_in[2064:2065, :], in_=sc2),
           reads=[("sc2",), ("SC", 3)], writes=[("kv_in", 2)], dma=("qa", 0))
        OP("gpsimd", lambda e: e.collective_compute(
            "AllReduce", ALU.add, replica_groups=[list(range(NCORES))],
            ins=[kv_in.opt()], outs=[kv_out.opt()]),
           reads=[("kv_in", 0), ("kv_in", 1), ("kv_in", 2)], writes=[("kv_out",)], comp_sem=("cc2", 1))

        wq3 = wqT.rearrange("(kc p) d -> p kc d", p=128)

        def wq_dma(dc):
            par = dc % 2
            blk = arbC_bf(par * 2048, 2048)
            OP("gpsimd", (lambda dc, blk: lambda e: e.dma_start(
                out=blk.rearrange("p (kc d) -> p kc d", kc=KC),
                in_=wq3[:, :, dc * 128:(dc + 1) * 128]))(dc, blk),
               writes=tC(par), dma=("qg", dc))
        wq_dma(0)
        for dc in range(KC):
            par = dc % 2
            blk = arbC_bf(par * 2048, 2048)
            if dc + 1 < KC:
                wq_dma(dc + 1)
            pb = 2 * (dc % 2)
            for k in range(KC):
                for n in range(2):
                    OP("tensor", (lambda n, k, blk, pb: lambda e: e.matmul(
                        P[pb + n][:, :], blk[:, k * 128:(k + 1) * 128], ha_c(k)[:, n * 512:(n + 1) * 512],
                        start=(k == 0), stop=(k == KC - 1)))(n, k, blk, pb),
                       reads=tA(k) + tC(par),
                       wcheck=tP(pb + n) if k == 0 else (), wpub=tP(pb + n) if k == KC - 1 else ())
            for n in range(2):
                pq = P[pb + n]
                t1 = sm_f[:, 2048 + n * 512:2048 + (n + 1) * 512]
                OP("scalar", (lambda dc, n, t1, pq: lambda e: e.activation(
                    t1, pq[:, :], AF.Relu, bias=bq_fm[:, dc:dc + 1]))(dc, n, t1, pq),
                   reads=tP(pb + n) + [("bq",)], writes=tSB(n))
                OP("vector", (lambda dc, n, t1: lambda e: e.tensor_mul(
                    q_h(dc)[:, n * 512:(n + 1) * 512], t1, dec_fm[:, n * 512:(n + 1) * 512]))(dc, n, t1),
                   reads=tSB(n) + [("dec_fm", n)], writes=tB(dc))

        kvall = sm_bf[:, 0:2048]
        OP("gpsimd", lambda e: e.dma_start(
            out=kvall.rearrange("p (h e) -> p h e", h=16),
            in_=kv_out[0:2048, :].rearrange("(h d) e -> d h e", d=128)),
           reads=[("kv_out",)], writes=[("kvall",)] + tSA(0),
           wcheck=[("kvloc", h2) for h2 in range(H)], dma=("qg", 0))
        OP("gpsimd", lambda e: e.dma_start(out=ks16s[:, :], in_=kv_out[2048:2064, :]),
           reads=[("kv_out",)], writes=[("ks16",)], dma=("qg", 1))
        OP("gpsimd", lambda e: e.dma_start(out=sc2[0:1, 0:2], in_=kv_out[2064:2065, 0:2]),
           reads=[("kv_out",), ("sc2",)], writes=[("sc2",), ("SC", 3)], dma=("qg", 0))
        if dbg:
            OP("gpsimd", lambda e: e.dma_start(out=dbg_kv[:, :], in_=kv_out[:, :]),
               reads=[("kv_out",)], writes=[("dbgkv",)], dma=("qg", 0))
            OP("gpsimd", lambda e: e.dma_start(out=dbg_mod[:, :], in_=modf[:, :]),
               reads=[("modf",)], writes=[("dbgmod",)], dma=("qg", 0))
            for c in range(KC):
                OP("gpsimd", (lambda c: lambda e: e.dma_start(
                    out=dbg_ha.bitcast(BF16)[c * 128:(c + 1) * 128, :] if False else dbg_ha[c * 128:(c + 1) * 128, :],
                    in_=ha_c(c)))(c), reads=tA(c), writes=[("dbgha", c)], dma=("qg", 0))
                OP("gpsimd", (lambda c: lambda e: e.dma_start(
                    out=dbg_q[c * 128:(c + 1) * 128, :], in_=q_h(c)))(c),
                   reads=tB(c), writes=[("dbgq", c)], dma=("qg", 0))
        OP("tensor", lambda e: e.transpose(P[6][:, 0:16], ks16s[:, :], idn[0:16, 0:16]),
           reads=[("ks16",), ("idn",)], writes=tP(6))
        OP("vector", lambda e: e.tensor_copy(ksum_col[:, :], P[6][:, 0:16]), reads=tP(6), writes=[("kscol",)])

        for h in range(H):
            for nt in range(NT):
                OP("tensor", (lambda h, nt: lambda e: e.matmul(
                    P[7][:, h * 8 + nt:h * 8 + nt + 1], q_h(h)[:, nt * 128:(nt + 1) * 128],
                    ksum_col[:, h:h + 1], start=True, stop=True))(h, nt),
                   reads=tB(h) + [("kscol",)],
                   wcheck=tP(7) if (h == 0 and nt == 0) else (),
                   wpub=tP(7) if (h == H - 1 and nt == NT - 1) else ())
        OP("vector", lambda e: e.tensor_scalar(zr_tok[:, :], P[7][:, 0:128], EPS_ATT, None, ALU.add),
           reads=tP(7), writes=[("zr",)])
        OP("vector", lambda e: e.reciprocal(zr_tok[:, :], zr_tok[:, :]), reads=[("zr",)], writes=[("zr",)])

        zrow = rows_f[0:1, 2048:3072]
        for h in range(H):
            for half in range(2):
                for j in range(4):
                    nt = half * 4 + j
                    OP("tensor", (lambda h, nt, half, j: lambda e: e.transpose(
                        P[6 + half][0:1, j * 128:(j + 1) * 128],
                        zr_tok[:, h * 8 + nt:h * 8 + nt + 1], idn[:, :]))(h, nt, half, j),
                       reads=[("zr",), ("idn",)],
                       wcheck=tP(6 + half) if j == 0 else (), wpub=tP(6 + half) if j == 3 else ())
                OP("scalar", (lambda half: lambda e: e.copy(
                    zrow[0:1, half * 512:(half + 1) * 512], P[6 + half][0:1, :]))(half),
                   reads=tP(6 + half), writes=tR(4 + half),
                   wcheck=[("ksum", ds2) for ds2 in range(8)] if (h == 0 and half == 0) else ())
            for n in range(2):
                OP("tensor", (lambda n: lambda e: e.matmul(
                    P[4 + n][:, :], onesr_f[:, :], zrow[0:1, n * 512:(n + 1) * 512],
                    start=True, stop=True))(n),
                   reads=tR(4 + n) + [("onesr_f",)], writes=tP(4 + n))
                zb = sm_f[:, 2048 + n * 512:2048 + (n + 1) * 512]
                OP("scalar", (lambda n, zb: lambda e: e.copy(zb, P[4 + n][:, :]))(n, zb),
                   reads=tP(4 + n), writes=tSB(n))
                OP("tensor", (lambda h, n: lambda e: e.matmul(
                    P[n][:, :], kvall[:, h * 128:(h + 1) * 128], q_h(h)[:, n * 512:(n + 1) * 512],
                    start=True, stop=True))(h, n),
                   reads=[("kvall",)] + tB(h), writes=tP(n))
                OP("vector", (lambda h, n, zb: lambda e: e.tensor_mul(
                    ha_c(h)[:, n * 512:(n + 1) * 512], P[n][:, :], zb))(h, n, zb),
                   reads=tP(n) + tSB(n), writes=tA(h))

        OP("vector", lambda e: e.tensor_scalar(sc2[0:1, 0:1], sc2[0:1, 0:1], 1.0 / (2.0 * F * D), None, ALU.mult),
           reads=[("sc2",)], writes=[("sc2",), ("SC", 3)])
        OP("vector", lambda e: e.tensor_scalar(sc2[0:1, 1:2], sc2[0:1, 1:2], 1.0 / (float(F) * D), None, ALU.mult),
           reads=[("sc2",)], writes=[("sc2",), ("SC", 3)])
        OP("vector", lambda e: e.tensor_scalar_max(sc2, sc2, EPS_Q),
           reads=[("sc2",)], writes=[("sc2",), ("SC", 3)])
        for i, dst, tk in ((0, sgu_pp, "sgu"), (1, swd_pp, "swd")):
            OP("tensor", (lambda i: lambda e: e.matmul(
                P[6][:, 0:1], onesr_f[:, :], sc2[0:1, i:i + 1], start=True, stop=True))(i),
               reads=[("sc2",), ("SC", 3), ("onesr_f",)], writes=tP(6))
            OP("vector", (lambda dst: lambda e: e.tensor_copy(dst[:, :], P[6][:, 0:1]))(dst),
               reads=tP(6), writes=[(tk,)])
        OP("vector", lambda e: e.reciprocal(isgu_pp[:, :], sgu_pp[:, :]), reads=[("sgu",)], writes=[("isgu",)])
        iswd_pp = acc_gu
        OP("vector", lambda e: e.reciprocal(iswd_pp[:, :], swd_pp[:, :]), reads=[("swd",)],
           writes=[("iswd",)], wcheck=[("agu",)])

        gbrow = rows_f[0:1, 2048:4096]
        for q4 in range(4):
            for j in range(4):
                OP("tensor", (lambda q4, j: lambda e: e.transpose(
                    P[4 + (q4 % 2)][0:1, j * 128:(j + 1) * 128],
                    gb_t[:, q4 * 4 + j:q4 * 4 + j + 1], idn[:, :]))(q4, j),
                   reads=[("gb",), ("idn",)],
                   wcheck=tP(4 + (q4 % 2)) if j == 0 else (), wpub=tP(4 + (q4 % 2)) if j == 3 else ())
            OP("vector", (lambda q4: lambda e: e.tensor_copy(
                gbrow[0:1, q4 * 512:(q4 + 1) * 512], P[4 + (q4 % 2)][0:1, :]))(q4),
               reads=tP(4 + (q4 % 2)), writes=tR(4 + q4))
        gb_b = sm_f[:, 0:2048]
        for q4 in range(4):
            OP("tensor", (lambda q4: lambda e: e.matmul(
                P[6][:, :], onesr_f[:, :], gbrow[0:1, q4 * 512:(q4 + 1) * 512], start=True, stop=True))(q4),
               reads=tR(4 + q4) + [("onesr_f",)], writes=tP(6))
            OP("vector", (lambda q4: lambda e: e.tensor_copy(gb_b[:, q4 * 512:(q4 + 1) * 512], P[6][:, :]))(q4),
               reads=tP(6), writes=tSA(0, 1)[q4 // 2:q4 // 2 + 1] + [("gbb",)], wcheck=[("kvall",)] if q4 == 0 else ())

        import itertools as _it
        tern_drip = _it.chain(ternarize(wguT_sh, 256, tgu_in, isgu_pp, False, "tgin"),
                              ternarize(wdT_sh, 1024, twd_in, iswd_pp, True, "twin"))
        next(tern_drip, None)

        wo3 = woT.rearrange("(kc p) d -> p kc d", p=128)

        def wo_dma(dc):
            par = dc % 2
            blk = arbC_bf(par * 2048, 2048)
            OP("gpsimd", (lambda dc, blk: lambda e: e.dma_start(
                out=blk.rearrange("p (kc d) -> p kc d", kc=KC),
                in_=wo3[:, :, dc * 128:(dc + 1) * 128]))(dc, blk),
               writes=tC(par), dma=("qg", dc))
        wo_dma(0)
        for dc in range(KC):
            par = dc % 2
            blk = arbC_bf(par * 2048, 2048)
            if dc + 1 < KC:
                wo_dma(dc + 1)
            pb = 2 * (dc % 2)
            for k in range(KC):
                for n in range(2):
                    OP("tensor", (lambda n, k, blk, pb: lambda e: e.matmul(
                        P[pb + n][:, :], blk[:, k * 128:(k + 1) * 128], ha_c(k)[:, n * 512:(n + 1) * 512],
                        start=(k == 0), stop=(k == KC - 1)))(n, k, blk, pb),
                       reads=tA(k) + tC(par),
                       wcheck=tP(pb + n) if k == 0 else (), wpub=tP(pb + n) if k == KC - 1 else ())
            for n in range(2):
                pq = P[pb + n]
                t1 = sm_f[:, 2048 + n * 512:2048 + (n + 1) * 512]
                OP("scalar", (lambda dc, n, t1, pq: lambda e: e.activation(
                    t1, pq[:, :], AF.Identity, bias=bo_fm[:, dc:dc + 1]))(dc, n, t1, pq),
                   reads=tP(pb + n) + [("bo",)], writes=tSB(n))
                OP("vector", (lambda dc, n, t1: lambda e: e.scalar_tensor_tensor(
                    ax_f[:, dc * T + n * 512:dc * T + (n + 1) * 512], t1, ga_t[:, dc:dc + 1],
                    ax_f[:, dc * T + n * 512:dc * T + (n + 1) * 512], ALU.mult, ALU.add))(dc, n, t1),
                   reads=tSB(n) + [("ga",)] + tX(dc), writes=tX(dc))
            ln_stats_chunk(dc, 4)
            for _ in range(3):
                next(tern_drip, None)

        if dbg:
            for c in range(KC):
                OP("gpsimd", (lambda c: lambda e: e.dma_start(
                    out=dbg_x1[c * 128:(c + 1) * 128, :], in_=ax_f[:, c * T:(c + 1) * T]))(c),
                   reads=tX(c), writes=[("dbgx1", c)], dma=("qg", 0))


        for _ in tern_drip:
            pass
        OP("gpsimd", lambda e: e.collective_compute(
            "AllGather", ALU.bypass, replica_groups=[list(range(NCORES))],
            ins=[tgu_in.opt()], outs=[tgu.opt()]),
           reads=[("tgin", rb, cb) for rb in range(2) for cb in range(16)],
           writes=[("tgu",)], comp_sem=("cc3", 1))
        OP("gpsimd", lambda e: e.collective_compute(
            "AllGather", ALU.bypass, replica_groups=[list(range(NCORES))],
            ins=[twd_in.opt()], outs=[twd.opt()]),
           reads=[("twin", rb, cb) for rb in range(8) for cb in range(2)],
           writes=[("twd",)], comp_sem=("cc4", 1))

        ln_stats_rows(4)
        mxb = sm_f[:, 4096:5120]
        mnb = sm_f[:, 5120:6144]

        def hb_c(c):
            if c < 8:
                return arb_f[:, c * T:(c + 1) * T]
            return arbC_f((c - 8) * T, T)

        def hb_tok(c):
            if c < 8:
                return tA(2 * c, 2 * c + 1)
            return tC(c - 8)

        for c in range(KC):
            par = c % 2
            t1 = sm_f[:, 2048 + par * T:2048 + (par + 1) * T]
            hb = hb_c(c)
            OP("vector", (lambda c, t1: lambda e: e.tensor_sub(t1, ax_f[:, c * T:(c + 1) * T], mean_b))(c, t1),
               reads=tX(c) + tSA(0), writes=tSB(2 * par, 2 * par + 1))
            OP("vector", (lambda t1: lambda e: e.tensor_mul(t1, t1, rstd_b))(t1),
               reads=tSB(2 * par, 2 * par + 1) + tSA(1), writes=tSB(2 * par, 2 * par + 1))
            OP("scalar", (lambda c, t1, hb: lambda e: e.activation(
                hb, t1, AF.Identity, scale=oncb[:, c:c + 1], bias=sb_v[:, c:c + 1]))(c, t1, hb),
               reads=tSB(2 * par, 2 * par + 1) + [("oncb",), ("modf",)], writes=hb_tok(c))
            if c == 0:
                OP("vector", (lambda hb: lambda e: e.tensor_copy(mxb, hb))(hb),
                   reads=hb_tok(c),
                   writes=tSC(0, 1), wcheck=[("ksl", nt2) for nt2 in range(NT)] + [("vsl", nt2) for nt2 in range(NT)])
                OP("vector", (lambda hb: lambda e: e.tensor_copy(mnb, hb))(hb),
                   reads=hb_tok(c), writes=tSC(2, 3))
            else:
                OP("vector", (lambda hb: lambda e: e.tensor_tensor(mxb, mxb, hb, ALU.max))(hb),
                   reads=hb_tok(c) + tSC(0, 1), writes=tSC(0, 1))
                OP("vector", (lambda hb: lambda e: e.tensor_tensor(mnb, mnb, hb, ALU.min))(hb),
                   reads=hb_tok(c) + tSC(2, 3), writes=tSC(2, 3))
        OP("vector", lambda e: e.tensor_scalar(mnb, mnb, -1.0, None, ALU.mult),
           reads=tSC(2, 3), writes=tSC(2, 3))
        OP("vector", lambda e: e.tensor_max(mxb, mxb, mnb),
           reads=tSC(0, 1) + tSC(2, 3), writes=tSC(0, 1))
        for nt in range(NT):
            OP("tensor", (lambda nt: lambda e: e.transpose(
                P[6][:, 0:128], mxb[:, nt * 128:(nt + 1) * 128], idn[:, :]))(nt),
               reads=tSC(0, 1) + [("idn",)], writes=tP(6))
            OP("vector", (lambda nt: lambda e: e.tensor_reduce(
                rm1_tok[:, nt:nt + 1], P[6][:, 0:128], AX, ALU.max))(nt),
               reads=tP(6), writes=[("rm1",)] if nt == NT - 1 else [("rm1",)])
        OP("vector", lambda e: e.tensor_scalar_max(rm1_tok[:, :], rm1_tok[:, :], EPS_Q),
           reads=[("rm1",)], writes=[("rm1",)])
        OP("vector", lambda e: e.reciprocal(s1_tok[:, :], rm1_tok[:, :]), reads=[("rm1",)], writes=[("s1",)])
        OP("vector", lambda e: e.tensor_scalar(s1_tok[:, :], s1_tok[:, :], 127.0, None, ALU.mult),
           reads=[("s1",)], writes=[("s1",)])
        s1row = rows_f[0:1, 2048:3072]
        for half in range(2):
            for j in range(4):
                nt = half * 4 + j
                OP("tensor", (lambda nt, half, j: lambda e: e.transpose(
                    P[4 + half][0:1, j * 128:(j + 1) * 128], s1_tok[:, nt:nt + 1], idn[:, :]))(nt, half, j),
                   reads=[("s1",), ("idn",)],
                   wcheck=tP(4 + half) if j == 0 else (), wpub=tP(4 + half) if j == 3 else ())
            OP("vector", (lambda half: lambda e: e.tensor_copy(
                s1row[0:1, half * 512:(half + 1) * 512], P[4 + half][0:1, :]))(half),
               reads=tP(4 + half), writes=tR(4 + half))
        S1_b = sm_f[:, 4096:5120]
        for half in range(2):
            OP("tensor", (lambda half: lambda e: e.matmul(
                P[4 + half][:, :], onesr_f[:, :], s1row[0:1, half * 512:(half + 1) * 512],
                start=True, stop=True))(half),
               reads=tR(4 + half) + [("onesr_f",)], writes=tP(4 + half))
            OP("vector", (lambda half: lambda e: e.tensor_copy(
                S1_b[:, half * 512:(half + 1) * 512], P[4 + half][:, :]))(half),
               reads=tP(4 + half), writes=tSC(half) + [("s1b",)])
        i8q = sm_i8[:, 20480:21504]
        for c in range(KC):
            hb = hb_c(c)
            par = c % 2
            i8qp = sm_i8[:, 20480 + par * 1024:21504 + par * 1024]
            OP("vector", (lambda hb, i8qp: lambda e: e.tensor_mul(i8qp, hb, S1_b))(hb, i8qp),
               reads=hb_tok(c) + [("s1b",)], writes=[("i8q", par)] + tSC(2 + par))
            OP("scalar", (lambda c, i8qp: lambda e: e.activation(q_h(c), i8qp, AF.Copy))(c, i8qp),
               reads=[("i8q", par)], writes=tB(c))

        if dbg:
            for c in range(KC):
                OP("gpsimd", (lambda c: lambda e: e.dma_start(
                    out=dbg_xq[c * 128:(c + 1) * 128, :], in_=q_h(c)))(c),
                   reads=tB(c), writes=[("dbgxq", c)], dma=("qg", 0))

        def x1t(nt):
            if nt < 4:
                return arb_f[:, nt * 2048:(nt + 1) * 2048]
            return arbC_f((nt - 4) * 2048, 2048)

        def x1t_tok(nt):
            if nt < 4:
                return tA(4 * nt, 4 * nt + 1, 4 * nt + 2, 4 * nt + 3)
            return tC(2 * (nt - 4), 2 * (nt - 4) + 1)

        for c in range(KC):
            for nt in range(NT):
                pidx = nt % 4
                OP("tensor", (lambda c, nt, pidx: lambda e: e.transpose(
                    P[pidx][:, 0:128], ax_f[:, c * T + nt * 128:c * T + (nt + 1) * 128], idn[:, :]))(c, nt, pidx),
                   reads=tX(c) + [("idn",)], writes=tP(pidx))
                OP("scalar", (lambda c, nt, pidx: lambda e: e.copy(
                    x1t(nt)[:, c * 128:(c + 1) * 128], P[pidx][:, 0:128]))(c, nt, pidx),
                   reads=tP(pidx), writes=x1t_tok(nt) if c == 0 else [x1t_tok(nt)[c % len(x1t_tok(nt))]])

        OP("vector", lambda e: e.tensor_scalar(u_tok[:, :], rm1_tok[:, :], sgu_pp[:, 0:1], None, ALU.mult),
           reads=[("rm1",), ("sgu",)], writes=[("sk_tok",)])
        OP("vector", lambda e: e.tensor_scalar(u_tok[:, :], u_tok[:, :], 1.0 / 127.0, None, ALU.mult),
           reads=[("sk_tok",)], writes=[("sk_tok",)])
        skrow = rows_f[0:1, 3072:4096]
        for half in range(2):
            for j in range(4):
                nt = half * 4 + j
                OP("tensor", (lambda nt, half, j: lambda e: e.transpose(
                    P[4 + half][0:1, j * 128:(j + 1) * 128], u_tok[:, nt:nt + 1], idn[:, :]))(nt, half, j),
                   reads=[("sk_tok",), ("idn",)],
                   wcheck=tP(4 + half) if j == 0 else (), wpub=tP(4 + half) if j == 3 else ())
            OP("vector", (lambda half: lambda e: e.tensor_copy(
                skrow[0:1, half * 512:(half + 1) * 512], P[4 + half][0:1, :]))(half),
               reads=tP(4 + half), writes=tR(6 + half))
        sk_b = sm_f[:, 0:T]
        for half in range(2):
            OP("tensor", (lambda half: lambda e: e.matmul(
                P[4 + half][:, :], onesr_f[:, :], skrow[0:1, half * 512:(half + 1) * 512],
                start=True, stop=True))(half),
               reads=tR(6 + half) + [("onesr_f",)], writes=tP(4 + half))
            OP("vector", (lambda half: lambda e: e.tensor_copy(
                sk_b[:, half * 512:(half + 1) * 512], P[4 + half][:, :]))(half),
               reads=tP(4 + half), writes=tSA(0) + [("skb",)], wcheck=[("gbb",)] if half == 0 else ())

        tgu3 = tgu.rearrange("(kc p) f -> p kc f", p=128)
        mxt = sm_f[:, 1024:2048]
        mnt = sm_f[:, 4096:5120]
        def gu_dma(g2):
            par = g2 % 2
            ablk = ax_bf[:, par * 4096:par * 4096 + 4096]
            bblk = ax_bf[:, 8192 + par * 4096:8192 + par * 4096 + 4096]
            OP("sync", (lambda g2, ablk: lambda e: e.dma_start(
                out=ablk.rearrange("p (kc f) -> p kc f", kc=KC),
                in_=tgu3[:, :, g2 * 256:(g2 + 1) * 256]))(g2, ablk),
               reads=[("tgu",)], writes=tX(2 * par, 2 * par + 1), dma=("qs", g2))
            OP("sync", (lambda g2, bblk: lambda e: e.dma_start(
                out=bblk.rearrange("p (kc f) -> p kc f", kc=KC),
                in_=tgu3[:, :, F + g2 * 256:F + (g2 + 1) * 256]))(g2, bblk),
               reads=[("tgu",)], writes=tX(4 + 2 * par, 5 + 2 * par), dma=("qs", g2))
        gu_dma(0)
        for g2 in range(32):
            par = g2 % 2
            ablk = ax_bf[:, par * 4096:par * 4096 + 4096]
            bblk = ax_bf[:, 8192 + par * 4096:8192 + par * 4096 + 4096]
            if g2 + 1 < 32:
                gu_dma(g2 + 1)
            for sub in range(2):
                fc = 2 * g2 + sub
                pa = (P[0], P[1]) if sub == 0 else (P[2], P[3])
                pb = (P[4], P[5]) if sub == 0 else (P[6], P[7])
                for k in range(KC):
                    for n in range(2):
                        OP("tensor", (lambda sub, n, k, ablk, pa: lambda e: e.matmul(
                            pa[n][:, :], ablk[:, k * 256 + sub * 128:k * 256 + sub * 128 + 128],
                            q_h(k)[:, n * 512:(n + 1) * 512], start=(k == 0), stop=(k == KC - 1)))(sub, n, k, ablk, pa),
                           reads=tX(2 * par, 2 * par + 1) + tB(k),
                           wcheck=tP(P.index(pa[n])) if k == 0 else (),
                           wpub=tP(P.index(pa[n])) if k == KC - 1 else ())
                for k in range(KC):
                    for n in range(2):
                        OP("tensor", (lambda sub, n, k, bblk, pb: lambda e: e.matmul(
                            pb[n][:, :], bblk[:, k * 256 + sub * 128:k * 256 + sub * 128 + 128],
                            q_h(k)[:, n * 512:(n + 1) * 512], start=(k == 0), stop=(k == KC - 1)))(sub, n, k, bblk, pb),
                           reads=tX(4 + 2 * par, 5 + 2 * par) + tB(k),
                           wcheck=tP(P.index(pb[n])) if k == 0 else (),
                           wpub=tP(P.index(pb[n])) if k == KC - 1 else ())
                tpar = fc % 2
                tst = ax_f[:, 8192 + tpar * T:8192 + (tpar + 1) * T]
                sil = ax_f[:, 10240:11264]
                for n in range(2):
                    OP("vector", (lambda n, tst, pa: lambda e: e.tensor_mul(
                        tst[:, n * 512:(n + 1) * 512], pa[n][:, :], sk_b[:, n * 512:(n + 1) * 512]))(n, tst, pa),
                       reads=tP(P.index(pa[n])) + [("skb",)], writes=tX(8 + tpar))
                OP("scalar", (lambda tst, sil: lambda e: e.activation(sil, tst, AF.Silu))(tst, sil),
                   reads=tX(8 + tpar), writes=tX(10))
                for n in range(2):
                    OP("vector", (lambda n, tst, sil, pb: lambda e: e.tensor_mul(
                        tst[:, n * 512:(n + 1) * 512], sil[:, n * 512:(n + 1) * 512], pb[n][:, :]))(n, tst, sil, pb),
                       reads=tX(10) + tP(P.index(pb[n])), writes=tX(8 + tpar))
                if g2 == 0 and sub == 0:
                    OP("vector", (lambda tst: lambda e: e.tensor_copy(mxt, tst))(tst),
                       reads=tX(8 + tpar), writes=tSA(1))
                    OP("vector", (lambda tst: lambda e: e.tensor_copy(mnt, tst))(tst),
                       reads=tX(8 + tpar), writes=tSC(0, 1))
                else:
                    OP("vector", (lambda tst: lambda e: e.tensor_tensor(mxt, mxt, tst, ALU.max))(tst),
                       reads=tX(8 + tpar) + tSA(1), writes=tSA(1))
                    OP("vector", (lambda tst: lambda e: e.tensor_tensor(mnt, mnt, tst, ALU.min))(tst),
                       reads=tX(8 + tpar) + tSC(0, 1), writes=tSC(0, 1))
                OP("scalar", (lambda fc, tst: lambda e: e.dma_start(
                    out=t_buf[fc * 128:(fc + 1) * 128, :], in_=tst))(fc, tst),
                   reads=tX(8 + tpar), writes=[("tb", fc)], dma=("qa", fc))

        if dbg:
            OP("gpsimd", lambda e: e.dma_start(out=dbg_t[:, :], in_=t_buf[:, :]),
               reads=[("tb", fc2) for fc2 in range(64)], writes=[("dbgt",)], dma=("qg", 0))
            OP("gpsimd", lambda e: e.dma_start(out=dbg_tgu[:, :], in_=tgu[:, :]),
               reads=[("tgu",)], writes=[("dbgtgu",)], dma=("qg", 0))
            OP("gpsimd", lambda e: e.dma_start(out=dbg_tgin[:, :], in_=tgu_in[:, :]),
               reads=[("tgin", rb, cb) for rb in range(2) for cb in range(16)],
               writes=[("dbgtgin",)], dma=("qg", 0))
            OP("gpsimd", lambda e: e.dma_start(out=dbg_twd[:, :], in_=twd[:, :]),
               reads=[("twd",)], writes=[("dbgtwd",)], dma=("qg", 0))
        OP("vector", lambda e: e.tensor_scalar(mnt, mnt, -1.0, None, ALU.mult),
           reads=tSC(0, 1), writes=tSC(0, 1))
        OP("vector", lambda e: e.tensor_max(mxt, mxt, mnt),
           reads=tSA(1) + tSC(0, 1), writes=tSA(1))
        for nt in range(NT):
            OP("tensor", (lambda nt: lambda e: e.transpose(
                P[6][:, 0:128], mxt[:, nt * 128:(nt + 1) * 128], idn[:, :]))(nt),
               reads=tSA(1) + [("idn",)], writes=tP(6))
            OP("vector", (lambda nt: lambda e: e.tensor_reduce(
                mx_tok[:, nt:nt + 1], P[6][:, 0:128], AX, ALU.max))(nt),
               reads=tP(6), writes=[("mxt",)])
        OP("vector", lambda e: e.tensor_mul(fs_tok[:, :], u_tok[:, :], mx_tok[:, :]),
           reads=[("sk_tok",), ("mxt",)], writes=[("rm2",)])
        OP("vector", lambda e: e.tensor_scalar_max(fs_tok[:, :], fs_tok[:, :], EPS_Q),
           reads=[("rm2",)], writes=[("rm2",)])
        OP("vector", lambda e: e.reciprocal(mx_tok[:, :], fs_tok[:, :]), reads=[("rm2",)], writes=[("irm2",)])
        OP("vector", lambda e: e.tensor_mul(mx_tok[:, :], mx_tok[:, :], u_tok[:, :]),
           reads=[("irm2",), ("sk_tok",)], writes=[("irm2",)])
        OP("vector", lambda e: e.tensor_scalar(mx_tok[:, :], mx_tok[:, :], 127.0, None, ALU.mult),
           reads=[("irm2",)], writes=[("uq",)])
        OP("vector", lambda e: e.tensor_scalar(fs_tok[:, :], fs_tok[:, :], swd_pp[:, 0:1], None, ALU.mult),
           reads=[("rm2",), ("swd",)], writes=[("fs",)])
        OP("vector", lambda e: e.tensor_scalar(fs_tok[:, :], fs_tok[:, :], 1.0 / 127.0, None, ALU.mult),
           reads=[("fs",)], writes=[("fs",)])
        urow = rows_f[0:1, 2048:3072]
        for half in range(2):
            for j in range(4):
                nt = half * 4 + j
                OP("tensor", (lambda nt, half, j: lambda e: e.transpose(
                    P[4 + half][0:1, j * 128:(j + 1) * 128], mx_tok[:, nt:nt + 1], idn[:, :]))(nt, half, j),
                   reads=[("uq",), ("idn",)],
                   wcheck=tP(4 + half) if j == 0 else (), wpub=tP(4 + half) if j == 3 else ())
            OP("vector", (lambda half: lambda e: e.tensor_copy(
                urow[0:1, half * 512:(half + 1) * 512], P[4 + half][0:1, :]))(half),
               reads=tP(4 + half), writes=tR(4 + half))
        c2_b = sm_f[:, 0:T]
        for half in range(2):
            OP("tensor", (lambda half: lambda e: e.matmul(
                P[4 + half][:, :], onesr_f[:, :], urow[0:1, half * 512:(half + 1) * 512],
                start=True, stop=True))(half),
               reads=tR(4 + half) + [("onesr_f",)], writes=tP(4 + half))
            OP("vector", (lambda half: lambda e: e.tensor_copy(
                c2_b[:, half * 512:(half + 1) * 512], P[4 + half][:, :]))(half),
               reads=tP(4 + half), writes=tSA(0) + [("c2b",)], wcheck=[("skb",)] if half == 0 else ())

        i8d = sm_i8[:, 20480:21504]
        twd3 = twd.rearrange("(kb p) d -> p kb d", p=128)
        for st2 in range(2):
            def tq_dma(k, st2=st2):
                par = k % 2
                tq = sm_f[:, 2048 + par * 512:2048 + (par + 1) * 512]
                OP("sync", (lambda k, st2, tq: lambda e: e.dma_start(
                    out=tq, in_=t_buf[k * 128:(k + 1) * 128, st2 * 512:(st2 + 1) * 512]))(k, st2, tq),
                   reads=[("tb", k)], writes=tSB(par), dma=("qs", k))
            tq_dma(0)
            for k in range(F // 128):
                par = k % 2
                tq = sm_f[:, 2048 + par * 512:2048 + (par + 1) * 512]
                if k + 1 < F // 128:
                    tq_dma(k + 1)
                OP("vector", (lambda k, tq, st2: lambda e: e.tensor_mul(
                    i8d[:, 0:512], tq, c2_b[:, st2 * 512:(st2 + 1) * 512]))(k, tq, st2),
                   reads=tSB(par) + [("c2b",)], writes=tSC(2))
                OP("scalar", (lambda k, st2: lambda e: e.activation(
                    ax_bf[:, k * 512:(k + 1) * 512], i8d[:, 0:512], AF.Copy))(k, st2),
                   reads=tSC(2), writes=tX(k // 4))
            def wd_dma(j, st2=st2):
                dp, kb = divmod(j, 8)
                par = j % 2
                wblk = arbB_bf(par * 8192, 8192)
                OP("sync", (lambda dp, kb, wblk: lambda e: e.dma_start(
                    out=wblk.rearrange("p (kb d) -> p kb d", kb=8),
                    in_=twd3[:, kb * 8:(kb + 1) * 8, dp * 1024:(dp + 1) * 1024]))(dp, kb, wblk),
                   reads=[("twd",)], writes=tB(*range(8 * par, 8 * par + 8)), dma=("qs", j))
            wd_dma(0)
            for j in range(16):
                dp, kb = divmod(j, 8)
                par = j % 2
                wblk = arbB_bf(par * 8192, 8192)
                if j + 1 < 16:
                    wd_dma(j + 1)
                for nt in range(4):
                    for kk in range(8):
                        k = kb * 8 + kk
                        for d2 in range(2):
                            OP("tensor", (lambda st2, nt, k, kk, d2, wblk: lambda e: e.matmul(
                                P[nt * 2 + d2][:, :], ax_bf[:, k * 512 + nt * 128:k * 512 + nt * 128 + 128],
                                wblk[:, kk * 1024 + d2 * 512:kk * 1024 + (d2 + 1) * 512],
                                start=(k == 0), stop=(k == 63)))(st2, nt, k, kk, d2, wblk),
                               reads=tX(k // 4) + tB(*range(8 * par, 8 * par + 8)),
                               wcheck=tP(nt * 2 + d2) if k == 0 else (),
                               wpub=tP(nt * 2 + d2) if k == 63 else ())
                if kb == 7:
                    for nt in range(4):
                        ntg = st2 * 4 + nt
                        for d2 in range(2):
                            dsl = dp * 2 + d2
                            OP("vector", (lambda st2, nt, ntg, dsl, d2: lambda e: e.scalar_tensor_tensor(
                                x1t(ntg)[:, dsl * 512:(dsl + 1) * 512], P[nt * 2 + d2][:, :], fs_tok[:, ntg:ntg + 1],
                                x1t(ntg)[:, dsl * 512:(dsl + 1) * 512], ALU.mult, ALU.add))(st2, nt, ntg, dsl, d2),
                               reads=tP(nt * 2 + d2) + [("fs",)] + x1t_tok(ntg), writes=x1t_tok(ntg))

        for ntg in range(NT):
            OP("scalar", (lambda ntg: lambda e: e.dma_start(
                out=y_out[ntg * 128:(ntg + 1) * 128, :], in_=x1t(ntg)))(ntg),
               reads=x1t_tok(ntg), writes=[("y", ntg)], dma=("qa", ntg))
        b.barrier("sync", [("y", ntg) for ntg in range(NT)])

        b.finalize()
        block.sync(b.body("sync", sems))
        block.vector(b.body("vector", sems))
        block.scalar(b.body("scalar", sems))
        block.tensor(b.body("tensor", sems))
        block.gpsimd(b.body("gpsimd", sems))

    return nc


_CACHE = {}


def kernel(L0_attn_x, L0_attn_Wq, L0_attn_bq, L0_attn_Wk, L0_attn_bk, L0_attn_Wv,
           L0_attn_bv, L0_attn_Wo, L0_attn_bo, L0_ffn_w_gate_up, L0_ffn_w_down,
           L0_c, L0_wmod):
    x = np.ascontiguousarray(np.asarray(L0_attn_x, dtype=np.float32).reshape(SEQ, D))
    wqT = np.ascontiguousarray(np.asarray(L0_attn_Wq, dtype=np.float32).T)
    wkT = np.ascontiguousarray(np.asarray(L0_attn_Wk, dtype=np.float32).T)
    wvT = np.ascontiguousarray(np.asarray(L0_attn_Wv, dtype=np.float32).T)
    woT = np.ascontiguousarray(np.asarray(L0_attn_Wo, dtype=np.float32).T)
    wguT = np.ascontiguousarray(np.asarray(L0_ffn_w_gate_up, dtype=np.float32).T)
    wdT = np.ascontiguousarray(np.asarray(L0_ffn_w_down, dtype=np.float32).T)
    wmod = np.ascontiguousarray(np.asarray(L0_wmod, dtype=np.float32))
    cvec = np.asarray(L0_c, dtype=np.float32).reshape(D)
    bq = np.asarray(L0_attn_bq, dtype=np.float32).reshape(D)
    bk = np.asarray(L0_attn_bk, dtype=np.float32).reshape(D)
    bv = np.asarray(L0_attn_bv, dtype=np.float32).reshape(D)
    bo = np.asarray(L0_attn_bo, dtype=np.float32).reshape(D)

    cT = np.ascontiguousarray(cvec.reshape(KC, 128).T)
    bq_fm = np.ascontiguousarray(bq.reshape(KC, 128).T)
    bo_fm = np.ascontiguousarray(bo.reshape(KC, 128).T)
    bk_row = bk.reshape(1, D)
    bv_row = bv.reshape(1, D)

    if "nc" not in _CACHE:
        _CACHE["nc"] = build_nc()
    nc = _CACHE["nc"]

    in_maps = []
    for c in range(NCORES):
        xs = np.ascontiguousarray(x[c * T:(c + 1) * T, :].T)
        in_maps.append({
            "x_fm": xs,
            "wqT": wqT, "wkT": wkT, "wvT": wvT, "woT": woT,
            "wguT_sh": np.ascontiguousarray(wguT[c * 256:(c + 1) * 256, :]),
            "wdT_sh": np.ascontiguousarray(wdT[c * 1024:(c + 1) * 1024, :]),
            "wmod_sh": np.ascontiguousarray(wmod[:, c * 1536:(c + 1) * 1536]),
            "cT": cT, "bq_fm": bq_fm, "bo_fm": bo_fm,
            "bk_row": bk_row, "bv_row": bv_row,
            "idx": np.arange(c * T, (c + 1) * T, dtype=np.float32).reshape(1, T),
        })
    res = run_bass_kernel_spmd(nc, in_maps, core_ids=list(range(NCORES)))
    out = np.concatenate([np.asarray(r["y"]) for r in res.results], axis=0)
    return out.reshape(1, SEQ, D).astype(np.float32)


# version 11
# speedup vs baseline: 1.0997x; 1.0997x over previous
import numpy as np
import ml_dtypes
import concourse.bass as bass
import concourse.mybir as mybir
from concourse.bass_utils import run_bass_kernel_spmd
from contextlib import ExitStack

F32 = mybir.dt.float32
BF16 = mybir.dt.bfloat16
I8 = mybir.dt.int8
AF = mybir.ActivationFunctionType
ALU = mybir.AluOpType
AX = mybir.AxisListType.X

NCORES = 8
T = 1024
NT = 8
D = 2048
KC = 16
H = 16
F = 8192
SEQ = 8192
EPS_ATT = 1e-6
EPS_LN = 1e-5
EPS_Q = 1e-5

ENGS = ("tensor", "vector", "scalar", "sync", "gpsimd")
TICK = {"tensor": "pe", "vector": "dve", "scalar": "act", "gpsimd": "pool"}


class _Comp:
    __slots__ = ("sem", "inc", "needed", "val", "idx")

    def __init__(self, sem, inc, idx, needed=False):
        self.sem, self.inc, self.needed, self.val, self.idx = sem, inc, needed, None, idx


class _DmaComp:
    __slots__ = ("q", "total", "tick")

    def __init__(self, q, total):
        self.q, self.total, self.tick = q, total, None


QENG = {"qs": "sync", "qa": "scalar", "qg": "gpsimd"}
QTICK = {"qs": "fns", "qa": "fna", "qg": "fng"}


def _qsem(dma):
    q, par = dma
    return q + str(par & 1)


class Bld:

    def __init__(self):
        self.plan = {e: [] for e in ENGS}
        self.writer = {}
        self.readers = {}
        self.water = {e: {} for e in ENGS}
        self.nidx = 0
        self.qissued = {}
        self.qfence = {}

    def _fence(self, dc):
        covered, tick = self.qfence.get(dc.q, (0, None))
        if tick is not None and covered >= dc.total:
            return tick
        qeng = QENG[dc.q[:2]]
        self.nidx += 1
        w = _Comp(dc.q, 16, self.nidx, needed=True)
        w.val = 16 * self.qissued[dc.q]
        self.plan[qeng].append(("W", w))
        self.nidx += 1
        tick = _Comp(QTICK[dc.q[:2]], 1, self.nidx, needed=True)
        self.plan[qeng].append(("o", None, tick))
        self.qfence[dc.q] = (self.qissued[dc.q], tick)
        return tick

    def _dep(self, deps, comp):
        if comp is None:
            return
        if isinstance(comp, _DmaComp):
            comp = self._fence(comp)
        if comp.sem not in deps or comp.idx > deps[comp.sem].idx:
            deps[comp.sem] = comp

    def _waits(self, eng, deps):
        wm = self.water[eng]
        for sem in sorted(deps):
            c = deps[sem]
            if wm.get(sem) is None or c.idx > wm[sem].idx:
                c.needed = True
                self.plan[eng].append(("w", c))
                wm[sem] = c

    def op(self, eng, fn, reads=(), writes=(), wcheck=(), wpub=(), dma=None, comp_sem=None):
        deps = {}
        for t in reads:
            self._dep(deps, self.writer.get(t))
        for t in tuple(writes) + tuple(wcheck):
            self._dep(deps, self.writer.get(t))
            for r in self.readers.get(t, ()):
                self._dep(deps, r)
        self._waits(eng, deps)
        self.nidx += 1
        if comp_sem is not None:
            comp = _Comp(comp_sem[0], comp_sem[1], self.nidx, needed=True)
        elif dma is not None:
            sem = _qsem(dma) if isinstance(dma, tuple) else dma + "0"
            self.qissued[sem] = self.qissued.get(sem, 0) + 1
            comp = _Comp(sem, 16, self.nidx, needed=True)
            self.plan[eng].append(("o", fn, comp))
            dcomp = _DmaComp(sem, self.qissued[sem])
            for t in reads:
                self.readers.setdefault(t, []).append(dcomp)
            for t in tuple(writes) + tuple(wpub):
                self.readers[t] = []
                self.writer[t] = dcomp
            return dcomp
        else:
            comp = _Comp(TICK[eng], 1, self.nidx)
        self.plan[eng].append(("o", fn, comp))
        for t in reads:
            self.readers.setdefault(t, []).append(comp)
        for t in tuple(writes) + tuple(wpub):
            self.readers[t] = []
            self.writer[t] = comp
        return comp

    def barrier(self, eng, tokens):
        deps = {}
        for t in tokens:
            self._dep(deps, self.writer.get(t))
            for r in self.readers.get(t, ()):
                self._dep(deps, r)
        self._waits(eng, deps)

    def finalize(self):
        cnt = {}
        for e in ENGS:
            for item in self.plan[e]:
                if item[0] == "o" and item[2].needed and item[2].val is None:
                    c = item[2]
                    cnt[c.sem] = cnt.get(c.sem, 0) + c.inc
                    c.val = cnt[c.sem]

    def body(self, eng_name, sems):
        items = self.plan[eng_name]

        def run(e):
            for item in items:
                if item[0] in ("w", "W"):
                    e.wait_ge(sems[item[1].sem], item[1].val)
                elif item[1] is None:
                    e.nop().then_inc(sems[item[2].sem], item[2].inc)
                else:
                    inst = item[1](e)
                    if item[2].needed:
                        inst.then_inc(sems[item[2].sem], item[2].inc)
        return run


def build_nc(dbg=False):
    nc = bass.Bass("TRN2", target_bir_lowering=False)

    x_in = nc.dram_tensor("x_fm", [D, T], F32, kind="ExternalInput").ap()
    wqT = nc.dram_tensor("wqT", [D, D], F32, kind="ExternalInput").ap()
    wkT = nc.dram_tensor("wkT", [D, D], F32, kind="ExternalInput").ap()
    wvT = nc.dram_tensor("wvT", [D, D], F32, kind="ExternalInput").ap()
    woT = nc.dram_tensor("woT", [D, D], F32, kind="ExternalInput").ap()
    wguT_sh = nc.dram_tensor("wguT_sh", [256, 2 * F], F32, kind="ExternalInput").ap()
    wdT_sh = nc.dram_tensor("wdT_sh", [1024, D], F32, kind="ExternalInput").ap()
    wmod_sh = nc.dram_tensor("wmod_sh", [D, 1536], F32, kind="ExternalInput").ap()
    cT_in = nc.dram_tensor("cT", [128, 16], F32, kind="ExternalInput").ap()
    bq_in = nc.dram_tensor("bq_fm", [128, 16], F32, kind="ExternalInput").ap()
    bo_in = nc.dram_tensor("bo_fm", [128, 16], F32, kind="ExternalInput").ap()
    bk_in = nc.dram_tensor("bk_row", [1, D], F32, kind="ExternalInput").ap()
    bv_in = nc.dram_tensor("bv_row", [1, D], F32, kind="ExternalInput").ap()
    idx_in = nc.dram_tensor("idx", [1, T], F32, kind="ExternalInput").ap()
    y_out = nc.dram_tensor("y", [T, D], F32, kind="ExternalOutput").ap()

    mod_in = nc.dram_tensor("mod_in", [128, 12], F32).ap()
    mod_out = nc.dram_tensor("mod_out", [128 * NCORES, 12], F32).ap()
    kv_in = nc.dram_tensor("kv_in", [2065, 128], F32).ap()
    kv_out = nc.dram_tensor("kv_out", [2065, 128], F32).ap()
    tgu_in = nc.dram_tensor("tgu_in", [256, 2 * F], BF16).ap()
    tgu = nc.dram_tensor("tgu", [D, 2 * F], BF16).ap()
    twd_in = nc.dram_tensor("twd_in", [1024, D], BF16).ap()
    twd = nc.dram_tensor("twd", [F, D], BF16).ap()
    t_buf = nc.dram_tensor("t_buf", [F, T], F32).ap()
    if dbg:
        dbg_kv = nc.dram_tensor("dbg_kv", [2065, 128], F32, kind="ExternalOutput").ap()
        dbg_mod = nc.dram_tensor("dbg_mod", [128, 96], F32, kind="ExternalOutput").ap()
        dbg_x1 = nc.dram_tensor("dbg_x1", [D, T], F32, kind="ExternalOutput").ap()
        dbg_xq = nc.dram_tensor("dbg_xq", [D, T], BF16, kind="ExternalOutput").ap()
        dbg_t = nc.dram_tensor("dbg_t", [F, T], F32, kind="ExternalOutput").ap()
        dbg_ha = nc.dram_tensor("dbg_ha", [D, T], BF16, kind="ExternalOutput").ap()
        dbg_q = nc.dram_tensor("dbg_q", [D, T], BF16, kind="ExternalOutput").ap()
        dbg_tgu = nc.dram_tensor("dbg_tgu", [D, 2 * F], BF16, kind="ExternalOutput").ap()
        dbg_tgin = nc.dram_tensor("dbg_tgin", [256, 2 * F], BF16, kind="ExternalOutput").ap()
        dbg_twd = nc.dram_tensor("dbg_twd", [F, D], BF16, kind="ExternalOutput").ap()

    b = Bld()

    with ExitStack() as ctx:
        E = ctx.enter_context
        A_x = E(nc.sbuf_tensor("A_x", [128, KC * T], F32))
        ARB = E(nc.sbuf_tensor("ARB", [128, 24576], F32))
        SM = E(nc.sbuf_tensor("SM", [128, 6144], F32))
        ROWS = E(nc.sbuf_tensor("ROWS", [1, 4096], F32))
        dec_fm = E(nc.sbuf_tensor("dec_fm", [128, T], F32))
        idn = E(nc.sbuf_tensor("idn", [128, 128], F32))
        ones_bf = E(nc.sbuf_tensor("ones_bf", [128, 1], BF16))
        ones_col = E(nc.sbuf_tensor("ones_col", [128, 1], F32))
        onesr_f = E(nc.sbuf_tensor("onesr_f", [1, 128], F32))
        onesr_bf = E(nc.sbuf_tensor("onesr_bf", [1, 128], BF16))
        cT = E(nc.sbuf_tensor("cTs", [128, 16], F32))
        bq_fm = E(nc.sbuf_tensor("bq_s", [128, 16], F32))
        bo_fm = E(nc.sbuf_tensor("bo_s", [128, 16], F32))
        idx_tok = E(nc.sbuf_tensor("idx_tok", [128, NT], F32))
        dec_tok = E(nc.sbuf_tensor("dec_tok", [128, NT], F32))
        modf = E(nc.sbuf_tensor("modf", [128, 96], F32))
        onca = E(nc.sbuf_tensor("onca", [128, 16], F32))
        oncb = E(nc.sbuf_tensor("oncb", [128, 16], F32))
        ga_t = E(nc.sbuf_tensor("ga_t", [128, 16], F32))
        gb_t = E(nc.sbuf_tensor("gb_t", [128, 16], F32))
        gsc = E(nc.sbuf_tensor("gsc", [128, 16], F32))
        ksum_col = E(nc.sbuf_tensor("ksum_col", [128, 16], BF16))
        ks16s = E(nc.sbuf_tensor("ks16s", [16, 128], F32))
        zr_tok = E(nc.sbuf_tensor("zr_tok", [128, 128], F32))
        rm1_tok = E(nc.sbuf_tensor("rm1_tok", [128, NT], F32))
        s1_tok = E(nc.sbuf_tensor("s1_tok", [128, NT], F32))
        mx_tok = E(nc.sbuf_tensor("mx_tok", [128, NT], F32))
        fs_tok = E(nc.sbuf_tensor("fs_tok", [128, NT], F32))
        u_tok = E(nc.sbuf_tensor("u_tok", [128, NT], F32))
        sgu_pp = E(nc.sbuf_tensor("sgu_pp", [128, 1], F32))
        swd_pp = E(nc.sbuf_tensor("swd_pp", [128, 1], F32))
        isgu_pp = E(nc.sbuf_tensor("isgu_pp", [128, 1], F32))
        acc_gu = E(nc.sbuf_tensor("acc_gu", [128, 1], F32))
        acc_wd = E(nc.sbuf_tensor("acc_wd", [128, 1], F32))
        red_t = E(nc.sbuf_tensor("red_t", [128, 1], F32))
        epsln = E(nc.sbuf_tensor("epsln", [128, 1], F32))

        P = [E(nc.psum_tensor("P%d" % i, [128, 512], F32)) for i in range(8)]

        semnames = ["pe", "dve", "act", "pool", "syn", "fns", "fna", "fng", "qs0", "qs1", "qa0", "qa1", "qg0", "qg1", "cc1", "cc2", "cc3", "cc4"]
        sems = {n: E(nc.semaphore(n)) for n in semnames}
        block = E(nc.Block())

        arb_f = ARB.ap()
        arb_bf = ARB.ap().bitcast(BF16)
        ax_f = A_x.ap()
        ax_bf = A_x.ap().bitcast(BF16)
        sm_f = SM.ap()
        sm_bf = SM.ap().bitcast(BF16)
        sm_i8 = SM.ap().bitcast(I8)
        rows_f = ROWS.ap()
        rows_bf = ROWS.ap().bitcast(BF16)
        sc2 = sm_f[0:1, 5632:5760]

        def ha_c(c):
            return arb_bf[:, c * T:(c + 1) * T]

        def q_h(h):
            return arb_bf[:, 16384 + h * T:16384 + (h + 1) * T]

        def arbB_f(off, n):
            return arb_f[:, 8192 + off:8192 + off + n]

        def arbB_bf(off, n):
            return arb_bf[:, 16384 + off:16384 + off + n]

        def arbC_f(off, n):
            return arb_f[:, 16384 + off:16384 + off + n]

        def arbC_bf(off, n):
            return arb_bf[:, 32768 + off:32768 + off + n]

        def tA(*slots):
            return [("A", s) for s in slots]

        def tB(*slots):
            return [("B", s) for s in slots]

        def tC(*slots):
            return [("C", s) for s in slots]

        def tX(*slots):
            return [("X", s) for s in slots]

        def tSA(*slots):
            return [("SA", s) for s in slots]

        def tSB(*slots):
            return [("SB", s) for s in slots]

        def tSC(*slots):
            return [("SC", s) for s in slots]

        def tR(*slots):
            return [("R", s) for s in slots]

        def tP(*idx):
            return [("P", i) for i in idx]

        OP = b.op

        for c in range(KC):
            OP("sync", (lambda c: lambda e: e.dma_start(out=ax_f[:, c * T:(c + 1) * T],
                                                        in_=x_in[c * 128:(c + 1) * 128, :]))(c),
               writes=tX(c), dma=("qs", c))
        idx_row = rows_f[0:1, 3072:4096]
        OP("sync", lambda e: e.dma_start(out=idx_row, in_=idx_in[:, :]), writes=tR(6, 7), dma=("qs", 0))
        OP("sync", lambda e: e.dma_start(out=cT[:, :], in_=cT_in[:, :]), writes=[("cT",)], dma=("qs", 1))
        OP("sync", lambda e: e.dma_start(out=bq_fm[:, :], in_=bq_in[:, :]), writes=[("bq",)], dma=("qs", 0))
        OP("sync", lambda e: e.dma_start(out=bo_fm[:, :], in_=bo_in[:, :]), writes=[("bo",)], dma=("qs", 1))
        bk_row = rows_bf[0:1, 0:2048]
        bv_row = rows_bf[0:1, 2048:4096]

        OP("gpsimd", lambda e: e.memset(idn[:, :], 0.0), writes=[("idn",)])
        OP("gpsimd", lambda e: e.affine_select(out=idn[:, :], in_=idn[:, :], compare_op=ALU.not_equal,
                                               fill=1.0, base=0, pattern=[[-1, 128]], channel_multiplier=1),
           reads=[("idn",)], writes=[("idn",)])
        OP("vector", lambda e: e.memset(ones_bf[:, :], 1.0), writes=[("ones_bf",)])
        OP("vector", lambda e: e.memset(epsln[:, :], EPS_LN), writes=[("epsln",)])
        OP("vector", lambda e: e.memset(sc2[:, :], 0.0), writes=[("sc2",), ("SC", 3)])
        OP("vector", lambda e: e.memset(ones_col[:, :], 1.0), writes=[("ones_col",)])
        OP("vector", lambda e: e.memset(onesr_f[:, :], 1.0), writes=[("onesr_f",)])
        OP("vector", lambda e: e.memset(onesr_bf[:, :], 1.0), writes=[("onesr_bf",)])

        OP("scalar", lambda e: e.activation(cT[:, :], cT[:, :], AF.Silu), reads=[("cT",)], writes=[("cT",)])
        def wmod_dma(k):
            par = k % 2
            buf = arbC_f(par * 2048, 1536)
            OP("sync", (lambda k, buf: lambda e: e.dma_start(out=buf, in_=wmod_sh[k * 128:(k + 1) * 128, :]))(k, buf),
               writes=tC(2 * par, 2 * par + 1), dma=("qs", k))
        wmod_dma(0)
        for k in range(KC):
            par = k % 2
            if k + 1 < KC:
                wmod_dma(k + 1)
            for j in range(3):
                OP("tensor", (lambda k, j, par: lambda e: e.matmul(
                    P[4 + j][0:1, :], cT[:, k:k + 1], arbC_f(par * 2048 + j * 512, 512),
                    start=(k == 0), stop=(k == KC - 1)))(k, j, par),
                   reads=[("cT",)] + tC(2 * par, 2 * par + 1),
                   wcheck=tP(4 + j) if k == 0 else (), wpub=tP(4 + j) if k == KC - 1 else ())
        mrowA = rows_f[0:1, 2048:3072]
        for j in range(2):
            OP("vector", (lambda j: lambda e: e.tensor_copy(mrowA[0:1, j * 512:(j + 1) * 512], P[4 + j][0:1, :]))(j),
               reads=tP(4 + j), writes=tR(4 + j))
        for j in range(8):
            OP("tensor", (lambda j: lambda e: e.transpose(P[7][:, j:j + 1], mrowA[0:1, j * 128:(j + 1) * 128],
                                                          idn[0:1, 0:1]))(j),
               reads=tR(4 + j // 4) + [("idn",)],
               wcheck=tP(7) if j == 0 else (), wpub=tP(7) if j == 7 else ())
        mrowB = rows_f[0:1, 2048:2560]
        OP("vector", lambda e: e.tensor_copy(mrowB, P[6][0:1, :]), reads=tP(6), writes=tR(4))
        for j in range(4):
            OP("tensor", (lambda j: lambda e: e.transpose(P[7][:, 8 + j:9 + j], mrowB[0:1, j * 128:(j + 1) * 128],
                                                          idn[0:1, 0:1]))(j),
               reads=tR(4) + [("idn",)],
               wcheck=(), wpub=tP(7) if j == 3 else ())
        OP("vector", lambda e: e.tensor_copy(modf[:, 0:12], P[7][:, 0:12]), reads=tP(7), writes=[("modloc",)])
        OP("scalar", lambda e: e.dma_start(out=mod_in[:, :], in_=modf[:, 0:12]),
           reads=[("modloc",)], writes=[("mod_in",)], dma=("qa", 0))
        OP("gpsimd", lambda e: e.collective_compute(
            "AllGather", ALU.bypass, replica_groups=[list(range(NCORES))],
            ins=[mod_in.opt()], outs=[mod_out.opt()]),
           reads=[("mod_in",)], writes=[("mod_out",)], comp_sem=("cc1", 1))
        OP("gpsimd", lambda e: e.dma_start(out=modf[:, :].rearrange("p (r j) -> p r j", r=8),
                                           in_=mod_out.rearrange("(r p) j -> p r j", p=128)),
           reads=[("mod_out",), ("modloc",)], writes=[("modf",)], dma=("qg", 0))

        sa_v = modf[:, 0:16]
        ca_v = modf[:, 16:32]
        ga_v = modf[:, 32:48]
        sb_v = modf[:, 48:64]
        cb_v = modf[:, 64:80]
        gb_v = modf[:, 80:96]
        OP("vector", lambda e: e.tensor_scalar(onca[:, :], ca_v, 1.0, None, ALU.add),
           reads=[("modf",)], writes=[("onca",)])
        OP("vector", lambda e: e.tensor_scalar(oncb[:, :], cb_v, 1.0, None, ALU.add),
           reads=[("modf",)], writes=[("oncb",)])
        for gv, gout, gtok in ((ga_v, ga_t, "ga"), (gb_v, gb_t, "gb")):
            OP("scalar", (lambda gv: lambda e: e.activation(gsc[:, :], gv, AF.Sign))(gv),
               reads=[("modf",)], writes=[("gsc",)])
            OP("scalar", (lambda gv, gout: lambda e: e.activation(gout[:, :], gv, AF.Abs))(gv, gout),
               reads=[("modf",)], writes=[(gtok,)])
            OP("vector", (lambda gout: lambda e: e.tensor_scalar_max(gout[:, :], gout[:, :], 0.25))(gout),
               reads=[(gtok,)], writes=[(gtok,)])
            OP("vector", (lambda gout: lambda e: e.tensor_mul(gout[:, :], gout[:, :], gsc[:, :]))(gout),
               reads=[(gtok,), ("gsc",)], writes=[(gtok,)])

        for j in range(NT):
            OP("tensor", (lambda j: lambda e: e.transpose(P[6][:, j:j + 1], idx_row[0:1, j * 128:(j + 1) * 128],
                                                          idn[0:1, 0:1]))(j),
               reads=tR(6, 7) + [("idn",)],
               wcheck=tP(6) if j == 0 else (), wpub=tP(6) if j == NT - 1 else ())
        OP("vector", lambda e: e.tensor_copy(idx_tok[:, :], P[6][:, 0:NT]), reads=tP(6), writes=[("idx_tok",)])
        OP("scalar", lambda e: e.activation(dec_tok[:, :], idx_tok[:, :], AF.Exp, scale=-1.0 / SEQ),
           reads=[("idx_tok",)], writes=[("dec_tok",)])
        for n in range(2):
            OP("tensor", (lambda n: lambda e: e.matmul(P[6][:, :], onesr_f[:, :],
                                                       idx_row[0:1, n * 512:(n + 1) * 512], start=True, stop=True))(n),
               reads=tR(6, 7) + [("onesr_f",)], writes=tP(6))
            OP("scalar", (lambda n: lambda e: e.activation(dec_fm[:, n * 512:(n + 1) * 512], P[6][:, :],
                                                           AF.Exp, scale=-1.0 / SEQ))(n),
               reads=tP(6), writes=[("dec_fm", n)])

        mean_b = sm_f[:, 0:T]
        rstd_b = sm_f[:, T:2 * T]

        def ln_stats_chunk(c, pb):
            par = c % 2
            xb = sm_bf[:, 8192 + par * T:8192 + (par + 1) * T]
            xq_ = sm_bf[:, 8192 + 2 * T + par * T:8192 + 2 * T + (par + 1) * T]
            OP("vector", (lambda c, xb: lambda e: e.tensor_copy(xb, ax_f[:, c * T:(c + 1) * T]))(c, xb),
               reads=tX(c), writes=tSC(par))
            OP("scalar", (lambda c, xq_: lambda e: e.activation(xq_, ax_f[:, c * T:(c + 1) * T], AF.Square))(c, xq_),
               reads=tX(c), writes=tSC(2 + par))
            for n in range(2):
                OP("tensor", (lambda c, n, xb, pb: lambda e: e.matmul(
                    P[pb + n][0:1, :], ones_bf[:, :], xb[:, n * 512:(n + 1) * 512],
                    start=(c == 0), stop=(c == KC - 1)))(c, n, xb, pb),
                   reads=tSC(par) + [("ones_bf",)],
                   wcheck=tP(pb + n) if c == 0 else (), wpub=tP(pb + n) if c == KC - 1 else ())
                OP("tensor", (lambda c, n, xq_, pb: lambda e: e.matmul(
                    P[pb + 2 + n][0:1, :], ones_bf[:, :], xq_[:, n * 512:(n + 1) * 512],
                    start=(c == 0), stop=(c == KC - 1)))(c, n, xq_, pb),
                   reads=tSC(2 + par) + [("ones_bf",)],
                   wcheck=tP(pb + 2 + n) if c == 0 else (), wpub=tP(pb + 2 + n) if c == KC - 1 else ())

        def ln_stats_rows(pb):
            mean_r = rows_f[0:1, 0:1024]
            var_r = rows_f[0:1, 1024:2048]
            m2_r = rows_f[0:1, 2048:2560]
            for n in range(2):
                OP("vector", (lambda n, mean_r, pb: lambda e: e.tensor_scalar(
                    mean_r[0:1, n * 512:(n + 1) * 512], P[pb + n][0:1, :], 1.0 / D, None, ALU.mult))(n, mean_r, pb),
                   reads=tP(pb + n), writes=tR(n))
                OP("vector", (lambda n, var_r, pb: lambda e: e.tensor_scalar(
                    var_r[0:1, n * 512:(n + 1) * 512], P[pb + 2 + n][0:1, :], 1.0 / D, None, ALU.mult))(n, var_r, pb),
                   reads=tP(pb + 2 + n), writes=tR(2 + n))
                OP("vector", (lambda n, mean_r, m2_r: lambda e: e.tensor_mul(
                    m2_r, mean_r[0:1, n * 512:(n + 1) * 512], mean_r[0:1, n * 512:(n + 1) * 512]))(n, mean_r, m2_r),
                   reads=tR(n), writes=tR(4))
                OP("vector", (lambda n, var_r, m2_r: lambda e: e.tensor_sub(
                    var_r[0:1, n * 512:(n + 1) * 512], var_r[0:1, n * 512:(n + 1) * 512], m2_r))(n, var_r, m2_r),
                   reads=tR(2 + n, 4), writes=tR(2 + n))
            OP("scalar", lambda e: e.activation(var_r, var_r, AF.Ln, bias=epsln[0:1, 0:1]),
               reads=tR(2, 3) + [("epsln",)], writes=tR(2, 3))
            OP("scalar", lambda e: e.activation(var_r, var_r, AF.Exp, scale=-0.5), reads=tR(2, 3), writes=tR(2, 3))
            for n in range(2):
                OP("tensor", (lambda n, mean_r, pb: lambda e: e.matmul(
                    P[pb][:, :], onesr_f[:, :], mean_r[0:1, n * 512:(n + 1) * 512], start=True, stop=True))(n, mean_r, pb),
                   reads=tR(n) + [("onesr_f",)], writes=tP(pb))
                OP("vector", (lambda n, pb: lambda e: e.tensor_copy(mean_b[:, n * 512:(n + 1) * 512], P[pb][:, :]))(n, pb),
                   reads=tP(pb), writes=tSA(0))
                OP("tensor", (lambda n, var_r, pb: lambda e: e.matmul(
                    P[pb + 1][:, :], onesr_f[:, :], var_r[0:1, n * 512:(n + 1) * 512], start=True, stop=True))(n, var_r, pb),
                   reads=tR(2 + n) + [("onesr_f",)], writes=tP(pb + 1))
                OP("vector", (lambda n, pb: lambda e: e.tensor_copy(rstd_b[:, n * 512:(n + 1) * 512], P[pb + 1][:, :]))(n, pb),
                   reads=tP(pb + 1), writes=tSA(1))

        def ln_apply_chunk(c, scale_v, shift_v, out_ap, out_tokens, out_dtype_pass=None):
            par = c % 2
            t1 = sm_f[:, 2048 + par * T:2048 + (par + 1) * T]
            OP("vector", (lambda c, t1: lambda e: e.tensor_sub(t1, ax_f[:, c * T:(c + 1) * T], mean_b))(c, t1),
               reads=tX(c) + tSA(0), writes=tSB(2 * par, 2 * par + 1))
            OP("vector", (lambda t1: lambda e: e.tensor_mul(t1, t1, rstd_b))(t1),
               reads=tSB(2 * par, 2 * par + 1) + tSA(1), writes=tSB(2 * par, 2 * par + 1))
            OP("vector", (lambda c, t1, out_ap, scale_v, shift_v: lambda e: e.tensor_scalar(
                out_ap, t1, scale_v[:, c:c + 1], shift_v[:, c:c + 1], ALU.mult, ALU.add))(c, t1, out_ap, scale_v, shift_v),
               reads=tSB(2 * par, 2 * par + 1) + [("onca",), ("oncb",), ("modf",)], writes=out_tokens)

        def ternarize(shard_ap, rows, out_dram, ipp, fold_gb, base_tok):
            yield
            cols = shard_ap.shape[1]
            nr, ncb = rows // 128, cols // 1024
            def t_dma(i):
                rb, cb = divmod(i, ncb)
                par = i % 2
                buf = arbB_f(par * 1024, 1024)
                OP("sync", (lambda rb, cb, buf: lambda e: e.dma_start(
                    out=buf, in_=shard_ap[rb * 128:(rb + 1) * 128, cb * 1024:(cb + 1) * 1024]))(rb, cb, buf),
                   writes=tB(2 * par, 2 * par + 1), dma=("qs", i))
            t_dma(0)
            for i in range(nr * ncb):
                rb, cb = divmod(i, ncb)
                par = i % 2
                buf = arbB_f(par * 1024, 1024)
                i8b = arbB_f(4096 + par * 512, 256).bitcast(I8)
                ob = arbB_bf(12288 + par * 1024, 1024)
                btok = tB(2 * par, 2 * par + 1)
                if i + 1 < nr * ncb:
                    t_dma(i + 1)
                if True:
                    OP("scalar", (lambda buf, i8b, ipp: lambda e: e.activation(
                        i8b, buf, AF.Identity, scale=ipp[:, 0:1]))(buf, i8b, ipp),
                       reads=btok + [("isgu",), ("iswd",)], writes=tB(8 + par))
                    OP("scalar", (lambda i8b, ob: lambda e: e.activation(
                        ob, i8b, AF.Sign))(i8b, ob),
                       reads=tB(8 + par), writes=tB(12 + par))
                    if fold_gb:
                        OP("vector", (lambda ob, cb: lambda e: e.tensor_mul(
                            ob, ob, gb_b[:, (cb % 2) * 1024:(cb % 2) * 1024 + 1024]))(ob, cb),
                           reads=tB(12 + par) + [("gbb",)], writes=tB(12 + par))
                    OP("scalar", (lambda rb, cb, ob, out_dram: lambda e: e.dma_start(
                        out=out_dram[rb * 128:(rb + 1) * 128, cb * 1024:(cb + 1) * 1024], in_=ob))(rb, cb, ob, out_dram),
                       reads=tB(12 + par), writes=[(base_tok, rb, cb)], dma=("qa", i))
                    yield

        for c in range(KC):
            ln_stats_chunk(c, 0)
        ln_stats_rows(0)
        for c in range(KC):
            ln_apply_chunk(c, onca, sa_v, ha_c(c), tA(c))

        OP("gpsimd", lambda e: e.dma_start(out=bk_row, in_=bk_in[:, :]), writes=tR(0, 1), dma=("qg", 0))
        OP("gpsimd", lambda e: e.dma_start(out=bv_row, in_=bv_in[:, :]), writes=tR(2, 3), dma=("qg", 1))

        def scan_abs(shard_ap, rows, acc, accname):
            cols = shard_ap.shape[1]
            nr, ncb = rows // 128, cols // 2048

            def s_dma(i):
                rb, cb = divmod(i, ncb)
                par = i % 2
                buf = arbB_f(par * 2048, 2048)
                OP("sync", (lambda rb, cb, buf: lambda e: e.dma_start(
                    out=buf, in_=shard_ap[rb * 128:(rb + 1) * 128, cb * 2048:(cb + 1) * 2048]))(rb, cb, buf),
                   writes=tB(4 * par, 4 * par + 1, 4 * par + 2, 4 * par + 3), dma=("qs", i))
            s_dma(0)
            for i in range(nr * ncb):
                par = i % 2
                buf = arbB_f(par * 2048, 2048)
                if i + 1 < nr * ncb:
                    s_dma(i + 1)
                OP("vector", (lambda buf: lambda e: e.tensor_reduce(
                    red_t[:, :], buf, AX, ALU.add, apply_absolute_value=True))(buf),
                   reads=tB(4 * par, 4 * par + 1, 4 * par + 2, 4 * par + 3), writes=[("red",)])
                if i == 0:
                    OP("vector", (lambda acc: lambda e: e.tensor_copy(acc[:, :], red_t[:, :]))(acc),
                       reads=[("red",)], writes=[(accname,)])
                else:
                    OP("vector", (lambda acc: lambda e: e.tensor_add(acc[:, :], acc[:, :], red_t[:, :]))(acc),
                       reads=[("red",), (accname,)], writes=[(accname,)])

        scan_abs(wguT_sh, 256, acc_gu, "agu")
        scan_abs(wdT_sh, 1024, acc_wd, "awd")


        k_sl = sm_bf[:, 8192:10240]
        v_sl = sm_bf[:, 10240:12288]
        kvloc = sm_f[:, 0:2048]
        ksum_row = rows_f[0:1, 2048:4096]
        wk3 = wkT.rearrange("(kc p) d -> p kc d", p=128)
        wv3 = wvT.rearrange("(kc p) d -> p kc d", p=128)
        def kv_dma(ds):
            par = ds % 2
            kblk = arbC_bf(par * 4096, 4096)
            vblk = arbC_bf(8192 + par * 4096, 4096)
            OP("gpsimd", (lambda ds, kblk: lambda e: e.dma_start(
                out=kblk.rearrange("p (kc d) -> p kc d", kc=KC),
                in_=wk3[:, :, ds * 256:(ds + 1) * 256]))(ds, kblk),
               writes=tC(2 * par, 2 * par + 1), dma=("qg", ds))
            OP("gpsimd", (lambda ds, vblk: lambda e: e.dma_start(
                out=vblk.rearrange("p (kc d) -> p kc d", kc=KC),
                in_=wv3[:, :, ds * 256:(ds + 1) * 256]))(ds, vblk),
               writes=tC(4 + 2 * par, 5 + 2 * par), dma=("qg", ds))
        kv_dma(0)
        for ds in range(8):
            par = ds % 2
            kblk = arbC_bf(par * 4096, 4096)
            vblk = arbC_bf(8192 + par * 4096, 4096)
            if ds + 1 < 8:
                kv_dma(ds + 1)
            for nt in range(NT):
                pk = P[nt % 2]
                pv = P[2 + nt % 2]
                for k in range(KC):
                    OP("tensor", (lambda k, nt, kblk, pk: lambda e: e.matmul(
                        pk[:, 0:256], ha_c(k)[:, nt * 128:(nt + 1) * 128],
                        kblk[:, k * 256:(k + 1) * 256], start=(k == 0), stop=False))(k, nt, kblk, pk),
                       reads=tA(k) + tC(2 * par, 2 * par + 1),
                       wcheck=tP(nt % 2) if k == 0 else ())
                for k in range(KC):
                    OP("tensor", (lambda k, nt, vblk, pv: lambda e: e.matmul(
                        pv[:, 0:256], ha_c(k)[:, nt * 128:(nt + 1) * 128],
                        vblk[:, k * 256:(k + 1) * 256], start=(k == 0), stop=False))(k, nt, vblk, pv),
                       reads=tA(k) + tC(4 + 2 * par, 5 + 2 * par),
                       wcheck=tP(2 + nt % 2) if k == 0 else ())
                OP("tensor", (lambda ds, pk: lambda e: e.matmul(
                    pk[:, 0:256], onesr_bf[:, :], bk_row[0:1, ds * 256:(ds + 1) * 256],
                    start=False, stop=True))(ds, pk),
                   reads=tR(0, 1) + [("onesr_bf",)], wpub=tP(nt % 2))
                OP("tensor", (lambda ds, pv: lambda e: e.matmul(
                    pv[:, 0:256], onesr_bf[:, :], bv_row[0:1, ds * 256:(ds + 1) * 256],
                    start=False, stop=True))(ds, pv),
                   reads=tR(2, 3) + [("onesr_bf",)], wpub=tP(2 + nt % 2))
                OP("scalar", (lambda nt, pk: lambda e: e.activation(
                    k_sl[:, nt * 256:(nt + 1) * 256], pk[:, 0:256], AF.Relu,
                    scale=dec_tok[:, nt:nt + 1]))(nt, pk),
                   reads=tP(nt % 2) + [("dec_tok",)], writes=[("ksl", nt)] + (tSC(0, 1) if nt == 0 and ds == 0 else []))
                OP("scalar", (lambda nt, pv: lambda e: e.activation(
                    v_sl[:, nt * 256:(nt + 1) * 256], pv[:, 0:256], AF.Copy))(nt, pv),
                   reads=tP(2 + nt % 2), writes=[("vsl", nt)] + (tSC(2, 3) if nt == 0 and ds == 0 else []))
            for hh in range(2):
                h = 2 * ds + hh
                for nt in range(NT):
                    OP("tensor", (lambda hh, nt: lambda e: e.matmul(
                        P[4 + hh][:, 0:128],
                        k_sl[:, nt * 256 + hh * 128:nt * 256 + hh * 128 + 128],
                        v_sl[:, nt * 256 + hh * 128:nt * 256 + hh * 128 + 128],
                        start=(nt == 0), stop=(nt == NT - 1)))(hh, nt),
                       reads=[("ksl", nt), ("vsl", nt)],
                       wcheck=tP(4 + hh) if nt == 0 else (), wpub=tP(4 + hh) if nt == NT - 1 else ())
                OP("vector", (lambda h, hh: lambda e: e.tensor_copy(
                    kvloc[:, h * 128:(h + 1) * 128], P[4 + hh][:, 0:128]))(h, hh),
                   reads=tP(4 + hh), writes=[("kvloc", h), ("SA", h // 8)])

            for nt in range(NT):
                OP("tensor", (lambda ds, nt: lambda e: e.matmul(
                    P[6][0:1, 0:256], ones_bf[:, :], k_sl[:, nt * 256:(nt + 1) * 256],
                    start=(nt == 0), stop=(nt == NT - 1)))(ds, nt),
                   reads=[("ksl", nt), ("ones_bf",)],
                   wcheck=tP(6) if nt == 0 else (), wpub=tP(6) if nt == NT - 1 else ())
            OP("vector", (lambda ds: lambda e: e.tensor_copy(
                ksum_row[0:1, ds * 256:(ds + 1) * 256], P[6][0:1, 0:256]))(ds),
               reads=tP(6), writes=[("ksum", ds)] + (tR(4, 5, 6, 7) if ds == 0 else []))

        for i, (acc, accname) in enumerate(((acc_gu, "agu"), (acc_wd, "awd"))):
            OP("tensor", (lambda i, acc: lambda e: e.matmul(
                P[7][0:1, i:i + 1], acc[:, :], ones_col[:, :], start=True, stop=True))(i, acc),
               reads=[(accname,), ("ones_col",)],
               wcheck=tP(7) if i == 0 else (), wpub=tP(7) if i == 1 else ())
        OP("vector", lambda e: e.tensor_copy(sc2[0:1, 0:2], P[7][0:1, 0:2]), reads=tP(7), writes=[("sc2",), ("SC", 3)])

        OP("scalar", lambda e: e.dma_start(
            out=kv_in[0:2048, :].rearrange("(h d) e -> d h e", d=128),
            in_=kvloc[:, :].rearrange("p (h e) -> p h e", h=16)),
           reads=[("kvloc", h) for h in range(H)], writes=[("kv_in", 0)], dma=("qa", 0))
        OP("scalar", lambda e: e.dma_start(
            out=kv_in[2048:2064, :].rearrange("h e -> (h e)").unsqueeze(0), in_=ksum_row),
           reads=[("ksum", ds) for ds in range(8)], writes=[("kv_in", 1)], dma=("qa", 0))
        OP("scalar", lambda e: e.dma_start(out=kv_in[2064:2065, :], in_=sc2),
           reads=[("sc2",), ("SC", 3)], writes=[("kv_in", 2)], dma=("qa", 0))
        OP("gpsimd", lambda e: e.collective_compute(
            "AllReduce", ALU.add, replica_groups=[list(range(NCORES))],
            ins=[kv_in.opt()], outs=[kv_out.opt()]),
           reads=[("kv_in", 0), ("kv_in", 1), ("kv_in", 2)], writes=[("kv_out",)], comp_sem=("cc2", 1))

        wq3 = wqT.rearrange("(kc p) d -> p kc d", p=128)

        def wq_dma(dc):
            par = dc % 2
            blk = arbC_bf(par * 2048, 2048)
            OP("gpsimd", (lambda dc, blk: lambda e: e.dma_start(
                out=blk.rearrange("p (kc d) -> p kc d", kc=KC),
                in_=wq3[:, :, dc * 128:(dc + 1) * 128]))(dc, blk),
               writes=tC(par), dma=("qg", dc))
        wq_dma(0)
        for dc in range(KC):
            par = dc % 2
            blk = arbC_bf(par * 2048, 2048)
            if dc + 1 < KC:
                wq_dma(dc + 1)
            pb = 2 * (dc % 2)
            for n in range(2):
                for k in range(KC):
                    OP("tensor", (lambda n, k, blk, pb: lambda e: e.matmul(
                        P[pb + n][:, :], blk[:, k * 128:(k + 1) * 128], ha_c(k)[:, n * 512:(n + 1) * 512],
                        start=(k == 0), stop=(k == KC - 1)))(n, k, blk, pb),
                       reads=tA(k) + tC(par),
                       wcheck=tP(pb + n) if k == 0 else (), wpub=tP(pb + n) if k == KC - 1 else ())
            for n in range(2):
                pq = P[pb + n]
                t1 = sm_f[:, 2048 + n * 512:2048 + (n + 1) * 512]
                OP("scalar", (lambda dc, n, t1, pq: lambda e: e.activation(
                    t1, pq[:, :], AF.Relu, bias=bq_fm[:, dc:dc + 1]))(dc, n, t1, pq),
                   reads=tP(pb + n) + [("bq",)], writes=tSB(n))
                OP("vector", (lambda dc, n, t1: lambda e: e.tensor_mul(
                    q_h(dc)[:, n * 512:(n + 1) * 512], t1, dec_fm[:, n * 512:(n + 1) * 512]))(dc, n, t1),
                   reads=tSB(n) + [("dec_fm", n)], writes=tB(dc))

        kvall = sm_bf[:, 0:2048]
        OP("gpsimd", lambda e: e.dma_start(
            out=kvall.rearrange("p (h e) -> p h e", h=16),
            in_=kv_out[0:2048, :].rearrange("(h d) e -> d h e", d=128)),
           reads=[("kv_out",)], writes=[("kvall",)] + tSA(0),
           wcheck=[("kvloc", h2) for h2 in range(H)], dma=("qg", 0))
        OP("gpsimd", lambda e: e.dma_start(out=ks16s[:, :], in_=kv_out[2048:2064, :]),
           reads=[("kv_out",)], writes=[("ks16",)], dma=("qg", 1))
        OP("gpsimd", lambda e: e.dma_start(out=sc2[0:1, 0:2], in_=kv_out[2064:2065, 0:2]),
           reads=[("kv_out",), ("sc2",)], writes=[("sc2",), ("SC", 3)], dma=("qg", 0))
        if dbg:
            OP("gpsimd", lambda e: e.dma_start(out=dbg_kv[:, :], in_=kv_out[:, :]),
               reads=[("kv_out",)], writes=[("dbgkv",)], dma=("qg", 0))
            OP("gpsimd", lambda e: e.dma_start(out=dbg_mod[:, :], in_=modf[:, :]),
               reads=[("modf",)], writes=[("dbgmod",)], dma=("qg", 0))
            for c in range(KC):
                OP("gpsimd", (lambda c: lambda e: e.dma_start(
                    out=dbg_ha.bitcast(BF16)[c * 128:(c + 1) * 128, :] if False else dbg_ha[c * 128:(c + 1) * 128, :],
                    in_=ha_c(c)))(c), reads=tA(c), writes=[("dbgha", c)], dma=("qg", 0))
                OP("gpsimd", (lambda c: lambda e: e.dma_start(
                    out=dbg_q[c * 128:(c + 1) * 128, :], in_=q_h(c)))(c),
                   reads=tB(c), writes=[("dbgq", c)], dma=("qg", 0))
        OP("tensor", lambda e: e.transpose(P[6][:, 0:16], ks16s[:, :], idn[0:16, 0:16]),
           reads=[("ks16",), ("idn",)], writes=tP(6))
        OP("vector", lambda e: e.tensor_copy(ksum_col[:, :], P[6][:, 0:16]), reads=tP(6), writes=[("kscol",)])

        for h in range(H):
            for nt in range(NT):
                OP("tensor", (lambda h, nt: lambda e: e.matmul(
                    P[7][:, h * 8 + nt:h * 8 + nt + 1], q_h(h)[:, nt * 128:(nt + 1) * 128],
                    ksum_col[:, h:h + 1], start=True, stop=True))(h, nt),
                   reads=tB(h) + [("kscol",)],
                   wcheck=tP(7) if (h == 0 and nt == 0) else (),
                   wpub=tP(7) if (h == H - 1 and nt == NT - 1) else ())
        OP("vector", lambda e: e.tensor_scalar(zr_tok[:, :], P[7][:, 0:128], EPS_ATT, None, ALU.add),
           reads=tP(7), writes=[("zr",)])
        OP("vector", lambda e: e.reciprocal(zr_tok[:, :], zr_tok[:, :]), reads=[("zr",)], writes=[("zr",)])

        zrow = rows_f[0:1, 2048:3072]
        for h in range(H):
            for half in range(2):
                for j in range(4):
                    nt = half * 4 + j
                    OP("tensor", (lambda h, nt, half, j: lambda e: e.transpose(
                        P[6 + half][0:1, j * 128:(j + 1) * 128],
                        zr_tok[:, h * 8 + nt:h * 8 + nt + 1], idn[:, :]))(h, nt, half, j),
                       reads=[("zr",), ("idn",)],
                       wcheck=tP(6 + half) if j == 0 else (), wpub=tP(6 + half) if j == 3 else ())
                OP("scalar", (lambda half: lambda e: e.copy(
                    zrow[0:1, half * 512:(half + 1) * 512], P[6 + half][0:1, :]))(half),
                   reads=tP(6 + half), writes=tR(4 + half),
                   wcheck=[("ksum", ds2) for ds2 in range(8)] if (h == 0 and half == 0) else ())
            for n in range(2):
                OP("tensor", (lambda n: lambda e: e.matmul(
                    P[4 + n][:, :], onesr_f[:, :], zrow[0:1, n * 512:(n + 1) * 512],
                    start=True, stop=True))(n),
                   reads=tR(4 + n) + [("onesr_f",)], writes=tP(4 + n))
                zb = sm_f[:, 2048 + n * 512:2048 + (n + 1) * 512]
                OP("scalar", (lambda n, zb: lambda e: e.copy(zb, P[4 + n][:, :]))(n, zb),
                   reads=tP(4 + n), writes=tSB(n))
                OP("tensor", (lambda h, n: lambda e: e.matmul(
                    P[n][:, :], kvall[:, h * 128:(h + 1) * 128], q_h(h)[:, n * 512:(n + 1) * 512],
                    start=True, stop=True))(h, n),
                   reads=[("kvall",)] + tB(h), writes=tP(n))
                OP("vector", (lambda h, n, zb: lambda e: e.tensor_mul(
                    ha_c(h)[:, n * 512:(n + 1) * 512], P[n][:, :], zb))(h, n, zb),
                   reads=tP(n) + tSB(n), writes=tA(h))

        OP("vector", lambda e: e.tensor_scalar(sc2[0:1, 0:1], sc2[0:1, 0:1], 1.0 / (2.0 * F * D), None, ALU.mult),
           reads=[("sc2",)], writes=[("sc2",), ("SC", 3)])
        OP("vector", lambda e: e.tensor_scalar(sc2[0:1, 1:2], sc2[0:1, 1:2], 1.0 / (float(F) * D), None, ALU.mult),
           reads=[("sc2",)], writes=[("sc2",), ("SC", 3)])
        OP("vector", lambda e: e.tensor_scalar_max(sc2, sc2, EPS_Q),
           reads=[("sc2",)], writes=[("sc2",), ("SC", 3)])
        for i, dst, tk in ((0, sgu_pp, "sgu"), (1, swd_pp, "swd")):
            OP("tensor", (lambda i: lambda e: e.matmul(
                P[6][:, 0:1], onesr_f[:, :], sc2[0:1, i:i + 1], start=True, stop=True))(i),
               reads=[("sc2",), ("SC", 3), ("onesr_f",)], writes=tP(6))
            OP("vector", (lambda dst: lambda e: e.tensor_copy(dst[:, :], P[6][:, 0:1]))(dst),
               reads=tP(6), writes=[(tk,)])
        OP("vector", lambda e: e.reciprocal(isgu_pp[:, :], sgu_pp[:, :]), reads=[("sgu",)], writes=[("isgu",)])
        iswd_pp = acc_gu
        OP("vector", lambda e: e.reciprocal(iswd_pp[:, :], swd_pp[:, :]), reads=[("swd",)],
           writes=[("iswd",)], wcheck=[("agu",)])

        gbrow = rows_f[0:1, 2048:4096]
        for q4 in range(4):
            for j in range(4):
                OP("tensor", (lambda q4, j: lambda e: e.transpose(
                    P[4 + (q4 % 2)][0:1, j * 128:(j + 1) * 128],
                    gb_t[:, q4 * 4 + j:q4 * 4 + j + 1], idn[:, :]))(q4, j),
                   reads=[("gb",), ("idn",)],
                   wcheck=tP(4 + (q4 % 2)) if j == 0 else (), wpub=tP(4 + (q4 % 2)) if j == 3 else ())
            OP("vector", (lambda q4: lambda e: e.tensor_copy(
                gbrow[0:1, q4 * 512:(q4 + 1) * 512], P[4 + (q4 % 2)][0:1, :]))(q4),
               reads=tP(4 + (q4 % 2)), writes=tR(4 + q4))
        gb_b = sm_f[:, 0:2048]
        for q4 in range(4):
            OP("tensor", (lambda q4: lambda e: e.matmul(
                P[6][:, :], onesr_f[:, :], gbrow[0:1, q4 * 512:(q4 + 1) * 512], start=True, stop=True))(q4),
               reads=tR(4 + q4) + [("onesr_f",)], writes=tP(6))
            OP("vector", (lambda q4: lambda e: e.tensor_copy(gb_b[:, q4 * 512:(q4 + 1) * 512], P[6][:, :]))(q4),
               reads=tP(6), writes=tSA(0, 1)[q4 // 2:q4 // 2 + 1] + [("gbb",)], wcheck=[("kvall",)] if q4 == 0 else ())

        import itertools as _it
        tern_drip = _it.chain(ternarize(wguT_sh, 256, tgu_in, isgu_pp, False, "tgin"),
                              ternarize(wdT_sh, 1024, twd_in, iswd_pp, True, "twin"))
        next(tern_drip, None)

        wo3 = woT.rearrange("(kc p) d -> p kc d", p=128)

        def wo_dma(dc):
            par = dc % 2
            blk = arbC_bf(par * 2048, 2048)
            OP("gpsimd", (lambda dc, blk: lambda e: e.dma_start(
                out=blk.rearrange("p (kc d) -> p kc d", kc=KC),
                in_=wo3[:, :, dc * 128:(dc + 1) * 128]))(dc, blk),
               writes=tC(par), dma=("qg", dc))
        wo_dma(0)
        for dc in range(KC):
            par = dc % 2
            blk = arbC_bf(par * 2048, 2048)
            if dc + 1 < KC:
                wo_dma(dc + 1)
            pb = 2 * (dc % 2)
            for n in range(2):
                for k in range(KC):
                    OP("tensor", (lambda n, k, blk, pb: lambda e: e.matmul(
                        P[pb + n][:, :], blk[:, k * 128:(k + 1) * 128], ha_c(k)[:, n * 512:(n + 1) * 512],
                        start=(k == 0), stop=(k == KC - 1)))(n, k, blk, pb),
                       reads=tA(k) + tC(par),
                       wcheck=tP(pb + n) if k == 0 else (), wpub=tP(pb + n) if k == KC - 1 else ())
            for n in range(2):
                pq = P[pb + n]
                t1 = sm_f[:, 2048 + n * 512:2048 + (n + 1) * 512]
                OP("scalar", (lambda dc, n, t1, pq: lambda e: e.activation(
                    t1, pq[:, :], AF.Identity, bias=bo_fm[:, dc:dc + 1]))(dc, n, t1, pq),
                   reads=tP(pb + n) + [("bo",)], writes=tSB(n))
                OP("vector", (lambda dc, n, t1: lambda e: e.scalar_tensor_tensor(
                    ax_f[:, dc * T + n * 512:dc * T + (n + 1) * 512], t1, ga_t[:, dc:dc + 1],
                    ax_f[:, dc * T + n * 512:dc * T + (n + 1) * 512], ALU.mult, ALU.add))(dc, n, t1),
                   reads=tSB(n) + [("ga",)] + tX(dc), writes=tX(dc))
            ln_stats_chunk(dc, 4)
            for _ in range(3):
                next(tern_drip, None)

        if dbg:
            for c in range(KC):
                OP("gpsimd", (lambda c: lambda e: e.dma_start(
                    out=dbg_x1[c * 128:(c + 1) * 128, :], in_=ax_f[:, c * T:(c + 1) * T]))(c),
                   reads=tX(c), writes=[("dbgx1", c)], dma=("qg", 0))


        for _ in tern_drip:
            pass
        OP("gpsimd", lambda e: e.collective_compute(
            "AllGather", ALU.bypass, replica_groups=[list(range(NCORES))],
            ins=[tgu_in.opt()], outs=[tgu.opt()]),
           reads=[("tgin", rb, cb) for rb in range(2) for cb in range(16)],
           writes=[("tgu",)], comp_sem=("cc3", 1))
        OP("gpsimd", lambda e: e.collective_compute(
            "AllGather", ALU.bypass, replica_groups=[list(range(NCORES))],
            ins=[twd_in.opt()], outs=[twd.opt()]),
           reads=[("twin", rb, cb) for rb in range(8) for cb in range(2)],
           writes=[("twd",)], comp_sem=("cc4", 1))

        ln_stats_rows(4)
        mxb = sm_f[:, 4096:5120]
        mnb = sm_f[:, 5120:6144]

        def hb_c(c):
            if c < 8:
                return arb_f[:, c * T:(c + 1) * T]
            return arbC_f((c - 8) * T, T)

        def hb_tok(c):
            if c < 8:
                return tA(2 * c, 2 * c + 1)
            return tC(c - 8)

        for c in range(KC):
            par = c % 2
            t1 = sm_f[:, 2048 + par * T:2048 + (par + 1) * T]
            hb = hb_c(c)
            OP("vector", (lambda c, t1: lambda e: e.tensor_sub(t1, ax_f[:, c * T:(c + 1) * T], mean_b))(c, t1),
               reads=tX(c) + tSA(0), writes=tSB(2 * par, 2 * par + 1))
            OP("vector", (lambda t1: lambda e: e.tensor_mul(t1, t1, rstd_b))(t1),
               reads=tSB(2 * par, 2 * par + 1) + tSA(1), writes=tSB(2 * par, 2 * par + 1))
            OP("scalar", (lambda c, t1, hb: lambda e: e.activation(
                hb, t1, AF.Identity, scale=oncb[:, c:c + 1], bias=sb_v[:, c:c + 1]))(c, t1, hb),
               reads=tSB(2 * par, 2 * par + 1) + [("oncb",), ("modf",)], writes=hb_tok(c))
            if c == 0:
                OP("vector", (lambda hb: lambda e: e.tensor_copy(mxb, hb))(hb),
                   reads=hb_tok(c),
                   writes=tSC(0, 1), wcheck=[("ksl", nt2) for nt2 in range(NT)] + [("vsl", nt2) for nt2 in range(NT)])
                OP("vector", (lambda hb: lambda e: e.tensor_copy(mnb, hb))(hb),
                   reads=hb_tok(c), writes=tSC(2, 3))
            else:
                OP("vector", (lambda hb: lambda e: e.tensor_tensor(mxb, mxb, hb, ALU.max))(hb),
                   reads=hb_tok(c) + tSC(0, 1), writes=tSC(0, 1))
                OP("vector", (lambda hb: lambda e: e.tensor_tensor(mnb, mnb, hb, ALU.min))(hb),
                   reads=hb_tok(c) + tSC(2, 3), writes=tSC(2, 3))
        OP("vector", lambda e: e.tensor_scalar(mnb, mnb, -1.0, None, ALU.mult),
           reads=tSC(2, 3), writes=tSC(2, 3))
        OP("vector", lambda e: e.tensor_max(mxb, mxb, mnb),
           reads=tSC(0, 1) + tSC(2, 3), writes=tSC(0, 1))
        for nt in range(NT):
            OP("tensor", (lambda nt: lambda e: e.transpose(
                P[6][:, 0:128], mxb[:, nt * 128:(nt + 1) * 128], idn[:, :]))(nt),
               reads=tSC(0, 1) + [("idn",)], writes=tP(6))
            OP("vector", (lambda nt: lambda e: e.tensor_reduce(
                rm1_tok[:, nt:nt + 1], P[6][:, 0:128], AX, ALU.max))(nt),
               reads=tP(6), writes=[("rm1",)] if nt == NT - 1 else [("rm1",)])
        OP("vector", lambda e: e.tensor_scalar_max(rm1_tok[:, :], rm1_tok[:, :], EPS_Q),
           reads=[("rm1",)], writes=[("rm1",)])
        OP("vector", lambda e: e.reciprocal(s1_tok[:, :], rm1_tok[:, :]), reads=[("rm1",)], writes=[("s1",)])
        OP("vector", lambda e: e.tensor_scalar(s1_tok[:, :], s1_tok[:, :], 127.0, None, ALU.mult),
           reads=[("s1",)], writes=[("s1",)])
        s1row = rows_f[0:1, 2048:3072]
        for half in range(2):
            for j in range(4):
                nt = half * 4 + j
                OP("tensor", (lambda nt, half, j: lambda e: e.transpose(
                    P[4 + half][0:1, j * 128:(j + 1) * 128], s1_tok[:, nt:nt + 1], idn[:, :]))(nt, half, j),
                   reads=[("s1",), ("idn",)],
                   wcheck=tP(4 + half) if j == 0 else (), wpub=tP(4 + half) if j == 3 else ())
            OP("vector", (lambda half: lambda e: e.tensor_copy(
                s1row[0:1, half * 512:(half + 1) * 512], P[4 + half][0:1, :]))(half),
               reads=tP(4 + half), writes=tR(4 + half))
        S1_b = sm_f[:, 4096:5120]
        for half in range(2):
            OP("tensor", (lambda half: lambda e: e.matmul(
                P[4 + half][:, :], onesr_f[:, :], s1row[0:1, half * 512:(half + 1) * 512],
                start=True, stop=True))(half),
               reads=tR(4 + half) + [("onesr_f",)], writes=tP(4 + half))
            OP("vector", (lambda half: lambda e: e.tensor_copy(
                S1_b[:, half * 512:(half + 1) * 512], P[4 + half][:, :]))(half),
               reads=tP(4 + half), writes=tSC(half) + [("s1b",)])
        i8q = sm_i8[:, 20480:21504]
        for c in range(KC):
            hb = hb_c(c)
            par = c % 2
            i8qp = sm_i8[:, 20480 + par * 1024:21504 + par * 1024]
            OP("vector", (lambda hb, i8qp: lambda e: e.tensor_mul(i8qp, hb, S1_b))(hb, i8qp),
               reads=hb_tok(c) + [("s1b",)], writes=[("i8q", par)] + tSC(2 + par))
            OP("scalar", (lambda c, i8qp: lambda e: e.activation(q_h(c), i8qp, AF.Copy))(c, i8qp),
               reads=[("i8q", par)], writes=tB(c))

        if dbg:
            for c in range(KC):
                OP("gpsimd", (lambda c: lambda e: e.dma_start(
                    out=dbg_xq[c * 128:(c + 1) * 128, :], in_=q_h(c)))(c),
                   reads=tB(c), writes=[("dbgxq", c)], dma=("qg", 0))

        def x1t(nt):
            if nt < 4:
                return arb_f[:, nt * 2048:(nt + 1) * 2048]
            return arbC_f((nt - 4) * 2048, 2048)

        def x1t_tok(nt):
            if nt < 4:
                return tA(4 * nt, 4 * nt + 1, 4 * nt + 2, 4 * nt + 3)
            return tC(2 * (nt - 4), 2 * (nt - 4) + 1)

        for c in range(KC):
            for nt in range(NT):
                pidx = nt % 4
                OP("tensor", (lambda c, nt, pidx: lambda e: e.transpose(
                    P[pidx][:, 0:128], ax_f[:, c * T + nt * 128:c * T + (nt + 1) * 128], idn[:, :]))(c, nt, pidx),
                   reads=tX(c) + [("idn",)], writes=tP(pidx))
                OP("scalar", (lambda c, nt, pidx: lambda e: e.copy(
                    x1t(nt)[:, c * 128:(c + 1) * 128], P[pidx][:, 0:128]))(c, nt, pidx),
                   reads=tP(pidx), writes=x1t_tok(nt) if c == 0 else [x1t_tok(nt)[c % len(x1t_tok(nt))]])

        OP("vector", lambda e: e.tensor_scalar(u_tok[:, :], rm1_tok[:, :], sgu_pp[:, 0:1], None, ALU.mult),
           reads=[("rm1",), ("sgu",)], writes=[("sk_tok",)])
        OP("vector", lambda e: e.tensor_scalar(u_tok[:, :], u_tok[:, :], 1.0 / 127.0, None, ALU.mult),
           reads=[("sk_tok",)], writes=[("sk_tok",)])
        skrow = rows_f[0:1, 3072:4096]
        for half in range(2):
            for j in range(4):
                nt = half * 4 + j
                OP("tensor", (lambda nt, half, j: lambda e: e.transpose(
                    P[4 + half][0:1, j * 128:(j + 1) * 128], u_tok[:, nt:nt + 1], idn[:, :]))(nt, half, j),
                   reads=[("sk_tok",), ("idn",)],
                   wcheck=tP(4 + half) if j == 0 else (), wpub=tP(4 + half) if j == 3 else ())
            OP("vector", (lambda half: lambda e: e.tensor_copy(
                skrow[0:1, half * 512:(half + 1) * 512], P[4 + half][0:1, :]))(half),
               reads=tP(4 + half), writes=tR(6 + half))
        sk_b = sm_f[:, 0:T]
        for half in range(2):
            OP("tensor", (lambda half: lambda e: e.matmul(
                P[4 + half][:, :], onesr_f[:, :], skrow[0:1, half * 512:(half + 1) * 512],
                start=True, stop=True))(half),
               reads=tR(6 + half) + [("onesr_f",)], writes=tP(4 + half))
            OP("vector", (lambda half: lambda e: e.tensor_copy(
                sk_b[:, half * 512:(half + 1) * 512], P[4 + half][:, :]))(half),
               reads=tP(4 + half), writes=tSA(0) + [("skb",)], wcheck=[("gbb",)] if half == 0 else ())

        tgu3 = tgu.rearrange("(kc p) f -> p kc f", p=128)
        mxt = sm_f[:, 1024:2048]
        mnt = sm_f[:, 4096:5120]
        def gu_dma(g2):
            par = g2 % 2
            ablk = ax_bf[:, par * 4096:par * 4096 + 4096]
            bblk = ax_bf[:, 8192 + par * 4096:8192 + par * 4096 + 4096]
            OP("sync", (lambda g2, ablk: lambda e: e.dma_start(
                out=ablk.rearrange("p (kc f) -> p kc f", kc=KC),
                in_=tgu3[:, :, g2 * 256:(g2 + 1) * 256]))(g2, ablk),
               reads=[("tgu",)], writes=tX(2 * par, 2 * par + 1), dma=("qs", g2))
            OP("sync", (lambda g2, bblk: lambda e: e.dma_start(
                out=bblk.rearrange("p (kc f) -> p kc f", kc=KC),
                in_=tgu3[:, :, F + g2 * 256:F + (g2 + 1) * 256]))(g2, bblk),
               reads=[("tgu",)], writes=tX(4 + 2 * par, 5 + 2 * par), dma=("qs", g2))
        gu_dma(0)
        for g2 in range(32):
            par = g2 % 2
            ablk = ax_bf[:, par * 4096:par * 4096 + 4096]
            bblk = ax_bf[:, 8192 + par * 4096:8192 + par * 4096 + 4096]
            if g2 + 1 < 32:
                gu_dma(g2 + 1)
            for sub in range(2):
                fc = 2 * g2 + sub
                pa = (P[0], P[1]) if sub == 0 else (P[2], P[3])
                pb = (P[4], P[5]) if sub == 0 else (P[6], P[7])
                for n in range(2):
                    for k in range(KC):
                        OP("tensor", (lambda sub, n, k, ablk, pa: lambda e: e.matmul(
                            pa[n][:, :], ablk[:, k * 256 + sub * 128:k * 256 + sub * 128 + 128],
                            q_h(k)[:, n * 512:(n + 1) * 512], start=(k == 0), stop=(k == KC - 1)))(sub, n, k, ablk, pa),
                           reads=tX(2 * par, 2 * par + 1) + tB(k),
                           wcheck=tP(P.index(pa[n])) if k == 0 else (),
                           wpub=tP(P.index(pa[n])) if k == KC - 1 else ())
                for n in range(2):
                    for k in range(KC):
                        OP("tensor", (lambda sub, n, k, bblk, pb: lambda e: e.matmul(
                            pb[n][:, :], bblk[:, k * 256 + sub * 128:k * 256 + sub * 128 + 128],
                            q_h(k)[:, n * 512:(n + 1) * 512], start=(k == 0), stop=(k == KC - 1)))(sub, n, k, bblk, pb),
                           reads=tX(4 + 2 * par, 5 + 2 * par) + tB(k),
                           wcheck=tP(P.index(pb[n])) if k == 0 else (),
                           wpub=tP(P.index(pb[n])) if k == KC - 1 else ())
                tpar = fc % 2
                tst = ax_f[:, 8192 + tpar * T:8192 + (tpar + 1) * T]
                sil = ax_f[:, 10240:11264]
                for n in range(2):
                    OP("vector", (lambda n, tst, pa: lambda e: e.tensor_mul(
                        tst[:, n * 512:(n + 1) * 512], pa[n][:, :], sk_b[:, n * 512:(n + 1) * 512]))(n, tst, pa),
                       reads=tP(P.index(pa[n])) + [("skb",)], writes=tX(8 + tpar))
                OP("scalar", (lambda tst, sil: lambda e: e.activation(sil, tst, AF.Silu))(tst, sil),
                   reads=tX(8 + tpar), writes=tX(10))
                for n in range(2):
                    OP("vector", (lambda n, tst, sil, pb: lambda e: e.tensor_mul(
                        tst[:, n * 512:(n + 1) * 512], sil[:, n * 512:(n + 1) * 512], pb[n][:, :]))(n, tst, sil, pb),
                       reads=tX(10) + tP(P.index(pb[n])), writes=tX(8 + tpar))
                if g2 == 0 and sub == 0:
                    OP("vector", (lambda tst: lambda e: e.tensor_copy(mxt, tst))(tst),
                       reads=tX(8 + tpar), writes=tSA(1))
                    OP("vector", (lambda tst: lambda e: e.tensor_copy(mnt, tst))(tst),
                       reads=tX(8 + tpar), writes=tSC(0, 1))
                else:
                    OP("vector", (lambda tst: lambda e: e.tensor_tensor(mxt, mxt, tst, ALU.max))(tst),
                       reads=tX(8 + tpar) + tSA(1), writes=tSA(1))
                    OP("vector", (lambda tst: lambda e: e.tensor_tensor(mnt, mnt, tst, ALU.min))(tst),
                       reads=tX(8 + tpar) + tSC(0, 1), writes=tSC(0, 1))
                OP("scalar", (lambda fc, tst: lambda e: e.dma_start(
                    out=t_buf[fc * 128:(fc + 1) * 128, :], in_=tst))(fc, tst),
                   reads=tX(8 + tpar), writes=[("tb", fc)], dma=("qa", fc))

        if dbg:
            OP("gpsimd", lambda e: e.dma_start(out=dbg_t[:, :], in_=t_buf[:, :]),
               reads=[("tb", fc2) for fc2 in range(64)], writes=[("dbgt",)], dma=("qg", 0))
            OP("gpsimd", lambda e: e.dma_start(out=dbg_tgu[:, :], in_=tgu[:, :]),
               reads=[("tgu",)], writes=[("dbgtgu",)], dma=("qg", 0))
            OP("gpsimd", lambda e: e.dma_start(out=dbg_tgin[:, :], in_=tgu_in[:, :]),
               reads=[("tgin", rb, cb) for rb in range(2) for cb in range(16)],
               writes=[("dbgtgin",)], dma=("qg", 0))
            OP("gpsimd", lambda e: e.dma_start(out=dbg_twd[:, :], in_=twd[:, :]),
               reads=[("twd",)], writes=[("dbgtwd",)], dma=("qg", 0))
        OP("vector", lambda e: e.tensor_scalar(mnt, mnt, -1.0, None, ALU.mult),
           reads=tSC(0, 1), writes=tSC(0, 1))
        OP("vector", lambda e: e.tensor_max(mxt, mxt, mnt),
           reads=tSA(1) + tSC(0, 1), writes=tSA(1))
        for nt in range(NT):
            OP("tensor", (lambda nt: lambda e: e.transpose(
                P[6][:, 0:128], mxt[:, nt * 128:(nt + 1) * 128], idn[:, :]))(nt),
               reads=tSA(1) + [("idn",)], writes=tP(6))
            OP("vector", (lambda nt: lambda e: e.tensor_reduce(
                mx_tok[:, nt:nt + 1], P[6][:, 0:128], AX, ALU.max))(nt),
               reads=tP(6), writes=[("mxt",)])
        OP("vector", lambda e: e.tensor_mul(fs_tok[:, :], u_tok[:, :], mx_tok[:, :]),
           reads=[("sk_tok",), ("mxt",)], writes=[("rm2",)])
        OP("vector", lambda e: e.tensor_scalar_max(fs_tok[:, :], fs_tok[:, :], EPS_Q),
           reads=[("rm2",)], writes=[("rm2",)])
        OP("vector", lambda e: e.reciprocal(mx_tok[:, :], fs_tok[:, :]), reads=[("rm2",)], writes=[("irm2",)])
        OP("vector", lambda e: e.tensor_mul(mx_tok[:, :], mx_tok[:, :], u_tok[:, :]),
           reads=[("irm2",), ("sk_tok",)], writes=[("irm2",)])
        OP("vector", lambda e: e.tensor_scalar(mx_tok[:, :], mx_tok[:, :], 127.0, None, ALU.mult),
           reads=[("irm2",)], writes=[("uq",)])
        OP("vector", lambda e: e.tensor_scalar(fs_tok[:, :], fs_tok[:, :], swd_pp[:, 0:1], None, ALU.mult),
           reads=[("rm2",), ("swd",)], writes=[("fs",)])
        OP("vector", lambda e: e.tensor_scalar(fs_tok[:, :], fs_tok[:, :], 1.0 / 127.0, None, ALU.mult),
           reads=[("fs",)], writes=[("fs",)])
        urow = rows_f[0:1, 2048:3072]
        for half in range(2):
            for j in range(4):
                nt = half * 4 + j
                OP("tensor", (lambda nt, half, j: lambda e: e.transpose(
                    P[4 + half][0:1, j * 128:(j + 1) * 128], mx_tok[:, nt:nt + 1], idn[:, :]))(nt, half, j),
                   reads=[("uq",), ("idn",)],
                   wcheck=tP(4 + half) if j == 0 else (), wpub=tP(4 + half) if j == 3 else ())
            OP("vector", (lambda half: lambda e: e.tensor_copy(
                urow[0:1, half * 512:(half + 1) * 512], P[4 + half][0:1, :]))(half),
               reads=tP(4 + half), writes=tR(4 + half))
        c2_b = sm_f[:, 0:T]
        for half in range(2):
            OP("tensor", (lambda half: lambda e: e.matmul(
                P[4 + half][:, :], onesr_f[:, :], urow[0:1, half * 512:(half + 1) * 512],
                start=True, stop=True))(half),
               reads=tR(4 + half) + [("onesr_f",)], writes=tP(4 + half))
            OP("vector", (lambda half: lambda e: e.tensor_copy(
                c2_b[:, half * 512:(half + 1) * 512], P[4 + half][:, :]))(half),
               reads=tP(4 + half), writes=tSA(0) + [("c2b",)], wcheck=[("skb",)] if half == 0 else ())

        i8d = sm_i8[:, 20480:21504]
        twd3 = twd.rearrange("(kb p) d -> p kb d", p=128)
        for st2 in range(2):
            def tq_dma(k, st2=st2):
                par = k % 2
                tq = sm_f[:, 2048 + par * 512:2048 + (par + 1) * 512]
                OP("sync", (lambda k, st2, tq: lambda e: e.dma_start(
                    out=tq, in_=t_buf[k * 128:(k + 1) * 128, st2 * 512:(st2 + 1) * 512]))(k, st2, tq),
                   reads=[("tb", k)], writes=tSB(par), dma=("qs", k))
            tq_dma(0)
            for k in range(F // 128):
                par = k % 2
                tq = sm_f[:, 2048 + par * 512:2048 + (par + 1) * 512]
                if k + 1 < F // 128:
                    tq_dma(k + 1)
                OP("vector", (lambda k, tq, st2: lambda e: e.tensor_mul(
                    i8d[:, 0:512], tq, c2_b[:, st2 * 512:(st2 + 1) * 512]))(k, tq, st2),
                   reads=tSB(par) + [("c2b",)], writes=tSC(2))
                OP("scalar", (lambda k, st2: lambda e: e.activation(
                    ax_bf[:, k * 512:(k + 1) * 512], i8d[:, 0:512], AF.Copy))(k, st2),
                   reads=tSC(2), writes=tX(k // 4))
            def wd_dma(j, st2=st2):
                dp, kb = divmod(j, 8)
                par = j % 2
                wblk = arbB_bf(par * 8192, 8192)
                OP("sync", (lambda dp, kb, wblk: lambda e: e.dma_start(
                    out=wblk.rearrange("p (kb d) -> p kb d", kb=8),
                    in_=twd3[:, kb * 8:(kb + 1) * 8, dp * 1024:(dp + 1) * 1024]))(dp, kb, wblk),
                   reads=[("twd",)], writes=tB(*range(8 * par, 8 * par + 8)), dma=("qs", j))
            wd_dma(0)
            for j in range(16):
                dp, kb = divmod(j, 8)
                par = j % 2
                wblk = arbB_bf(par * 8192, 8192)
                if j + 1 < 16:
                    wd_dma(j + 1)
                for nt in range(4):
                    for d2 in range(2):
                        for kk in range(8):
                            k = kb * 8 + kk
                            OP("tensor", (lambda st2, nt, k, kk, d2, wblk: lambda e: e.matmul(
                                P[nt * 2 + d2][:, :], ax_bf[:, k * 512 + nt * 128:k * 512 + nt * 128 + 128],
                                wblk[:, kk * 1024 + d2 * 512:kk * 1024 + (d2 + 1) * 512],
                                start=(k == 0), stop=(k == 63)))(st2, nt, k, kk, d2, wblk),
                               reads=tX(k // 4) + tB(*range(8 * par, 8 * par + 8)),
                               wcheck=tP(nt * 2 + d2) if k == 0 else (),
                               wpub=tP(nt * 2 + d2) if k == 63 else ())
                if kb == 7:
                    for nt in range(4):
                        ntg = st2 * 4 + nt
                        for d2 in range(2):
                            dsl = dp * 2 + d2
                            OP("vector", (lambda st2, nt, ntg, dsl, d2: lambda e: e.scalar_tensor_tensor(
                                x1t(ntg)[:, dsl * 512:(dsl + 1) * 512], P[nt * 2 + d2][:, :], fs_tok[:, ntg:ntg + 1],
                                x1t(ntg)[:, dsl * 512:(dsl + 1) * 512], ALU.mult, ALU.add))(st2, nt, ntg, dsl, d2),
                               reads=tP(nt * 2 + d2) + [("fs",)] + x1t_tok(ntg), writes=x1t_tok(ntg))

        for ntg in range(NT):
            OP("scalar", (lambda ntg: lambda e: e.dma_start(
                out=y_out[ntg * 128:(ntg + 1) * 128, :], in_=x1t(ntg)))(ntg),
               reads=x1t_tok(ntg), writes=[("y", ntg)], dma=("qa", ntg))
        b.barrier("sync", [("y", ntg) for ntg in range(NT)])

        b.finalize()
        block.sync(b.body("sync", sems))
        block.vector(b.body("vector", sems))
        block.scalar(b.body("scalar", sems))
        block.tensor(b.body("tensor", sems))
        block.gpsimd(b.body("gpsimd", sems))

    return nc


_CACHE = {}


def kernel(L0_attn_x, L0_attn_Wq, L0_attn_bq, L0_attn_Wk, L0_attn_bk, L0_attn_Wv,
           L0_attn_bv, L0_attn_Wo, L0_attn_bo, L0_ffn_w_gate_up, L0_ffn_w_down,
           L0_c, L0_wmod):
    x = np.ascontiguousarray(np.asarray(L0_attn_x, dtype=np.float32).reshape(SEQ, D))
    wqT = np.ascontiguousarray(np.asarray(L0_attn_Wq, dtype=np.float32).T)
    wkT = np.ascontiguousarray(np.asarray(L0_attn_Wk, dtype=np.float32).T)
    wvT = np.ascontiguousarray(np.asarray(L0_attn_Wv, dtype=np.float32).T)
    woT = np.ascontiguousarray(np.asarray(L0_attn_Wo, dtype=np.float32).T)
    wguT = np.ascontiguousarray(np.asarray(L0_ffn_w_gate_up, dtype=np.float32).T)
    wdT = np.ascontiguousarray(np.asarray(L0_ffn_w_down, dtype=np.float32).T)
    wmod = np.ascontiguousarray(np.asarray(L0_wmod, dtype=np.float32))
    cvec = np.asarray(L0_c, dtype=np.float32).reshape(D)
    bq = np.asarray(L0_attn_bq, dtype=np.float32).reshape(D)
    bk = np.asarray(L0_attn_bk, dtype=np.float32).reshape(D)
    bv = np.asarray(L0_attn_bv, dtype=np.float32).reshape(D)
    bo = np.asarray(L0_attn_bo, dtype=np.float32).reshape(D)

    cT = np.ascontiguousarray(cvec.reshape(KC, 128).T)
    bq_fm = np.ascontiguousarray(bq.reshape(KC, 128).T)
    bo_fm = np.ascontiguousarray(bo.reshape(KC, 128).T)
    bk_row = bk.reshape(1, D)
    bv_row = bv.reshape(1, D)

    if "nc" not in _CACHE:
        _CACHE["nc"] = build_nc()
    nc = _CACHE["nc"]

    in_maps = []
    for c in range(NCORES):
        xs = np.ascontiguousarray(x[c * T:(c + 1) * T, :].T)
        in_maps.append({
            "x_fm": xs,
            "wqT": wqT, "wkT": wkT, "wvT": wvT, "woT": woT,
            "wguT_sh": np.ascontiguousarray(wguT[c * 256:(c + 1) * 256, :]),
            "wdT_sh": np.ascontiguousarray(wdT[c * 1024:(c + 1) * 1024, :]),
            "wmod_sh": np.ascontiguousarray(wmod[:, c * 1536:(c + 1) * 1536]),
            "cT": cT, "bq_fm": bq_fm, "bo_fm": bo_fm,
            "bk_row": bk_row, "bv_row": bv_row,
            "idx": np.arange(c * T, (c + 1) * T, dtype=np.float32).reshape(1, T),
        })
    res = run_bass_kernel_spmd(nc, in_maps, core_ids=list(range(NCORES)))
    out = np.concatenate([np.asarray(r["y"]) for r in res.results], axis=0)
    return out.reshape(1, SEQ, D).astype(np.float32)


# version 12
# speedup vs baseline: 1.1566x; 1.0517x over previous
import numpy as np
import ml_dtypes
import concourse.bass as bass
import concourse.mybir as mybir
from concourse.bass_utils import run_bass_kernel_spmd
from contextlib import ExitStack

F32 = mybir.dt.float32
BF16 = mybir.dt.bfloat16
I8 = mybir.dt.int8
AF = mybir.ActivationFunctionType
ALU = mybir.AluOpType
AX = mybir.AxisListType.X

NCORES = 8
T = 1024
NT = 8
D = 2048
KC = 16
H = 16
F = 8192
SEQ = 8192
EPS_ATT = 1e-6
EPS_LN = 1e-5
EPS_Q = 1e-5

ENGS = ("tensor", "vector", "scalar", "sync", "gpsimd")
TICK = {"tensor": "pe", "vector": "dve", "scalar": "act", "gpsimd": "pool"}


class _Comp:
    __slots__ = ("sem", "inc", "needed", "val", "idx")

    def __init__(self, sem, inc, idx, needed=False):
        self.sem, self.inc, self.needed, self.val, self.idx = sem, inc, needed, None, idx


class _DmaComp:
    __slots__ = ("q", "total", "tick")

    def __init__(self, q, total):
        self.q, self.total, self.tick = q, total, None


QENG = {"qs": "sync", "qa": "scalar", "qg": "gpsimd"}
QTICK = {"qs": "fns", "qa": "fna", "qg": "fng"}


def _qsem(dma):
    q, par = dma
    return q + str(par % (4 if q == "qs" else 2))


class Bld:

    def __init__(self):
        self.plan = {e: [] for e in ENGS}
        self.writer = {}
        self.readers = {}
        self.water = {e: {} for e in ENGS}
        self.nidx = 0
        self.qissued = {}
        self.qfence = {}

    def _fence(self, dc):
        covered, tick = self.qfence.get(dc.q, (0, None))
        if tick is not None and covered >= dc.total:
            return tick
        qeng = QENG[dc.q[:2]]
        self.nidx += 1
        w = _Comp(dc.q, 16, self.nidx, needed=True)
        w.val = 16 * self.qissued[dc.q]
        self.plan[qeng].append(("W", w))
        self.nidx += 1
        tick = _Comp(QTICK[dc.q[:2]], 1, self.nidx, needed=True)
        self.plan[qeng].append(("o", None, tick))
        self.qfence[dc.q] = (self.qissued[dc.q], tick)
        return tick

    def _dep(self, deps, comp):
        if comp is None:
            return
        if isinstance(comp, _DmaComp):
            comp = self._fence(comp)
        if comp.sem not in deps or comp.idx > deps[comp.sem].idx:
            deps[comp.sem] = comp

    def _waits(self, eng, deps):
        wm = self.water[eng]
        for sem in sorted(deps):
            c = deps[sem]
            if wm.get(sem) is None or c.idx > wm[sem].idx:
                c.needed = True
                self.plan[eng].append(("w", c))
                wm[sem] = c

    def op(self, eng, fn, reads=(), writes=(), wcheck=(), wpub=(), dma=None, comp_sem=None):
        deps = {}
        for t in reads:
            self._dep(deps, self.writer.get(t))
        for t in tuple(writes) + tuple(wcheck):
            self._dep(deps, self.writer.get(t))
            for r in self.readers.get(t, ()):
                self._dep(deps, r)
        self._waits(eng, deps)
        self.nidx += 1
        if comp_sem is not None:
            comp = _Comp(comp_sem[0], comp_sem[1], self.nidx, needed=True)
        elif dma is not None:
            sem = _qsem(dma) if isinstance(dma, tuple) else dma + "0"
            self.qissued[sem] = self.qissued.get(sem, 0) + 1
            comp = _Comp(sem, 16, self.nidx, needed=True)
            self.plan[eng].append(("o", fn, comp))
            dcomp = _DmaComp(sem, self.qissued[sem])
            for t in reads:
                self.readers.setdefault(t, []).append(dcomp)
            for t in tuple(writes) + tuple(wpub):
                self.readers[t] = []
                self.writer[t] = dcomp
            return dcomp
        else:
            comp = _Comp(TICK[eng], 1, self.nidx)
        self.plan[eng].append(("o", fn, comp))
        for t in reads:
            self.readers.setdefault(t, []).append(comp)
        for t in tuple(writes) + tuple(wpub):
            self.readers[t] = []
            self.writer[t] = comp
        return comp

    def barrier(self, eng, tokens):
        deps = {}
        for t in tokens:
            self._dep(deps, self.writer.get(t))
            for r in self.readers.get(t, ()):
                self._dep(deps, r)
        self._waits(eng, deps)

    def finalize(self):
        cnt = {}
        for e in ENGS:
            for item in self.plan[e]:
                if item[0] == "o" and item[2].needed and item[2].val is None:
                    c = item[2]
                    cnt[c.sem] = cnt.get(c.sem, 0) + c.inc
                    c.val = cnt[c.sem]

    def body(self, eng_name, sems):
        items = self.plan[eng_name]

        def run(e):
            for item in items:
                if item[0] in ("w", "W"):
                    e.wait_ge(sems[item[1].sem], item[1].val)
                elif item[1] is None:
                    e.nop().then_inc(sems[item[2].sem], item[2].inc)
                else:
                    inst = item[1](e)
                    if item[2].needed:
                        inst.then_inc(sems[item[2].sem], item[2].inc)
        return run


def build_nc(dbg=False):
    nc = bass.Bass("TRN2", target_bir_lowering=False)

    x_in = nc.dram_tensor("x_fm", [D, T], F32, kind="ExternalInput").ap()
    wqT = nc.dram_tensor("wqT", [D, D], F32, kind="ExternalInput").ap()
    wkT = nc.dram_tensor("wkT", [D, D], F32, kind="ExternalInput").ap()
    wvT = nc.dram_tensor("wvT", [D, D], F32, kind="ExternalInput").ap()
    woT = nc.dram_tensor("woT", [D, D], F32, kind="ExternalInput").ap()
    wguT_sh = nc.dram_tensor("wguT_sh", [256, 2 * F], F32, kind="ExternalInput").ap()
    wdT_sh = nc.dram_tensor("wdT_sh", [1024, D], F32, kind="ExternalInput").ap()
    wmod_sh = nc.dram_tensor("wmod_sh", [D, 1536], F32, kind="ExternalInput").ap()
    cT_in = nc.dram_tensor("cT", [128, 16], F32, kind="ExternalInput").ap()
    bq_in = nc.dram_tensor("bq_fm", [128, 16], F32, kind="ExternalInput").ap()
    bo_in = nc.dram_tensor("bo_fm", [128, 16], F32, kind="ExternalInput").ap()
    bk_in = nc.dram_tensor("bk_row", [1, D], F32, kind="ExternalInput").ap()
    bv_in = nc.dram_tensor("bv_row", [1, D], F32, kind="ExternalInput").ap()
    idx_in = nc.dram_tensor("idx", [1, T], F32, kind="ExternalInput").ap()
    y_out = nc.dram_tensor("y", [T, D], F32, kind="ExternalOutput").ap()

    mod_in = nc.dram_tensor("mod_in", [128, 12], F32).ap()
    mod_out = nc.dram_tensor("mod_out", [128 * NCORES, 12], F32).ap()
    kv_in = nc.dram_tensor("kv_in", [2065, 128], F32).ap()
    kv_out = nc.dram_tensor("kv_out", [2065, 128], F32).ap()
    tgu_in = nc.dram_tensor("tgu_in", [256, 2 * F], BF16).ap()
    tgu = nc.dram_tensor("tgu", [D, 2 * F], BF16).ap()
    twd_in = nc.dram_tensor("twd_in", [1024, D], BF16).ap()
    twd = nc.dram_tensor("twd", [F, D], BF16).ap()
    t_buf = nc.dram_tensor("t_buf", [F, T], F32).ap()
    if dbg:
        dbg_kv = nc.dram_tensor("dbg_kv", [2065, 128], F32, kind="ExternalOutput").ap()
        dbg_mod = nc.dram_tensor("dbg_mod", [128, 96], F32, kind="ExternalOutput").ap()
        dbg_x1 = nc.dram_tensor("dbg_x1", [D, T], F32, kind="ExternalOutput").ap()
        dbg_xq = nc.dram_tensor("dbg_xq", [D, T], BF16, kind="ExternalOutput").ap()
        dbg_t = nc.dram_tensor("dbg_t", [F, T], F32, kind="ExternalOutput").ap()
        dbg_ha = nc.dram_tensor("dbg_ha", [D, T], BF16, kind="ExternalOutput").ap()
        dbg_q = nc.dram_tensor("dbg_q", [D, T], BF16, kind="ExternalOutput").ap()
        dbg_tgu = nc.dram_tensor("dbg_tgu", [D, 2 * F], BF16, kind="ExternalOutput").ap()
        dbg_tgin = nc.dram_tensor("dbg_tgin", [256, 2 * F], BF16, kind="ExternalOutput").ap()
        dbg_twd = nc.dram_tensor("dbg_twd", [F, D], BF16, kind="ExternalOutput").ap()

    b = Bld()

    with ExitStack() as ctx:
        E = ctx.enter_context
        A_x = E(nc.sbuf_tensor("A_x", [128, KC * T], F32))
        ARB = E(nc.sbuf_tensor("ARB", [128, 24576], F32))
        SM = E(nc.sbuf_tensor("SM", [128, 6144], F32))
        ROWS = E(nc.sbuf_tensor("ROWS", [1, 4096], F32))
        dec_fm = E(nc.sbuf_tensor("dec_fm", [128, T], F32))
        idn = E(nc.sbuf_tensor("idn", [128, 128], F32))
        ones_bf = E(nc.sbuf_tensor("ones_bf", [128, 1], BF16))
        ones_col = E(nc.sbuf_tensor("ones_col", [128, 1], F32))
        onesr_f = E(nc.sbuf_tensor("onesr_f", [1, 128], F32))
        onesr_bf = E(nc.sbuf_tensor("onesr_bf", [1, 128], BF16))
        cT = E(nc.sbuf_tensor("cTs", [128, 16], F32))
        bq_fm = E(nc.sbuf_tensor("bq_s", [128, 16], F32))
        bo_fm = E(nc.sbuf_tensor("bo_s", [128, 16], F32))
        idx_tok = E(nc.sbuf_tensor("idx_tok", [128, NT], F32))
        dec_tok = E(nc.sbuf_tensor("dec_tok", [128, NT], F32))
        modf = E(nc.sbuf_tensor("modf", [128, 96], F32))
        onca = E(nc.sbuf_tensor("onca", [128, 16], F32))
        oncb = E(nc.sbuf_tensor("oncb", [128, 16], F32))
        ga_t = E(nc.sbuf_tensor("ga_t", [128, 16], F32))
        gb_t = E(nc.sbuf_tensor("gb_t", [128, 16], F32))
        gsc = E(nc.sbuf_tensor("gsc", [128, 16], F32))
        ksum_col = E(nc.sbuf_tensor("ksum_col", [128, 16], BF16))
        ks16s = E(nc.sbuf_tensor("ks16s", [16, 128], F32))
        zr_tok = E(nc.sbuf_tensor("zr_tok", [128, 128], F32))
        rm1_tok = E(nc.sbuf_tensor("rm1_tok", [128, NT], F32))
        s1_tok = E(nc.sbuf_tensor("s1_tok", [128, NT], F32))
        mx_tok = E(nc.sbuf_tensor("mx_tok", [128, NT], F32))
        fs_tok = E(nc.sbuf_tensor("fs_tok", [128, NT], F32))
        u_tok = E(nc.sbuf_tensor("u_tok", [128, NT], F32))
        sgu_pp = E(nc.sbuf_tensor("sgu_pp", [128, 1], F32))
        swd_pp = E(nc.sbuf_tensor("swd_pp", [128, 1], F32))
        isgu_pp = E(nc.sbuf_tensor("isgu_pp", [128, 1], F32))
        acc_gu = E(nc.sbuf_tensor("acc_gu", [128, 1], F32))
        acc_wd = E(nc.sbuf_tensor("acc_wd", [128, 1], F32))
        red_t = E(nc.sbuf_tensor("red_t", [128, 1], F32))
        epsln = E(nc.sbuf_tensor("epsln", [128, 1], F32))

        P = [E(nc.psum_tensor("P%d" % i, [128, 512], F32)) for i in range(8)]

        semnames = ["pe", "dve", "act", "pool", "syn", "fns", "fna", "fng", "qs0", "qs1", "qs2", "qs3", "qa0", "qa1", "qg0", "qg1", "cc1", "cc2", "cc3", "cc4"]
        sems = {n: E(nc.semaphore(n)) for n in semnames}
        block = E(nc.Block())

        arb_f = ARB.ap()
        arb_bf = ARB.ap().bitcast(BF16)
        ax_f = A_x.ap()
        ax_bf = A_x.ap().bitcast(BF16)
        sm_f = SM.ap()
        sm_bf = SM.ap().bitcast(BF16)
        sm_i8 = SM.ap().bitcast(I8)
        rows_f = ROWS.ap()
        rows_bf = ROWS.ap().bitcast(BF16)
        sc2 = sm_f[0:1, 5632:5760]

        def ha_c(c):
            return arb_bf[:, c * T:(c + 1) * T]

        def q_h(h):
            return arb_bf[:, 16384 + h * T:16384 + (h + 1) * T]

        def arbB_f(off, n):
            return arb_f[:, 8192 + off:8192 + off + n]

        def arbB_bf(off, n):
            return arb_bf[:, 16384 + off:16384 + off + n]

        def arbC_f(off, n):
            return arb_f[:, 16384 + off:16384 + off + n]

        def arbC_bf(off, n):
            return arb_bf[:, 32768 + off:32768 + off + n]

        def tA(*slots):
            return [("A", s) for s in slots]

        def tB(*slots):
            return [("B", s) for s in slots]

        def tC(*slots):
            return [("C", s) for s in slots]

        def tX(*slots):
            return [("X", s) for s in slots]

        def tSA(*slots):
            return [("SA", s) for s in slots]

        def tSB(*slots):
            return [("SB", s) for s in slots]

        def tSC(*slots):
            return [("SC", s) for s in slots]

        def tR(*slots):
            return [("R", s) for s in slots]

        def tP(*idx):
            return [("P", i) for i in idx]

        OP = b.op

        for c in range(KC):
            OP("gpsimd", (lambda c: lambda e: e.dma_start(out=ax_f[:, c * T:(c + 1) * T],
                                                          in_=x_in[c * 128:(c + 1) * 128, :]))(c),
               writes=tX(c), dma=("qg", c))
        idx_row = rows_f[0:1, 3072:4096]
        OP("sync", lambda e: e.dma_start(out=idx_row, in_=idx_in[:, :]), writes=tR(6, 7), dma=("qs", 0))
        OP("sync", lambda e: e.dma_start(out=cT[:, :], in_=cT_in[:, :]), writes=[("cT",)], dma=("qs", 1))
        OP("sync", lambda e: e.dma_start(out=bq_fm[:, :], in_=bq_in[:, :]), writes=[("bq",)], dma=("qs", 0))
        OP("sync", lambda e: e.dma_start(out=bo_fm[:, :], in_=bo_in[:, :]), writes=[("bo",)], dma=("qs", 1))
        bk_row = rows_bf[0:1, 0:2048]
        bv_row = rows_bf[0:1, 2048:4096]

        OP("gpsimd", lambda e: e.memset(idn[:, :], 0.0), writes=[("idn",)])
        OP("gpsimd", lambda e: e.affine_select(out=idn[:, :], in_=idn[:, :], compare_op=ALU.not_equal,
                                               fill=1.0, base=0, pattern=[[-1, 128]], channel_multiplier=1),
           reads=[("idn",)], writes=[("idn",)])
        OP("vector", lambda e: e.memset(ones_bf[:, :], 1.0), writes=[("ones_bf",)])
        OP("vector", lambda e: e.memset(epsln[:, :], EPS_LN), writes=[("epsln",)])
        OP("vector", lambda e: e.memset(sc2[:, :], 0.0), writes=[("sc2",), ("SC", 3)])
        OP("vector", lambda e: e.memset(ones_col[:, :], 1.0), writes=[("ones_col",)])
        OP("vector", lambda e: e.memset(onesr_f[:, :], 1.0), writes=[("onesr_f",)])
        OP("vector", lambda e: e.memset(onesr_bf[:, :], 1.0), writes=[("onesr_bf",)])

        OP("scalar", lambda e: e.activation(cT[:, :], cT[:, :], AF.Silu), reads=[("cT",)], writes=[("cT",)])
        def wmod_dma(k):
            par = k % 2
            buf = arbC_f(par * 2048, 1536)
            OP("sync", (lambda k, buf: lambda e: e.dma_start(out=buf, in_=wmod_sh[k * 128:(k + 1) * 128, :]))(k, buf),
               writes=tC(2 * par, 2 * par + 1), dma=("qs", k))
        wmod_dma(0)
        for k in range(KC):
            par = k % 2
            if k + 1 < KC:
                wmod_dma(k + 1)
            for j in range(3):
                OP("tensor", (lambda k, j, par: lambda e: e.matmul(
                    P[4 + j][0:1, :], cT[:, k:k + 1], arbC_f(par * 2048 + j * 512, 512),
                    start=(k == 0), stop=(k == KC - 1)))(k, j, par),
                   reads=[("cT",)] + tC(2 * par, 2 * par + 1),
                   wcheck=tP(4 + j) if k == 0 else (), wpub=tP(4 + j) if k == KC - 1 else ())
        mrowA = rows_f[0:1, 2048:3072]
        for j in range(2):
            OP("vector", (lambda j: lambda e: e.tensor_copy(mrowA[0:1, j * 512:(j + 1) * 512], P[4 + j][0:1, :]))(j),
               reads=tP(4 + j), writes=tR(4 + j))
        for j in range(8):
            OP("tensor", (lambda j: lambda e: e.transpose(P[7][:, j:j + 1], mrowA[0:1, j * 128:(j + 1) * 128],
                                                          idn[0:1, 0:1]))(j),
               reads=tR(4 + j // 4) + [("idn",)],
               wcheck=tP(7) if j == 0 else (), wpub=tP(7) if j == 7 else ())
        mrowB = rows_f[0:1, 2048:2560]
        OP("vector", lambda e: e.tensor_copy(mrowB, P[6][0:1, :]), reads=tP(6), writes=tR(4))
        for j in range(4):
            OP("tensor", (lambda j: lambda e: e.transpose(P[7][:, 8 + j:9 + j], mrowB[0:1, j * 128:(j + 1) * 128],
                                                          idn[0:1, 0:1]))(j),
               reads=tR(4) + [("idn",)],
               wcheck=(), wpub=tP(7) if j == 3 else ())
        OP("vector", lambda e: e.tensor_copy(modf[:, 0:12], P[7][:, 0:12]), reads=tP(7), writes=[("modloc",)])
        OP("scalar", lambda e: e.dma_start(out=mod_in[:, :], in_=modf[:, 0:12]),
           reads=[("modloc",)], writes=[("mod_in",)], dma=("qa", 0))
        OP("gpsimd", lambda e: e.collective_compute(
            "AllGather", ALU.bypass, replica_groups=[list(range(NCORES))],
            ins=[mod_in.opt()], outs=[mod_out.opt()]),
           reads=[("mod_in",)], writes=[("mod_out",)], comp_sem=("cc1", 1))
        OP("gpsimd", lambda e: e.dma_start(out=modf[:, :].rearrange("p (r j) -> p r j", r=8),
                                           in_=mod_out.rearrange("(r p) j -> p r j", p=128)),
           reads=[("mod_out",), ("modloc",)], writes=[("modf",)], dma=("qg", 0))

        sa_v = modf[:, 0:16]
        ca_v = modf[:, 16:32]
        ga_v = modf[:, 32:48]
        sb_v = modf[:, 48:64]
        cb_v = modf[:, 64:80]
        gb_v = modf[:, 80:96]
        OP("vector", lambda e: e.tensor_scalar(onca[:, :], ca_v, 1.0, None, ALU.add),
           reads=[("modf",)], writes=[("onca",)])
        OP("vector", lambda e: e.tensor_scalar(oncb[:, :], cb_v, 1.0, None, ALU.add),
           reads=[("modf",)], writes=[("oncb",)])
        for gv, gout, gtok in ((ga_v, ga_t, "ga"), (gb_v, gb_t, "gb")):
            OP("scalar", (lambda gv: lambda e: e.activation(gsc[:, :], gv, AF.Sign))(gv),
               reads=[("modf",)], writes=[("gsc",)])
            OP("scalar", (lambda gv, gout: lambda e: e.activation(gout[:, :], gv, AF.Abs))(gv, gout),
               reads=[("modf",)], writes=[(gtok,)])
            OP("vector", (lambda gout: lambda e: e.tensor_scalar_max(gout[:, :], gout[:, :], 0.25))(gout),
               reads=[(gtok,)], writes=[(gtok,)])
            OP("vector", (lambda gout: lambda e: e.tensor_mul(gout[:, :], gout[:, :], gsc[:, :]))(gout),
               reads=[(gtok,), ("gsc",)], writes=[(gtok,)])

        for j in range(NT):
            OP("tensor", (lambda j: lambda e: e.transpose(P[6][:, j:j + 1], idx_row[0:1, j * 128:(j + 1) * 128],
                                                          idn[0:1, 0:1]))(j),
               reads=tR(6, 7) + [("idn",)],
               wcheck=tP(6) if j == 0 else (), wpub=tP(6) if j == NT - 1 else ())
        OP("vector", lambda e: e.tensor_copy(idx_tok[:, :], P[6][:, 0:NT]), reads=tP(6), writes=[("idx_tok",)])
        OP("scalar", lambda e: e.activation(dec_tok[:, :], idx_tok[:, :], AF.Exp, scale=-1.0 / SEQ),
           reads=[("idx_tok",)], writes=[("dec_tok",)])
        for n in range(2):
            OP("tensor", (lambda n: lambda e: e.matmul(P[6][:, :], onesr_f[:, :],
                                                       idx_row[0:1, n * 512:(n + 1) * 512], start=True, stop=True))(n),
               reads=tR(6, 7) + [("onesr_f",)], writes=tP(6))
            OP("scalar", (lambda n: lambda e: e.activation(dec_fm[:, n * 512:(n + 1) * 512], P[6][:, :],
                                                           AF.Exp, scale=-1.0 / SEQ))(n),
               reads=tP(6), writes=[("dec_fm", n)])

        mean_b = sm_f[:, 0:T]
        rstd_b = sm_f[:, T:2 * T]

        def ln_stats_chunk(c, pb):
            par = c % 2
            xb = sm_bf[:, 8192 + par * T:8192 + (par + 1) * T]
            xq_ = sm_bf[:, 8192 + 2 * T + par * T:8192 + 2 * T + (par + 1) * T]
            OP("vector", (lambda c, xb: lambda e: e.tensor_copy(xb, ax_f[:, c * T:(c + 1) * T]))(c, xb),
               reads=tX(c), writes=tSC(par))
            OP("scalar", (lambda c, xq_: lambda e: e.activation(xq_, ax_f[:, c * T:(c + 1) * T], AF.Square))(c, xq_),
               reads=tX(c), writes=tSC(2 + par))
            for n in range(2):
                OP("tensor", (lambda c, n, xb, pb: lambda e: e.matmul(
                    P[pb + n][0:1, :], ones_bf[:, :], xb[:, n * 512:(n + 1) * 512],
                    start=(c == 0), stop=(c == KC - 1)))(c, n, xb, pb),
                   reads=tSC(par) + [("ones_bf",)],
                   wcheck=tP(pb + n) if c == 0 else (), wpub=tP(pb + n) if c == KC - 1 else ())
                OP("tensor", (lambda c, n, xq_, pb: lambda e: e.matmul(
                    P[pb + 2 + n][0:1, :], ones_bf[:, :], xq_[:, n * 512:(n + 1) * 512],
                    start=(c == 0), stop=(c == KC - 1)))(c, n, xq_, pb),
                   reads=tSC(2 + par) + [("ones_bf",)],
                   wcheck=tP(pb + 2 + n) if c == 0 else (), wpub=tP(pb + 2 + n) if c == KC - 1 else ())

        def ln_stats_rows(pb):
            mean_r = rows_f[0:1, 0:1024]
            var_r = rows_f[0:1, 1024:2048]
            m2_r = rows_f[0:1, 2048:2560]
            for n in range(2):
                OP("vector", (lambda n, mean_r, pb: lambda e: e.tensor_scalar(
                    mean_r[0:1, n * 512:(n + 1) * 512], P[pb + n][0:1, :], 1.0 / D, None, ALU.mult))(n, mean_r, pb),
                   reads=tP(pb + n), writes=tR(n))
                OP("vector", (lambda n, var_r, pb: lambda e: e.tensor_scalar(
                    var_r[0:1, n * 512:(n + 1) * 512], P[pb + 2 + n][0:1, :], 1.0 / D, None, ALU.mult))(n, var_r, pb),
                   reads=tP(pb + 2 + n), writes=tR(2 + n))
                OP("vector", (lambda n, mean_r, m2_r: lambda e: e.tensor_mul(
                    m2_r, mean_r[0:1, n * 512:(n + 1) * 512], mean_r[0:1, n * 512:(n + 1) * 512]))(n, mean_r, m2_r),
                   reads=tR(n), writes=tR(4))
                OP("vector", (lambda n, var_r, m2_r: lambda e: e.tensor_sub(
                    var_r[0:1, n * 512:(n + 1) * 512], var_r[0:1, n * 512:(n + 1) * 512], m2_r))(n, var_r, m2_r),
                   reads=tR(2 + n, 4), writes=tR(2 + n))
            OP("scalar", lambda e: e.activation(var_r, var_r, AF.Ln, bias=epsln[0:1, 0:1]),
               reads=tR(2, 3) + [("epsln",)], writes=tR(2, 3))
            OP("scalar", lambda e: e.activation(var_r, var_r, AF.Exp, scale=-0.5), reads=tR(2, 3), writes=tR(2, 3))
            for n in range(2):
                OP("tensor", (lambda n, mean_r, pb: lambda e: e.matmul(
                    P[pb][:, :], onesr_f[:, :], mean_r[0:1, n * 512:(n + 1) * 512], start=True, stop=True))(n, mean_r, pb),
                   reads=tR(n) + [("onesr_f",)], writes=tP(pb))
                OP("vector", (lambda n, pb: lambda e: e.tensor_copy(mean_b[:, n * 512:(n + 1) * 512], P[pb][:, :]))(n, pb),
                   reads=tP(pb), writes=tSA(0))
                OP("tensor", (lambda n, var_r, pb: lambda e: e.matmul(
                    P[pb + 1][:, :], onesr_f[:, :], var_r[0:1, n * 512:(n + 1) * 512], start=True, stop=True))(n, var_r, pb),
                   reads=tR(2 + n) + [("onesr_f",)], writes=tP(pb + 1))
                OP("vector", (lambda n, pb: lambda e: e.tensor_copy(rstd_b[:, n * 512:(n + 1) * 512], P[pb + 1][:, :]))(n, pb),
                   reads=tP(pb + 1), writes=tSA(1))

        def ln_apply_chunk(c, scale_v, shift_v, out_ap, out_tokens, out_dtype_pass=None):
            par = c % 2
            t1 = sm_f[:, 2048 + par * T:2048 + (par + 1) * T]
            OP("vector", (lambda c, t1: lambda e: e.tensor_sub(t1, ax_f[:, c * T:(c + 1) * T], mean_b))(c, t1),
               reads=tX(c) + tSA(0), writes=tSB(2 * par, 2 * par + 1))
            OP("vector", (lambda t1: lambda e: e.tensor_mul(t1, t1, rstd_b))(t1),
               reads=tSB(2 * par, 2 * par + 1) + tSA(1), writes=tSB(2 * par, 2 * par + 1))
            OP("scalar", (lambda c, t1, out_ap, scale_v, shift_v: lambda e: e.activation(
                out_ap, t1, AF.Identity, scale=scale_v[:, c:c + 1], bias=shift_v[:, c:c + 1]))(c, t1, out_ap, scale_v, shift_v),
               reads=tSB(2 * par, 2 * par + 1) + [("onca",), ("oncb",), ("modf",)], writes=out_tokens)

        def ternarize(shard_ap, rows, out_dram, ipp, fold_gb, base_tok):
            yield
            cols = shard_ap.shape[1]
            nr, ncb = rows // 128, cols // 1024
            def t_dma(i):
                rb, cb = divmod(i, ncb)
                par = i % 2
                buf = arbB_f(par * 1024, 1024)
                OP("sync", (lambda rb, cb, buf: lambda e: e.dma_start(
                    out=buf, in_=shard_ap[rb * 128:(rb + 1) * 128, cb * 1024:(cb + 1) * 1024]))(rb, cb, buf),
                   writes=tB(2 * par, 2 * par + 1), dma=("qs", 2 + i % 2))
            t_dma(0)
            for i in range(nr * ncb):
                rb, cb = divmod(i, ncb)
                par = i % 2
                buf = arbB_f(par * 1024, 1024)
                i8b = arbB_f(4096 + par * 512, 256).bitcast(I8)
                ob = arbB_bf(12288 + par * 1024, 1024)
                btok = tB(2 * par, 2 * par + 1)
                if i + 1 < nr * ncb:
                    t_dma(i + 1)
                if True:
                    OP("scalar", (lambda buf, i8b, ipp: lambda e: e.activation(
                        i8b, buf, AF.Identity, scale=ipp[:, 0:1]))(buf, i8b, ipp),
                       reads=btok + [("isgu",), ("iswd",)], writes=tB(8 + par))
                    OP("scalar", (lambda i8b, ob: lambda e: e.activation(
                        ob, i8b, AF.Sign))(i8b, ob),
                       reads=tB(8 + par), writes=tB(12 + par))
                    if fold_gb:
                        OP("vector", (lambda ob, cb: lambda e: e.tensor_mul(
                            ob, ob, gb_b[:, (cb % 2) * 1024:(cb % 2) * 1024 + 1024]))(ob, cb),
                           reads=tB(12 + par) + [("gbb",)], writes=tB(12 + par))
                    OP("scalar", (lambda rb, cb, ob, out_dram: lambda e: e.dma_start(
                        out=out_dram[rb * 128:(rb + 1) * 128, cb * 1024:(cb + 1) * 1024], in_=ob))(rb, cb, ob, out_dram),
                       reads=tB(12 + par), writes=[(base_tok, rb, cb)], dma=("qa", i))
                    yield

        for c in range(KC):
            ln_stats_chunk(c, 0)
        ln_stats_rows(0)
        for c in range(KC):
            ln_apply_chunk(c, onca, sa_v, ha_c(c), tA(c))

        OP("gpsimd", lambda e: e.dma_start(out=bk_row, in_=bk_in[:, :]), writes=tR(0, 1), dma=("qg", 0))
        OP("gpsimd", lambda e: e.dma_start(out=bv_row, in_=bv_in[:, :]), writes=tR(2, 3), dma=("qg", 1))

        def scan_abs(shard_ap, rows, acc, accname):
            cols = shard_ap.shape[1]
            nr, ncb = rows // 128, cols // 2048

            def s_dma(i):
                rb, cb = divmod(i, ncb)
                par = i % 2
                buf = arbB_f(par * 2048, 2048)
                OP("scalar", (lambda rb, cb, buf: lambda e: e.dma_start(
                    out=buf, in_=shard_ap[rb * 128:(rb + 1) * 128, cb * 2048:(cb + 1) * 2048]))(rb, cb, buf),
                   writes=tB(4 * par, 4 * par + 1, 4 * par + 2, 4 * par + 3), dma=("qa", i))
            s_dma(0)
            for i in range(nr * ncb):
                par = i % 2
                buf = arbB_f(par * 2048, 2048)
                if i + 1 < nr * ncb:
                    s_dma(i + 1)
                OP("vector", (lambda buf: lambda e: e.tensor_reduce(
                    red_t[:, :], buf, AX, ALU.add, apply_absolute_value=True))(buf),
                   reads=tB(4 * par, 4 * par + 1, 4 * par + 2, 4 * par + 3), writes=[("red",)])
                if i == 0:
                    OP("vector", (lambda acc: lambda e: e.tensor_copy(acc[:, :], red_t[:, :]))(acc),
                       reads=[("red",)], writes=[(accname,)])
                else:
                    OP("vector", (lambda acc: lambda e: e.tensor_add(acc[:, :], acc[:, :], red_t[:, :]))(acc),
                       reads=[("red",), (accname,)], writes=[(accname,)])

        scan_abs(wguT_sh, 256, acc_gu, "agu")
        scan_abs(wdT_sh, 1024, acc_wd, "awd")


        k_sl = sm_bf[:, 8192:10240]
        v_sl = sm_bf[:, 10240:12288]
        kvloc = sm_f[:, 0:2048]
        ksum_row = rows_f[0:1, 2048:4096]
        wk3 = wkT.rearrange("(kc p) d -> p kc d", p=128)
        wv3 = wvT.rearrange("(kc p) d -> p kc d", p=128)
        def kv_dma(ds):
            par = ds % 2
            kblk = arbC_bf(par * 4096, 4096)
            vblk = arbC_bf(8192 + par * 4096, 4096)
            OP("gpsimd", (lambda ds, kblk: lambda e: e.dma_start(
                out=kblk.rearrange("p (kc d) -> p kc d", kc=KC),
                in_=wk3[:, :, ds * 256:(ds + 1) * 256]))(ds, kblk),
               writes=tC(2 * par, 2 * par + 1), dma=("qg", ds))
            OP("gpsimd", (lambda ds, vblk: lambda e: e.dma_start(
                out=vblk.rearrange("p (kc d) -> p kc d", kc=KC),
                in_=wv3[:, :, ds * 256:(ds + 1) * 256]))(ds, vblk),
               writes=tC(4 + 2 * par, 5 + 2 * par), dma=("qg", ds))
        kv_dma(0)
        for ds in range(8):
            par = ds % 2
            kblk = arbC_bf(par * 4096, 4096)
            vblk = arbC_bf(8192 + par * 4096, 4096)
            if ds + 1 < 8:
                kv_dma(ds + 1)
            for nt in range(NT):
                pk = P[nt % 2]
                pv = P[2 + nt % 2]
                for k in range(KC):
                    OP("tensor", (lambda k, nt, kblk, pk: lambda e: e.matmul(
                        pk[:, 0:256], ha_c(k)[:, nt * 128:(nt + 1) * 128],
                        kblk[:, k * 256:(k + 1) * 256], start=(k == 0), stop=False))(k, nt, kblk, pk),
                       reads=tA(k) + tC(2 * par, 2 * par + 1),
                       wcheck=tP(nt % 2) if k == 0 else ())
                for k in range(KC):
                    OP("tensor", (lambda k, nt, vblk, pv: lambda e: e.matmul(
                        pv[:, 0:256], ha_c(k)[:, nt * 128:(nt + 1) * 128],
                        vblk[:, k * 256:(k + 1) * 256], start=(k == 0), stop=False))(k, nt, vblk, pv),
                       reads=tA(k) + tC(4 + 2 * par, 5 + 2 * par),
                       wcheck=tP(2 + nt % 2) if k == 0 else ())
                OP("tensor", (lambda ds, pk: lambda e: e.matmul(
                    pk[:, 0:256], onesr_bf[:, :], bk_row[0:1, ds * 256:(ds + 1) * 256],
                    start=False, stop=True))(ds, pk),
                   reads=tR(0, 1) + [("onesr_bf",)], wpub=tP(nt % 2))
                OP("tensor", (lambda ds, pv: lambda e: e.matmul(
                    pv[:, 0:256], onesr_bf[:, :], bv_row[0:1, ds * 256:(ds + 1) * 256],
                    start=False, stop=True))(ds, pv),
                   reads=tR(2, 3) + [("onesr_bf",)], wpub=tP(2 + nt % 2))
                OP("scalar", (lambda nt, pk: lambda e: e.activation(
                    k_sl[:, nt * 256:(nt + 1) * 256], pk[:, 0:256], AF.Relu,
                    scale=dec_tok[:, nt:nt + 1]))(nt, pk),
                   reads=tP(nt % 2) + [("dec_tok",)], writes=[("ksl", nt)] + (tSC(0, 1) if nt == 0 and ds == 0 else []))
                OP("scalar", (lambda nt, pv: lambda e: e.activation(
                    v_sl[:, nt * 256:(nt + 1) * 256], pv[:, 0:256], AF.Copy))(nt, pv),
                   reads=tP(2 + nt % 2), writes=[("vsl", nt)] + (tSC(2, 3) if nt == 0 and ds == 0 else []))
            for hh in range(2):
                h = 2 * ds + hh
                for nt in range(NT):
                    OP("tensor", (lambda hh, nt: lambda e: e.matmul(
                        P[4 + hh][:, 0:128],
                        k_sl[:, nt * 256 + hh * 128:nt * 256 + hh * 128 + 128],
                        v_sl[:, nt * 256 + hh * 128:nt * 256 + hh * 128 + 128],
                        start=(nt == 0), stop=(nt == NT - 1)))(hh, nt),
                       reads=[("ksl", nt), ("vsl", nt)],
                       wcheck=tP(4 + hh) if nt == 0 else (), wpub=tP(4 + hh) if nt == NT - 1 else ())
                OP("vector", (lambda h, hh: lambda e: e.tensor_copy(
                    kvloc[:, h * 128:(h + 1) * 128], P[4 + hh][:, 0:128]))(h, hh),
                   reads=tP(4 + hh), writes=[("kvloc", h), ("SA", h // 8)])

            for nt in range(NT):
                OP("tensor", (lambda ds, nt: lambda e: e.matmul(
                    P[6][0:1, 0:256], ones_bf[:, :], k_sl[:, nt * 256:(nt + 1) * 256],
                    start=(nt == 0), stop=(nt == NT - 1)))(ds, nt),
                   reads=[("ksl", nt), ("ones_bf",)],
                   wcheck=tP(6) if nt == 0 else (), wpub=tP(6) if nt == NT - 1 else ())
            OP("vector", (lambda ds: lambda e: e.tensor_copy(
                ksum_row[0:1, ds * 256:(ds + 1) * 256], P[6][0:1, 0:256]))(ds),
               reads=tP(6), writes=[("ksum", ds)] + (tR(4, 5, 6, 7) if ds == 0 else []))

        for i, (acc, accname) in enumerate(((acc_gu, "agu"), (acc_wd, "awd"))):
            OP("tensor", (lambda i, acc: lambda e: e.matmul(
                P[7][0:1, i:i + 1], acc[:, :], ones_col[:, :], start=True, stop=True))(i, acc),
               reads=[(accname,), ("ones_col",)],
               wcheck=tP(7) if i == 0 else (), wpub=tP(7) if i == 1 else ())
        OP("vector", lambda e: e.tensor_copy(sc2[0:1, 0:2], P[7][0:1, 0:2]), reads=tP(7), writes=[("sc2",), ("SC", 3)])

        OP("scalar", lambda e: e.dma_start(
            out=kv_in[0:2048, :].rearrange("(h d) e -> d h e", d=128),
            in_=kvloc[:, :].rearrange("p (h e) -> p h e", h=16)),
           reads=[("kvloc", h) for h in range(H)], writes=[("kv_in", 0)], dma=("qa", 0))
        OP("scalar", lambda e: e.dma_start(
            out=kv_in[2048:2064, :].rearrange("h e -> (h e)").unsqueeze(0), in_=ksum_row),
           reads=[("ksum", ds) for ds in range(8)], writes=[("kv_in", 1)], dma=("qa", 0))
        OP("scalar", lambda e: e.dma_start(out=kv_in[2064:2065, :], in_=sc2),
           reads=[("sc2",), ("SC", 3)], writes=[("kv_in", 2)], dma=("qa", 0))
        OP("gpsimd", lambda e: e.collective_compute(
            "AllReduce", ALU.add, replica_groups=[list(range(NCORES))],
            ins=[kv_in.opt()], outs=[kv_out.opt()]),
           reads=[("kv_in", 0), ("kv_in", 1), ("kv_in", 2)], writes=[("kv_out",)], comp_sem=("cc2", 1))

        wq3 = wqT.rearrange("(kc p) d -> p kc d", p=128)

        def wq_dma(dc):
            par = dc % 2
            blk = arbC_bf(par * 2048, 2048)
            OP("gpsimd", (lambda dc, blk: lambda e: e.dma_start(
                out=blk.rearrange("p (kc d) -> p kc d", kc=KC),
                in_=wq3[:, :, dc * 128:(dc + 1) * 128]))(dc, blk),
               writes=tC(par), dma=("qg", dc))
        wq_dma(0)
        for dc in range(KC):
            par = dc % 2
            blk = arbC_bf(par * 2048, 2048)
            if dc + 1 < KC:
                wq_dma(dc + 1)
            pb = 2 * (dc % 2)
            for n in range(2):
                for k in range(KC):
                    OP("tensor", (lambda n, k, blk, pb: lambda e: e.matmul(
                        P[pb + n][:, :], blk[:, k * 128:(k + 1) * 128], ha_c(k)[:, n * 512:(n + 1) * 512],
                        start=(k == 0), stop=(k == KC - 1)))(n, k, blk, pb),
                       reads=tA(k) + tC(par),
                       wcheck=tP(pb + n) if k == 0 else (), wpub=tP(pb + n) if k == KC - 1 else ())
            for n in range(2):
                pq = P[pb + n]
                t1 = sm_f[:, 2048 + n * 512:2048 + (n + 1) * 512]
                OP("scalar", (lambda dc, n, t1, pq: lambda e: e.activation(
                    t1, pq[:, :], AF.Relu, bias=bq_fm[:, dc:dc + 1]))(dc, n, t1, pq),
                   reads=tP(pb + n) + [("bq",)], writes=tSB(n))
                OP("vector", (lambda dc, n, t1: lambda e: e.tensor_mul(
                    q_h(dc)[:, n * 512:(n + 1) * 512], t1, dec_fm[:, n * 512:(n + 1) * 512]))(dc, n, t1),
                   reads=tSB(n) + [("dec_fm", n)], writes=tB(dc))

        kvall = sm_bf[:, 0:2048]
        OP("gpsimd", lambda e: e.dma_start(
            out=kvall.rearrange("p (h e) -> p h e", h=16),
            in_=kv_out[0:2048, :].rearrange("(h d) e -> d h e", d=128)),
           reads=[("kv_out",)], writes=[("kvall",)] + tSA(0),
           wcheck=[("kvloc", h2) for h2 in range(H)], dma=("qg", 0))
        OP("gpsimd", lambda e: e.dma_start(out=ks16s[:, :], in_=kv_out[2048:2064, :]),
           reads=[("kv_out",)], writes=[("ks16",)], dma=("qg", 1))
        OP("gpsimd", lambda e: e.dma_start(out=sc2[0:1, 0:2], in_=kv_out[2064:2065, 0:2]),
           reads=[("kv_out",), ("sc2",)], writes=[("sc2",), ("SC", 3)], dma=("qg", 0))
        if dbg:
            OP("gpsimd", lambda e: e.dma_start(out=dbg_kv[:, :], in_=kv_out[:, :]),
               reads=[("kv_out",)], writes=[("dbgkv",)], dma=("qg", 0))
            OP("gpsimd", lambda e: e.dma_start(out=dbg_mod[:, :], in_=modf[:, :]),
               reads=[("modf",)], writes=[("dbgmod",)], dma=("qg", 0))
            for c in range(KC):
                OP("gpsimd", (lambda c: lambda e: e.dma_start(
                    out=dbg_ha.bitcast(BF16)[c * 128:(c + 1) * 128, :] if False else dbg_ha[c * 128:(c + 1) * 128, :],
                    in_=ha_c(c)))(c), reads=tA(c), writes=[("dbgha", c)], dma=("qg", 0))
                OP("gpsimd", (lambda c: lambda e: e.dma_start(
                    out=dbg_q[c * 128:(c + 1) * 128, :], in_=q_h(c)))(c),
                   reads=tB(c), writes=[("dbgq", c)], dma=("qg", 0))
        OP("tensor", lambda e: e.transpose(P[6][:, 0:16], ks16s[:, :], idn[0:16, 0:16]),
           reads=[("ks16",), ("idn",)], writes=tP(6))
        OP("vector", lambda e: e.tensor_copy(ksum_col[:, :], P[6][:, 0:16]), reads=tP(6), writes=[("kscol",)])

        for h in range(H):
            for nt in range(NT):
                OP("tensor", (lambda h, nt: lambda e: e.matmul(
                    P[7][:, h * 8 + nt:h * 8 + nt + 1], q_h(h)[:, nt * 128:(nt + 1) * 128],
                    ksum_col[:, h:h + 1], start=True, stop=True))(h, nt),
                   reads=tB(h) + [("kscol",)],
                   wcheck=tP(7) if (h == 0 and nt == 0) else (),
                   wpub=tP(7) if (h == H - 1 and nt == NT - 1) else ())
        OP("vector", lambda e: e.tensor_scalar(zr_tok[:, :], P[7][:, 0:128], EPS_ATT, None, ALU.add),
           reads=tP(7), writes=[("zr",)])
        OP("vector", lambda e: e.reciprocal(zr_tok[:, :], zr_tok[:, :]), reads=[("zr",)], writes=[("zr",)])

        zrow = rows_f[0:1, 2048:3072]
        for h in range(H):
            pa = 2 * (h % 2)
            for n in range(2):
                OP("tensor", (lambda h, n, pa: lambda e: e.matmul(
                    P[pa + n][:, :], kvall[:, h * 128:(h + 1) * 128], q_h(h)[:, n * 512:(n + 1) * 512],
                    start=True, stop=True))(h, n, pa),
                   reads=[("kvall",)] + tB(h), writes=tP(pa + n))
            for half in range(2):
                for j in range(4):
                    nt = half * 4 + j
                    OP("tensor", (lambda h, nt, half, j: lambda e: e.transpose(
                        P[6 + half][0:1, j * 128:(j + 1) * 128],
                        zr_tok[:, h * 8 + nt:h * 8 + nt + 1], idn[:, :]))(h, nt, half, j),
                       reads=[("zr",), ("idn",)],
                       wcheck=tP(6 + half) if j == 0 else (), wpub=tP(6 + half) if j == 3 else ())
                OP("scalar", (lambda half: lambda e: e.copy(
                    zrow[0:1, half * 512:(half + 1) * 512], P[6 + half][0:1, :]))(half),
                   reads=tP(6 + half), writes=tR(4 + half),
                   wcheck=[("ksum", ds2) for ds2 in range(8)] if (h == 0 and half == 0) else ())
            for n in range(2):
                OP("tensor", (lambda n: lambda e: e.matmul(
                    P[4 + n][:, :], onesr_f[:, :], zrow[0:1, n * 512:(n + 1) * 512],
                    start=True, stop=True))(n),
                   reads=tR(4 + n) + [("onesr_f",)], writes=tP(4 + n))
                zb = sm_f[:, 2048 + n * 512:2048 + (n + 1) * 512]
                OP("scalar", (lambda n, zb: lambda e: e.copy(zb, P[4 + n][:, :]))(n, zb),
                   reads=tP(4 + n), writes=tSB(n))
                OP("vector", (lambda h, n, zb, pa: lambda e: e.tensor_mul(
                    ha_c(h)[:, n * 512:(n + 1) * 512], P[pa + n][:, :], zb))(h, n, zb, pa),
                   reads=tP(pa + n) + tSB(n), writes=tA(h))

        OP("vector", lambda e: e.tensor_scalar(sc2[0:1, 0:1], sc2[0:1, 0:1], 1.0 / (2.0 * F * D), None, ALU.mult),
           reads=[("sc2",)], writes=[("sc2",), ("SC", 3)])
        OP("vector", lambda e: e.tensor_scalar(sc2[0:1, 1:2], sc2[0:1, 1:2], 1.0 / (float(F) * D), None, ALU.mult),
           reads=[("sc2",)], writes=[("sc2",), ("SC", 3)])
        OP("vector", lambda e: e.tensor_scalar_max(sc2, sc2, EPS_Q),
           reads=[("sc2",)], writes=[("sc2",), ("SC", 3)])
        for i, dst, tk in ((0, sgu_pp, "sgu"), (1, swd_pp, "swd")):
            OP("tensor", (lambda i: lambda e: e.matmul(
                P[6][:, 0:1], onesr_f[:, :], sc2[0:1, i:i + 1], start=True, stop=True))(i),
               reads=[("sc2",), ("SC", 3), ("onesr_f",)], writes=tP(6))
            OP("vector", (lambda dst: lambda e: e.tensor_copy(dst[:, :], P[6][:, 0:1]))(dst),
               reads=tP(6), writes=[(tk,)])
        OP("vector", lambda e: e.reciprocal(isgu_pp[:, :], sgu_pp[:, :]), reads=[("sgu",)], writes=[("isgu",)])
        iswd_pp = acc_gu
        OP("vector", lambda e: e.reciprocal(iswd_pp[:, :], swd_pp[:, :]), reads=[("swd",)],
           writes=[("iswd",)], wcheck=[("agu",)])

        gbrow = rows_f[0:1, 2048:4096]
        for q4 in range(4):
            for j in range(4):
                OP("tensor", (lambda q4, j: lambda e: e.transpose(
                    P[4 + (q4 % 2)][0:1, j * 128:(j + 1) * 128],
                    gb_t[:, q4 * 4 + j:q4 * 4 + j + 1], idn[:, :]))(q4, j),
                   reads=[("gb",), ("idn",)],
                   wcheck=tP(4 + (q4 % 2)) if j == 0 else (), wpub=tP(4 + (q4 % 2)) if j == 3 else ())
            OP("vector", (lambda q4: lambda e: e.tensor_copy(
                gbrow[0:1, q4 * 512:(q4 + 1) * 512], P[4 + (q4 % 2)][0:1, :]))(q4),
               reads=tP(4 + (q4 % 2)), writes=tR(4 + q4))
        gb_b = sm_f[:, 0:2048]
        for q4 in range(4):
            OP("tensor", (lambda q4: lambda e: e.matmul(
                P[6][:, :], onesr_f[:, :], gbrow[0:1, q4 * 512:(q4 + 1) * 512], start=True, stop=True))(q4),
               reads=tR(4 + q4) + [("onesr_f",)], writes=tP(6))
            OP("vector", (lambda q4: lambda e: e.tensor_copy(gb_b[:, q4 * 512:(q4 + 1) * 512], P[6][:, :]))(q4),
               reads=tP(6), writes=tSA(0, 1)[q4 // 2:q4 // 2 + 1] + [("gbb",)], wcheck=[("kvall",)] if q4 == 0 else ())

        import itertools as _it
        tern_drip = _it.chain(ternarize(wguT_sh, 256, tgu_in, isgu_pp, False, "tgin"),
                              ternarize(wdT_sh, 1024, twd_in, iswd_pp, True, "twin"))
        next(tern_drip, None)

        wo3 = woT.rearrange("(kc p) d -> p kc d", p=128)

        def wo_dma(dc):
            par = dc % 2
            blk = arbC_bf(par * 2048, 2048)
            OP("gpsimd", (lambda dc, blk: lambda e: e.dma_start(
                out=blk.rearrange("p (kc d) -> p kc d", kc=KC),
                in_=wo3[:, :, dc * 128:(dc + 1) * 128]))(dc, blk),
               writes=tC(par), dma=("qg", dc))
        wo_dma(0)
        for dc in range(KC):
            par = dc % 2
            blk = arbC_bf(par * 2048, 2048)
            if dc + 1 < KC:
                wo_dma(dc + 1)
            pb = 2 * (dc % 2)
            for n in range(2):
                for k in range(KC):
                    OP("tensor", (lambda n, k, blk, pb: lambda e: e.matmul(
                        P[pb + n][:, :], blk[:, k * 128:(k + 1) * 128], ha_c(k)[:, n * 512:(n + 1) * 512],
                        start=(k == 0), stop=(k == KC - 1)))(n, k, blk, pb),
                       reads=tA(k) + tC(par),
                       wcheck=tP(pb + n) if k == 0 else (), wpub=tP(pb + n) if k == KC - 1 else ())
            for n in range(2):
                pq = P[pb + n]
                t1 = sm_f[:, 2048 + n * 512:2048 + (n + 1) * 512]
                OP("scalar", (lambda dc, n, t1, pq: lambda e: e.activation(
                    t1, pq[:, :], AF.Identity, bias=bo_fm[:, dc:dc + 1]))(dc, n, t1, pq),
                   reads=tP(pb + n) + [("bo",)], writes=tSB(n))
                OP("vector", (lambda dc, n, t1: lambda e: e.scalar_tensor_tensor(
                    ax_f[:, dc * T + n * 512:dc * T + (n + 1) * 512], t1, ga_t[:, dc:dc + 1],
                    ax_f[:, dc * T + n * 512:dc * T + (n + 1) * 512], ALU.mult, ALU.add))(dc, n, t1),
                   reads=tSB(n) + [("ga",)] + tX(dc), writes=tX(dc))
            ln_stats_chunk(dc, 4)
            for _ in range(3):
                next(tern_drip, None)

        if dbg:
            for c in range(KC):
                OP("gpsimd", (lambda c: lambda e: e.dma_start(
                    out=dbg_x1[c * 128:(c + 1) * 128, :], in_=ax_f[:, c * T:(c + 1) * T]))(c),
                   reads=tX(c), writes=[("dbgx1", c)], dma=("qg", 0))


        for _ in tern_drip:
            pass
        OP("gpsimd", lambda e: e.collective_compute(
            "AllGather", ALU.bypass, replica_groups=[list(range(NCORES))],
            ins=[tgu_in.opt()], outs=[tgu.opt()]),
           reads=[("tgin", rb, cb) for rb in range(2) for cb in range(16)],
           writes=[("tgu",)], comp_sem=("cc3", 1))
        OP("gpsimd", lambda e: e.collective_compute(
            "AllGather", ALU.bypass, replica_groups=[list(range(NCORES))],
            ins=[twd_in.opt()], outs=[twd.opt()]),
           reads=[("twin", rb, cb) for rb in range(8) for cb in range(2)],
           writes=[("twd",)], comp_sem=("cc4", 1))

        ln_stats_rows(4)
        mxb = sm_f[:, 4096:5120]
        mnb = sm_f[:, 5120:6144]

        def hb_c(c):
            if c < 8:
                return arb_f[:, c * T:(c + 1) * T]
            return arbC_f((c - 8) * T, T)

        def hb_tok(c):
            if c < 8:
                return tA(2 * c, 2 * c + 1)
            return tC(c - 8)

        for c in range(KC):
            par = c % 2
            t1 = sm_f[:, 2048 + par * T:2048 + (par + 1) * T]
            hb = hb_c(c)
            OP("vector", (lambda c, t1: lambda e: e.tensor_sub(t1, ax_f[:, c * T:(c + 1) * T], mean_b))(c, t1),
               reads=tX(c) + tSA(0), writes=tSB(2 * par, 2 * par + 1))
            OP("vector", (lambda t1: lambda e: e.tensor_mul(t1, t1, rstd_b))(t1),
               reads=tSB(2 * par, 2 * par + 1) + tSA(1), writes=tSB(2 * par, 2 * par + 1))
            OP("scalar", (lambda c, t1, hb: lambda e: e.activation(
                hb, t1, AF.Identity, scale=oncb[:, c:c + 1], bias=sb_v[:, c:c + 1]))(c, t1, hb),
               reads=tSB(2 * par, 2 * par + 1) + [("oncb",), ("modf",)], writes=hb_tok(c))
            if c == 0:
                OP("vector", (lambda hb: lambda e: e.tensor_copy(mxb, hb))(hb),
                   reads=hb_tok(c),
                   writes=tSC(0, 1), wcheck=[("ksl", nt2) for nt2 in range(NT)] + [("vsl", nt2) for nt2 in range(NT)])
                OP("vector", (lambda hb: lambda e: e.tensor_copy(mnb, hb))(hb),
                   reads=hb_tok(c), writes=tSC(2, 3))
            else:
                OP("vector", (lambda hb: lambda e: e.tensor_tensor(mxb, mxb, hb, ALU.max))(hb),
                   reads=hb_tok(c) + tSC(0, 1), writes=tSC(0, 1))
                OP("vector", (lambda hb: lambda e: e.tensor_tensor(mnb, mnb, hb, ALU.min))(hb),
                   reads=hb_tok(c) + tSC(2, 3), writes=tSC(2, 3))
        OP("vector", lambda e: e.tensor_scalar(mnb, mnb, -1.0, None, ALU.mult),
           reads=tSC(2, 3), writes=tSC(2, 3))
        OP("vector", lambda e: e.tensor_max(mxb, mxb, mnb),
           reads=tSC(0, 1) + tSC(2, 3), writes=tSC(0, 1))
        for nt in range(NT):
            OP("tensor", (lambda nt: lambda e: e.transpose(
                P[6][:, 0:128], mxb[:, nt * 128:(nt + 1) * 128], idn[:, :]))(nt),
               reads=tSC(0, 1) + [("idn",)], writes=tP(6))
            OP("vector", (lambda nt: lambda e: e.tensor_reduce(
                rm1_tok[:, nt:nt + 1], P[6][:, 0:128], AX, ALU.max))(nt),
               reads=tP(6), writes=[("rm1",)] if nt == NT - 1 else [("rm1",)])
        OP("vector", lambda e: e.tensor_scalar_max(rm1_tok[:, :], rm1_tok[:, :], EPS_Q),
           reads=[("rm1",)], writes=[("rm1",)])
        OP("vector", lambda e: e.reciprocal(s1_tok[:, :], rm1_tok[:, :]), reads=[("rm1",)], writes=[("s1",)])
        OP("vector", lambda e: e.tensor_scalar(s1_tok[:, :], s1_tok[:, :], 127.0, None, ALU.mult),
           reads=[("s1",)], writes=[("s1",)])
        s1row = rows_f[0:1, 2048:3072]
        for half in range(2):
            for j in range(4):
                nt = half * 4 + j
                OP("tensor", (lambda nt, half, j: lambda e: e.transpose(
                    P[4 + half][0:1, j * 128:(j + 1) * 128], s1_tok[:, nt:nt + 1], idn[:, :]))(nt, half, j),
                   reads=[("s1",), ("idn",)],
                   wcheck=tP(4 + half) if j == 0 else (), wpub=tP(4 + half) if j == 3 else ())
            OP("vector", (lambda half: lambda e: e.tensor_copy(
                s1row[0:1, half * 512:(half + 1) * 512], P[4 + half][0:1, :]))(half),
               reads=tP(4 + half), writes=tR(4 + half))
        S1_b = sm_f[:, 4096:5120]
        for half in range(2):
            OP("tensor", (lambda half: lambda e: e.matmul(
                P[4 + half][:, :], onesr_f[:, :], s1row[0:1, half * 512:(half + 1) * 512],
                start=True, stop=True))(half),
               reads=tR(4 + half) + [("onesr_f",)], writes=tP(4 + half))
            OP("vector", (lambda half: lambda e: e.tensor_copy(
                S1_b[:, half * 512:(half + 1) * 512], P[4 + half][:, :]))(half),
               reads=tP(4 + half), writes=tSC(half) + [("s1b",)])
        i8q = sm_i8[:, 20480:21504]
        for c in range(KC):
            hb = hb_c(c)
            par = c % 2
            i8qp = sm_i8[:, 20480 + par * 1024:21504 + par * 1024]
            OP("vector", (lambda hb, i8qp: lambda e: e.tensor_mul(i8qp, hb, S1_b))(hb, i8qp),
               reads=hb_tok(c) + [("s1b",)], writes=[("i8q", par)] + tSC(2 + par))
            OP("scalar", (lambda c, i8qp: lambda e: e.activation(q_h(c), i8qp, AF.Copy))(c, i8qp),
               reads=[("i8q", par)], writes=tB(c))

        if dbg:
            for c in range(KC):
                OP("gpsimd", (lambda c: lambda e: e.dma_start(
                    out=dbg_xq[c * 128:(c + 1) * 128, :], in_=q_h(c)))(c),
                   reads=tB(c), writes=[("dbgxq", c)], dma=("qg", 0))

        def x1t(nt):
            if nt < 4:
                return arb_f[:, nt * 2048:(nt + 1) * 2048]
            return arbC_f((nt - 4) * 2048, 2048)

        def x1t_tok(nt):
            if nt < 4:
                return tA(4 * nt, 4 * nt + 1, 4 * nt + 2, 4 * nt + 3)
            return tC(2 * (nt - 4), 2 * (nt - 4) + 1)

        for c in range(KC):
            for nt in range(NT):
                pidx = nt % 4
                OP("tensor", (lambda c, nt, pidx: lambda e: e.transpose(
                    P[pidx][:, 0:128], ax_f[:, c * T + nt * 128:c * T + (nt + 1) * 128], idn[:, :]))(c, nt, pidx),
                   reads=tX(c) + [("idn",)], writes=tP(pidx))
                OP("scalar", (lambda c, nt, pidx: lambda e: e.copy(
                    x1t(nt)[:, c * 128:(c + 1) * 128], P[pidx][:, 0:128]))(c, nt, pidx),
                   reads=tP(pidx), writes=x1t_tok(nt) if c == 0 else [x1t_tok(nt)[c % len(x1t_tok(nt))]])

        OP("vector", lambda e: e.tensor_scalar(u_tok[:, :], rm1_tok[:, :], sgu_pp[:, 0:1], None, ALU.mult),
           reads=[("rm1",), ("sgu",)], writes=[("sk_tok",)])
        OP("vector", lambda e: e.tensor_scalar(u_tok[:, :], u_tok[:, :], 1.0 / 127.0, None, ALU.mult),
           reads=[("sk_tok",)], writes=[("sk_tok",)])
        skrow = rows_f[0:1, 3072:4096]
        for half in range(2):
            for j in range(4):
                nt = half * 4 + j
                OP("tensor", (lambda nt, half, j: lambda e: e.transpose(
                    P[4 + half][0:1, j * 128:(j + 1) * 128], u_tok[:, nt:nt + 1], idn[:, :]))(nt, half, j),
                   reads=[("sk_tok",), ("idn",)],
                   wcheck=tP(4 + half) if j == 0 else (), wpub=tP(4 + half) if j == 3 else ())
            OP("vector", (lambda half: lambda e: e.tensor_copy(
                skrow[0:1, half * 512:(half + 1) * 512], P[4 + half][0:1, :]))(half),
               reads=tP(4 + half), writes=tR(6 + half))
        sk_b = sm_f[:, 0:T]
        for half in range(2):
            OP("tensor", (lambda half: lambda e: e.matmul(
                P[4 + half][:, :], onesr_f[:, :], skrow[0:1, half * 512:(half + 1) * 512],
                start=True, stop=True))(half),
               reads=tR(6 + half) + [("onesr_f",)], writes=tP(4 + half))
            OP("vector", (lambda half: lambda e: e.tensor_copy(
                sk_b[:, half * 512:(half + 1) * 512], P[4 + half][:, :]))(half),
               reads=tP(4 + half), writes=tSA(0) + [("skb",)], wcheck=[("gbb",)] if half == 0 else ())

        tgu3 = tgu.rearrange("(kc p) f -> p kc f", p=128)
        mxt = sm_f[:, 1024:2048]
        mnt = sm_f[:, 4096:5120]
        def gu_dma(g2):
            par = g2 % 2
            ablk = ax_bf[:, par * 4096:par * 4096 + 4096]
            bblk = ax_bf[:, 8192 + par * 4096:8192 + par * 4096 + 4096]
            OP("sync", (lambda g2, ablk: lambda e: e.dma_start(
                out=ablk.rearrange("p (kc f) -> p kc f", kc=KC),
                in_=tgu3[:, :, g2 * 256:(g2 + 1) * 256]))(g2, ablk),
               reads=[("tgu",)], writes=tX(2 * par, 2 * par + 1), dma=("qs", g2))
            OP("sync", (lambda g2, bblk: lambda e: e.dma_start(
                out=bblk.rearrange("p (kc f) -> p kc f", kc=KC),
                in_=tgu3[:, :, F + g2 * 256:F + (g2 + 1) * 256]))(g2, bblk),
               reads=[("tgu",)], writes=tX(4 + 2 * par, 5 + 2 * par), dma=("qs", g2))
        gu_dma(0)
        for g2 in range(32):
            par = g2 % 2
            ablk = ax_bf[:, par * 4096:par * 4096 + 4096]
            bblk = ax_bf[:, 8192 + par * 4096:8192 + par * 4096 + 4096]
            if g2 + 1 < 32:
                gu_dma(g2 + 1)
            for sub in range(2):
                fc = 2 * g2 + sub
                pa = (P[0], P[1]) if sub == 0 else (P[2], P[3])
                pb = (P[4], P[5]) if sub == 0 else (P[6], P[7])
                for n in range(2):
                    for k in range(KC):
                        OP("tensor", (lambda sub, n, k, ablk, pa: lambda e: e.matmul(
                            pa[n][:, :], ablk[:, k * 256 + sub * 128:k * 256 + sub * 128 + 128],
                            q_h(k)[:, n * 512:(n + 1) * 512], start=(k == 0), stop=(k == KC - 1)))(sub, n, k, ablk, pa),
                           reads=tX(2 * par, 2 * par + 1) + tB(k),
                           wcheck=tP(P.index(pa[n])) if k == 0 else (),
                           wpub=tP(P.index(pa[n])) if k == KC - 1 else ())
                for n in range(2):
                    for k in range(KC):
                        OP("tensor", (lambda sub, n, k, bblk, pb: lambda e: e.matmul(
                            pb[n][:, :], bblk[:, k * 256 + sub * 128:k * 256 + sub * 128 + 128],
                            q_h(k)[:, n * 512:(n + 1) * 512], start=(k == 0), stop=(k == KC - 1)))(sub, n, k, bblk, pb),
                           reads=tX(4 + 2 * par, 5 + 2 * par) + tB(k),
                           wcheck=tP(P.index(pb[n])) if k == 0 else (),
                           wpub=tP(P.index(pb[n])) if k == KC - 1 else ())
                tpar = fc % 2
                tst = ax_f[:, 8192 + tpar * T:8192 + (tpar + 1) * T]
                sil = ax_f[:, 10240:11264]
                for n in range(2):
                    OP("vector", (lambda n, tst, pa: lambda e: e.tensor_mul(
                        tst[:, n * 512:(n + 1) * 512], pa[n][:, :], sk_b[:, n * 512:(n + 1) * 512]))(n, tst, pa),
                       reads=tP(P.index(pa[n])) + [("skb",)], writes=tX(8 + tpar))
                OP("scalar", (lambda tst, sil: lambda e: e.activation(sil, tst, AF.Silu))(tst, sil),
                   reads=tX(8 + tpar), writes=tX(10))
                for n in range(2):
                    OP("vector", (lambda n, tst, sil, pb: lambda e: e.tensor_mul(
                        tst[:, n * 512:(n + 1) * 512], sil[:, n * 512:(n + 1) * 512], pb[n][:, :]))(n, tst, sil, pb),
                       reads=tX(10) + tP(P.index(pb[n])), writes=tX(8 + tpar))
                if g2 == 0 and sub == 0:
                    OP("vector", (lambda tst: lambda e: e.tensor_copy(mxt, tst))(tst),
                       reads=tX(8 + tpar), writes=tSA(1))
                    OP("vector", (lambda tst: lambda e: e.tensor_copy(mnt, tst))(tst),
                       reads=tX(8 + tpar), writes=tSC(0, 1))
                else:
                    OP("vector", (lambda tst: lambda e: e.tensor_tensor(mxt, mxt, tst, ALU.max))(tst),
                       reads=tX(8 + tpar) + tSA(1), writes=tSA(1))
                    OP("vector", (lambda tst: lambda e: e.tensor_tensor(mnt, mnt, tst, ALU.min))(tst),
                       reads=tX(8 + tpar) + tSC(0, 1), writes=tSC(0, 1))
                OP("scalar", (lambda fc, tst: lambda e: e.dma_start(
                    out=t_buf[fc * 128:(fc + 1) * 128, :], in_=tst))(fc, tst),
                   reads=tX(8 + tpar), writes=[("tb", fc)], dma=("qa", fc))

        if dbg:
            OP("gpsimd", lambda e: e.dma_start(out=dbg_t[:, :], in_=t_buf[:, :]),
               reads=[("tb", fc2) for fc2 in range(64)], writes=[("dbgt",)], dma=("qg", 0))
            OP("gpsimd", lambda e: e.dma_start(out=dbg_tgu[:, :], in_=tgu[:, :]),
               reads=[("tgu",)], writes=[("dbgtgu",)], dma=("qg", 0))
            OP("gpsimd", lambda e: e.dma_start(out=dbg_tgin[:, :], in_=tgu_in[:, :]),
               reads=[("tgin", rb, cb) for rb in range(2) for cb in range(16)],
               writes=[("dbgtgin",)], dma=("qg", 0))
            OP("gpsimd", lambda e: e.dma_start(out=dbg_twd[:, :], in_=twd[:, :]),
               reads=[("twd",)], writes=[("dbgtwd",)], dma=("qg", 0))
        OP("vector", lambda e: e.tensor_scalar(mnt, mnt, -1.0, None, ALU.mult),
           reads=tSC(0, 1), writes=tSC(0, 1))
        OP("vector", lambda e: e.tensor_max(mxt, mxt, mnt),
           reads=tSA(1) + tSC(0, 1), writes=tSA(1))
        for nt in range(NT):
            OP("tensor", (lambda nt: lambda e: e.transpose(
                P[6][:, 0:128], mxt[:, nt * 128:(nt + 1) * 128], idn[:, :]))(nt),
               reads=tSA(1) + [("idn",)], writes=tP(6))
            OP("vector", (lambda nt: lambda e: e.tensor_reduce(
                mx_tok[:, nt:nt + 1], P[6][:, 0:128], AX, ALU.max))(nt),
               reads=tP(6), writes=[("mxt",)])
        OP("vector", lambda e: e.tensor_mul(fs_tok[:, :], u_tok[:, :], mx_tok[:, :]),
           reads=[("sk_tok",), ("mxt",)], writes=[("rm2",)])
        OP("vector", lambda e: e.tensor_scalar_max(fs_tok[:, :], fs_tok[:, :], EPS_Q),
           reads=[("rm2",)], writes=[("rm2",)])
        OP("vector", lambda e: e.reciprocal(mx_tok[:, :], fs_tok[:, :]), reads=[("rm2",)], writes=[("irm2",)])
        OP("vector", lambda e: e.tensor_mul(mx_tok[:, :], mx_tok[:, :], u_tok[:, :]),
           reads=[("irm2",), ("sk_tok",)], writes=[("irm2",)])
        OP("vector", lambda e: e.tensor_scalar(mx_tok[:, :], mx_tok[:, :], 127.0, None, ALU.mult),
           reads=[("irm2",)], writes=[("uq",)])
        OP("vector", lambda e: e.tensor_scalar(fs_tok[:, :], fs_tok[:, :], swd_pp[:, 0:1], None, ALU.mult),
           reads=[("rm2",), ("swd",)], writes=[("fs",)])
        OP("vector", lambda e: e.tensor_scalar(fs_tok[:, :], fs_tok[:, :], 1.0 / 127.0, None, ALU.mult),
           reads=[("fs",)], writes=[("fs",)])
        urow = rows_f[0:1, 2048:3072]
        for half in range(2):
            for j in range(4):
                nt = half * 4 + j
                OP("tensor", (lambda nt, half, j: lambda e: e.transpose(
                    P[4 + half][0:1, j * 128:(j + 1) * 128], mx_tok[:, nt:nt + 1], idn[:, :]))(nt, half, j),
                   reads=[("uq",), ("idn",)],
                   wcheck=tP(4 + half) if j == 0 else (), wpub=tP(4 + half) if j == 3 else ())
            OP("vector", (lambda half: lambda e: e.tensor_copy(
                urow[0:1, half * 512:(half + 1) * 512], P[4 + half][0:1, :]))(half),
               reads=tP(4 + half), writes=tR(4 + half))
        c2_b = sm_f[:, 0:T]
        for half in range(2):
            OP("tensor", (lambda half: lambda e: e.matmul(
                P[4 + half][:, :], onesr_f[:, :], urow[0:1, half * 512:(half + 1) * 512],
                start=True, stop=True))(half),
               reads=tR(4 + half) + [("onesr_f",)], writes=tP(4 + half))
            OP("vector", (lambda half: lambda e: e.tensor_copy(
                c2_b[:, half * 512:(half + 1) * 512], P[4 + half][:, :]))(half),
               reads=tP(4 + half), writes=tSA(0) + [("c2b",)], wcheck=[("skb",)] if half == 0 else ())

        i8d = sm_i8[:, 20480:21504]
        twd3 = twd.rearrange("(kb p) d -> p kb d", p=128)
        for st2 in range(2):
            def tq_dma(k, st2=st2):
                par = k % 2
                tq = sm_f[:, 2048 + par * 512:2048 + (par + 1) * 512]
                OP("sync", (lambda k, st2, tq: lambda e: e.dma_start(
                    out=tq, in_=t_buf[k * 128:(k + 1) * 128, st2 * 512:(st2 + 1) * 512]))(k, st2, tq),
                   reads=[("tb", k)], writes=tSB(par), dma=("qs", 2 + k % 2))
            tq_dma(0)
            for k in range(F // 128):
                par = k % 2
                tq = sm_f[:, 2048 + par * 512:2048 + (par + 1) * 512]
                if k + 1 < F // 128:
                    tq_dma(k + 1)
                OP("vector", (lambda k, tq, st2: lambda e: e.tensor_mul(
                    i8d[:, 0:512], tq, c2_b[:, st2 * 512:(st2 + 1) * 512]))(k, tq, st2),
                   reads=tSB(par) + [("c2b",)], writes=tSC(2))
                OP("scalar", (lambda k, st2: lambda e: e.activation(
                    ax_bf[:, k * 512:(k + 1) * 512], i8d[:, 0:512], AF.Copy))(k, st2),
                   reads=tSC(2), writes=tX(k // 4))
            def wd_dma(j, st2=st2):
                dp, kb = divmod(j, 8)
                par = j % 2
                wblk = arbB_bf(par * 8192, 8192)
                OP("sync", (lambda dp, kb, wblk: lambda e: e.dma_start(
                    out=wblk.rearrange("p (kb d) -> p kb d", kb=8),
                    in_=twd3[:, kb * 8:(kb + 1) * 8, dp * 1024:(dp + 1) * 1024]))(dp, kb, wblk),
                   reads=[("twd",)], writes=tB(*range(8 * par, 8 * par + 8)), dma=("qs", j))
            wd_dma(0)
            for j in range(16):
                dp, kb = divmod(j, 8)
                par = j % 2
                wblk = arbB_bf(par * 8192, 8192)
                if j + 1 < 16:
                    wd_dma(j + 1)
                for nt in range(4):
                    for d2 in range(2):
                        for kk in range(8):
                            k = kb * 8 + kk
                            OP("tensor", (lambda st2, nt, k, kk, d2, wblk: lambda e: e.matmul(
                                P[nt * 2 + d2][:, :], ax_bf[:, k * 512 + nt * 128:k * 512 + nt * 128 + 128],
                                wblk[:, kk * 1024 + d2 * 512:kk * 1024 + (d2 + 1) * 512],
                                start=(k == 0), stop=(k == 63)))(st2, nt, k, kk, d2, wblk),
                               reads=tX(k // 4) + tB(*range(8 * par, 8 * par + 8)),
                               wcheck=tP(nt * 2 + d2) if k == 0 else (),
                               wpub=tP(nt * 2 + d2) if k == 63 else ())
                if kb == 7:
                    for nt in range(4):
                        ntg = st2 * 4 + nt
                        for d2 in range(2):
                            dsl = dp * 2 + d2
                            OP("vector", (lambda st2, nt, ntg, dsl, d2: lambda e: e.scalar_tensor_tensor(
                                x1t(ntg)[:, dsl * 512:(dsl + 1) * 512], P[nt * 2 + d2][:, :], fs_tok[:, ntg:ntg + 1],
                                x1t(ntg)[:, dsl * 512:(dsl + 1) * 512], ALU.mult, ALU.add))(st2, nt, ntg, dsl, d2),
                               reads=tP(nt * 2 + d2) + [("fs",)] + x1t_tok(ntg), writes=x1t_tok(ntg))

        for ntg in range(NT):
            OP("scalar", (lambda ntg: lambda e: e.dma_start(
                out=y_out[ntg * 128:(ntg + 1) * 128, :], in_=x1t(ntg)))(ntg),
               reads=x1t_tok(ntg), writes=[("y", ntg)], dma=("qa", ntg))
        b.barrier("sync", [("y", ntg) for ntg in range(NT)])

        b.finalize()
        block.sync(b.body("sync", sems))
        block.vector(b.body("vector", sems))
        block.scalar(b.body("scalar", sems))
        block.tensor(b.body("tensor", sems))
        block.gpsimd(b.body("gpsimd", sems))

    return nc


_CACHE = {}


def kernel(L0_attn_x, L0_attn_Wq, L0_attn_bq, L0_attn_Wk, L0_attn_bk, L0_attn_Wv,
           L0_attn_bv, L0_attn_Wo, L0_attn_bo, L0_ffn_w_gate_up, L0_ffn_w_down,
           L0_c, L0_wmod):
    x = np.ascontiguousarray(np.asarray(L0_attn_x, dtype=np.float32).reshape(SEQ, D))
    wqT = np.ascontiguousarray(np.asarray(L0_attn_Wq, dtype=np.float32).T)
    wkT = np.ascontiguousarray(np.asarray(L0_attn_Wk, dtype=np.float32).T)
    wvT = np.ascontiguousarray(np.asarray(L0_attn_Wv, dtype=np.float32).T)
    woT = np.ascontiguousarray(np.asarray(L0_attn_Wo, dtype=np.float32).T)
    wguT = np.ascontiguousarray(np.asarray(L0_ffn_w_gate_up, dtype=np.float32).T)
    wdT = np.ascontiguousarray(np.asarray(L0_ffn_w_down, dtype=np.float32).T)
    wmod = np.ascontiguousarray(np.asarray(L0_wmod, dtype=np.float32))
    cvec = np.asarray(L0_c, dtype=np.float32).reshape(D)
    bq = np.asarray(L0_attn_bq, dtype=np.float32).reshape(D)
    bk = np.asarray(L0_attn_bk, dtype=np.float32).reshape(D)
    bv = np.asarray(L0_attn_bv, dtype=np.float32).reshape(D)
    bo = np.asarray(L0_attn_bo, dtype=np.float32).reshape(D)

    cT = np.ascontiguousarray(cvec.reshape(KC, 128).T)
    bq_fm = np.ascontiguousarray(bq.reshape(KC, 128).T)
    bo_fm = np.ascontiguousarray(bo.reshape(KC, 128).T)
    bk_row = bk.reshape(1, D)
    bv_row = bv.reshape(1, D)

    if "nc" not in _CACHE:
        _CACHE["nc"] = build_nc()
    nc = _CACHE["nc"]

    in_maps = []
    for c in range(NCORES):
        xs = np.ascontiguousarray(x[c * T:(c + 1) * T, :].T)
        in_maps.append({
            "x_fm": xs,
            "wqT": wqT, "wkT": wkT, "wvT": wvT, "woT": woT,
            "wguT_sh": np.ascontiguousarray(wguT[c * 256:(c + 1) * 256, :]),
            "wdT_sh": np.ascontiguousarray(wdT[c * 1024:(c + 1) * 1024, :]),
            "wmod_sh": np.ascontiguousarray(wmod[:, c * 1536:(c + 1) * 1536]),
            "cT": cT, "bq_fm": bq_fm, "bo_fm": bo_fm,
            "bk_row": bk_row, "bv_row": bv_row,
            "idx": np.arange(c * T, (c + 1) * T, dtype=np.float32).reshape(1, T),
        })
    res = run_bass_kernel_spmd(nc, in_maps, core_ids=list(range(NCORES)))
    out = np.concatenate([np.asarray(r["y"]) for r in res.results], axis=0)
    return out.reshape(1, SEQ, D).astype(np.float32)


# version 13
# speedup vs baseline: 1.2594x; 1.0889x over previous
import numpy as np
import ml_dtypes
import concourse.bass as bass
import concourse.mybir as mybir
from concourse.bass_utils import run_bass_kernel_spmd
from contextlib import ExitStack

F32 = mybir.dt.float32
BF16 = mybir.dt.bfloat16
I8 = mybir.dt.int8
AF = mybir.ActivationFunctionType
ALU = mybir.AluOpType
AX = mybir.AxisListType.X

NCORES = 8
T = 1024
NT = 8
D = 2048
KC = 16
H = 16
F = 8192
SEQ = 8192
EPS_ATT = 1e-6
EPS_LN = 1e-5
EPS_Q = 1e-5

ENGS = ("tensor", "vector", "scalar", "sync", "gpsimd")
TICK = {"tensor": "pe", "vector": "dve", "scalar": "act", "gpsimd": "pool"}


class _Comp:
    __slots__ = ("sem", "inc", "needed", "val", "idx")

    def __init__(self, sem, inc, idx, needed=False):
        self.sem, self.inc, self.needed, self.val, self.idx = sem, inc, needed, None, idx


class _DmaComp:
    __slots__ = ("q", "total", "tick")

    def __init__(self, q, total):
        self.q, self.total, self.tick = q, total, None


QENG = {"qs": "sync", "qa": "scalar", "qg": "gpsimd"}
QTICK = {"qs": "fns", "qa": "fna", "qg": "fng"}


def _qsem(dma):
    q, par = dma
    return q + str(par % (4 if q == "qs" else 2))


class Bld:

    def __init__(self):
        self.plan = {e: [] for e in ENGS}
        self.writer = {}
        self.readers = {}
        self.water = {e: {} for e in ENGS}
        self.nidx = 0
        self.qissued = {}
        self.qfence = {}

    def _fence(self, dc):
        covered, tick = self.qfence.get(dc.q, (0, None))
        if tick is not None and covered >= dc.total:
            return tick
        qeng = QENG[dc.q[:2]]
        self.nidx += 1
        w = _Comp(dc.q, 16, self.nidx, needed=True)
        w.val = 16 * self.qissued[dc.q]
        self.plan[qeng].append(("W", w))
        self.nidx += 1
        tick = _Comp(QTICK[dc.q[:2]], 1, self.nidx, needed=True)
        self.plan[qeng].append(("o", None, tick))
        self.qfence[dc.q] = (self.qissued[dc.q], tick)
        return tick

    def _dep(self, deps, comp):
        if comp is None:
            return
        if isinstance(comp, _DmaComp):
            comp = self._fence(comp)
        if comp.sem not in deps or comp.idx > deps[comp.sem].idx:
            deps[comp.sem] = comp

    def _waits(self, eng, deps):
        wm = self.water[eng]
        for sem in sorted(deps):
            c = deps[sem]
            if wm.get(sem) is None or c.idx > wm[sem].idx:
                c.needed = True
                self.plan[eng].append(("w", c))
                wm[sem] = c

    def op(self, eng, fn, reads=(), writes=(), wcheck=(), wpub=(), dma=None, comp_sem=None):
        deps = {}
        for t in reads:
            self._dep(deps, self.writer.get(t))
        for t in tuple(writes) + tuple(wcheck):
            self._dep(deps, self.writer.get(t))
            for r in self.readers.get(t, ()):
                self._dep(deps, r)
        self._waits(eng, deps)
        self.nidx += 1
        if comp_sem is not None:
            comp = _Comp(comp_sem[0], comp_sem[1], self.nidx, needed=True)
        elif dma is not None:
            sem = _qsem(dma) if isinstance(dma, tuple) else dma + "0"
            self.qissued[sem] = self.qissued.get(sem, 0) + 1
            comp = _Comp(sem, 16, self.nidx, needed=True)
            self.plan[eng].append(("o", fn, comp))
            dcomp = _DmaComp(sem, self.qissued[sem])
            for t in reads:
                self.readers.setdefault(t, []).append(dcomp)
            for t in tuple(writes) + tuple(wpub):
                self.readers[t] = []
                self.writer[t] = dcomp
            return dcomp
        else:
            comp = _Comp(TICK[eng], 1, self.nidx)
        self.plan[eng].append(("o", fn, comp))
        for t in reads:
            self.readers.setdefault(t, []).append(comp)
        for t in tuple(writes) + tuple(wpub):
            self.readers[t] = []
            self.writer[t] = comp
        return comp

    def barrier(self, eng, tokens):
        deps = {}
        for t in tokens:
            self._dep(deps, self.writer.get(t))
            for r in self.readers.get(t, ()):
                self._dep(deps, r)
        self._waits(eng, deps)

    def finalize(self):
        cnt = {}
        for e in ENGS:
            for item in self.plan[e]:
                if item[0] == "o" and item[2].needed and item[2].val is None:
                    c = item[2]
                    cnt[c.sem] = cnt.get(c.sem, 0) + c.inc
                    c.val = cnt[c.sem]

    def body(self, eng_name, sems):
        items = self.plan[eng_name]

        def run(e):
            for item in items:
                if item[0] in ("w", "W"):
                    e.wait_ge(sems[item[1].sem], item[1].val)
                elif item[1] is None:
                    e.nop().then_inc(sems[item[2].sem], item[2].inc)
                else:
                    inst = item[1](e)
                    if item[2].needed:
                        inst.then_inc(sems[item[2].sem], item[2].inc)
        return run


def build_nc(dbg=False):
    nc = bass.Bass("TRN2", target_bir_lowering=False)

    x_in = nc.dram_tensor("x_fm", [D, T], F32, kind="ExternalInput").ap()
    wqT = nc.dram_tensor("wqT", [D, D], F32, kind="ExternalInput").ap()
    wkT = nc.dram_tensor("wkT", [D, D], F32, kind="ExternalInput").ap()
    wvT = nc.dram_tensor("wvT", [D, D], F32, kind="ExternalInput").ap()
    woT = nc.dram_tensor("woT", [D, D], F32, kind="ExternalInput").ap()
    wguT_sh = nc.dram_tensor("wguT_sh", [256, 2 * F], F32, kind="ExternalInput").ap()
    wdT_sh = nc.dram_tensor("wdT_sh", [1024, D], F32, kind="ExternalInput").ap()
    wmod_sh = nc.dram_tensor("wmod_sh", [D, 1536], F32, kind="ExternalInput").ap()
    cT_in = nc.dram_tensor("cT", [128, 16], F32, kind="ExternalInput").ap()
    bq_in = nc.dram_tensor("bq_fm", [128, 16], F32, kind="ExternalInput").ap()
    bo_in = nc.dram_tensor("bo_fm", [128, 16], F32, kind="ExternalInput").ap()
    bk_in = nc.dram_tensor("bk_row", [1, D], F32, kind="ExternalInput").ap()
    bv_in = nc.dram_tensor("bv_row", [1, D], F32, kind="ExternalInput").ap()
    idx_in = nc.dram_tensor("idx", [1, T], F32, kind="ExternalInput").ap()
    y_out = nc.dram_tensor("y", [T, D], F32, kind="ExternalOutput").ap()

    mod_in = nc.dram_tensor("mod_in", [128, 12], F32).ap()
    mod_out = nc.dram_tensor("mod_out", [128 * NCORES, 12], F32).ap()
    ws_in = nc.dram_tensor("ws_in", [1, 128], F32).ap()
    ws_out = nc.dram_tensor("ws_out", [1, 128], F32).ap()
    kv_in = nc.dram_tensor("kv_in", [2064, 128], F32).ap()
    kv_out = nc.dram_tensor("kv_out", [2064, 128], F32).ap()
    tgu_in = nc.dram_tensor("tgu_in", [256, 2 * F], I8).ap()
    tguA = nc.dram_tensor("tguA", [1024, 2 * F], I8, addr_space="Shared").ap()
    tguB = nc.dram_tensor("tguB", [1024, 2 * F], I8, addr_space="Shared").ap()
    twd_in = nc.dram_tensor("twd_in", [1024, D], I8).ap()
    twdA = nc.dram_tensor("twdA", [4096, D], I8, addr_space="Shared").ap()
    twdB = nc.dram_tensor("twdB", [4096, D], I8, addr_space="Shared").ap()
    t_buf = nc.dram_tensor("t_buf", [F, T], F32).ap()
    if dbg:
        dbg_kv = nc.dram_tensor("dbg_kv", [2065, 128], F32, kind="ExternalOutput").ap()
        dbg_mod = nc.dram_tensor("dbg_mod", [128, 96], F32, kind="ExternalOutput").ap()
        dbg_x1 = nc.dram_tensor("dbg_x1", [D, T], F32, kind="ExternalOutput").ap()
        dbg_xq = nc.dram_tensor("dbg_xq", [D, T], BF16, kind="ExternalOutput").ap()
        dbg_t = nc.dram_tensor("dbg_t", [F, T], F32, kind="ExternalOutput").ap()
        dbg_ha = nc.dram_tensor("dbg_ha", [D, T], BF16, kind="ExternalOutput").ap()
        dbg_q = nc.dram_tensor("dbg_q", [D, T], BF16, kind="ExternalOutput").ap()
        dbg_tgin = nc.dram_tensor("dbg_tgin", [256, 2 * F], I8, kind="ExternalOutput").ap()

    b = Bld()

    with ExitStack() as ctx:
        E = ctx.enter_context
        A_x = E(nc.sbuf_tensor("A_x", [128, KC * T], F32))
        ARB = E(nc.sbuf_tensor("ARB", [128, 24576], F32))
        SM = E(nc.sbuf_tensor("SM", [128, 6144], F32))
        ROWS = E(nc.sbuf_tensor("ROWS", [1, 4096], F32))
        dec_fm = E(nc.sbuf_tensor("dec_fm", [128, T], F32))
        idn = E(nc.sbuf_tensor("idn", [128, 128], F32))
        ones_bf = E(nc.sbuf_tensor("ones_bf", [128, 1], BF16))
        ones_col = E(nc.sbuf_tensor("ones_col", [128, 1], F32))
        onesr_f = E(nc.sbuf_tensor("onesr_f", [1, 128], F32))
        onesr_bf = E(nc.sbuf_tensor("onesr_bf", [1, 128], BF16))
        cT = E(nc.sbuf_tensor("cTs", [128, 16], F32))
        bq_fm = E(nc.sbuf_tensor("bq_s", [128, 16], F32))
        bo_fm = E(nc.sbuf_tensor("bo_s", [128, 16], F32))
        idx_tok = E(nc.sbuf_tensor("idx_tok", [128, NT], F32))
        dec_tok = E(nc.sbuf_tensor("dec_tok", [128, NT], F32))
        modf = E(nc.sbuf_tensor("modf", [128, 96], F32))
        onca = E(nc.sbuf_tensor("onca", [128, 16], F32))
        oncb = E(nc.sbuf_tensor("oncb", [128, 16], F32))
        ga_t = E(nc.sbuf_tensor("ga_t", [128, 16], F32))
        gb_t = E(nc.sbuf_tensor("gb_t", [128, 16], F32))
        gsc = E(nc.sbuf_tensor("gsc", [128, 16], F32))
        ksum_col = E(nc.sbuf_tensor("ksum_col", [128, 16], BF16))
        ks16s = E(nc.sbuf_tensor("ks16s", [16, 128], F32))
        zr_tok = E(nc.sbuf_tensor("zr_tok", [128, 128], F32))
        rm1_tok = E(nc.sbuf_tensor("rm1_tok", [128, NT], F32))
        s1_tok = E(nc.sbuf_tensor("s1_tok", [128, NT], F32))
        mx_tok = E(nc.sbuf_tensor("mx_tok", [128, NT], F32))
        fs_tok = E(nc.sbuf_tensor("fs_tok", [128, NT], F32))
        u_tok = E(nc.sbuf_tensor("u_tok", [128, NT], F32))
        sgu_pp = E(nc.sbuf_tensor("sgu_pp", [128, 1], F32))
        swd_pp = E(nc.sbuf_tensor("swd_pp", [128, 1], F32))
        isgu_pp = E(nc.sbuf_tensor("isgu_pp", [128, 1], F32))
        acc_gu = E(nc.sbuf_tensor("acc_gu", [128, 1], F32))
        acc_wd = E(nc.sbuf_tensor("acc_wd", [128, 1], F32))
        red_t = E(nc.sbuf_tensor("red_t", [128, 1], F32))
        epsln = E(nc.sbuf_tensor("epsln", [128, 1], F32))

        P = [E(nc.psum_tensor("P%d" % i, [128, 512], F32)) for i in range(8)]

        semnames = ["pe", "dve", "act", "pool", "syn", "fns", "fna", "fng", "qs0", "qs1", "qs2", "qs3", "qa0", "qa1", "qg0", "qg1", "cc1", "cc2", "cc3", "cc4", "cc5", "cc6", "cc7"]
        sems = {n: E(nc.semaphore(n)) for n in semnames}
        block = E(nc.Block())

        arb_f = ARB.ap()
        arb_bf = ARB.ap().bitcast(BF16)
        arb_i8 = ARB.ap().bitcast(I8)
        ax_f = A_x.ap()
        ax_bf = A_x.ap().bitcast(BF16)
        sm_f = SM.ap()
        sm_bf = SM.ap().bitcast(BF16)
        sm_i8 = SM.ap().bitcast(I8)
        rows_f = ROWS.ap()
        rows_bf = ROWS.ap().bitcast(BF16)
        sc2 = None

        def ha_c(c):
            return arb_bf[:, c * T:(c + 1) * T]

        def q_h(h):
            return arb_bf[:, 16384 + h * T:16384 + (h + 1) * T]

        def arbB_f(off, n):
            return arb_f[:, 8192 + off:8192 + off + n]

        def arbB_bf(off, n):
            return arb_bf[:, 16384 + off:16384 + off + n]

        def arbC_f(off, n):
            return arb_f[:, 16384 + off:16384 + off + n]

        def arbC_bf(off, n):
            return arb_bf[:, 32768 + off:32768 + off + n]

        def arbC_i8(off, n):
            return arb_i8[:, 65536 + off:65536 + off + n]

        def tA(*slots):
            return [("A", s) for s in slots]

        def tB(*slots):
            return [("B", s) for s in slots]

        def tC(*slots):
            return [("C", s) for s in slots]

        def tX(*slots):
            return [("X", s) for s in slots]

        def tSA(*slots):
            return [("SA", s) for s in slots]

        def tSB(*slots):
            return [("SB", s) for s in slots]

        def tSC(*slots):
            return [("SC", s) for s in slots]

        def tR(*slots):
            return [("R", s) for s in slots]

        def tP(*idx):
            return [("P", i) for i in idx]

        OP = b.op

        for c in range(KC):
            OP("gpsimd", (lambda c: lambda e: e.dma_start(out=ax_f[:, c * T:(c + 1) * T],
                                                          in_=x_in[c * 128:(c + 1) * 128, :]))(c),
               writes=tX(c), dma=("qg", c))
        idx_row = rows_f[0:1, 3072:4096]
        OP("sync", lambda e: e.dma_start(out=idx_row, in_=idx_in[:, :]), writes=tR(6, 7), dma=("qs", 0))
        OP("sync", lambda e: e.dma_start(out=cT[:, :], in_=cT_in[:, :]), writes=[("cT",)], dma=("qs", 1))
        OP("sync", lambda e: e.dma_start(out=bq_fm[:, :], in_=bq_in[:, :]), writes=[("bq",)], dma=("qs", 0))
        OP("sync", lambda e: e.dma_start(out=bo_fm[:, :], in_=bo_in[:, :]), writes=[("bo",)], dma=("qs", 1))
        bk_row = rows_bf[0:1, 0:2048]
        bv_row = rows_bf[0:1, 2048:4096]

        OP("gpsimd", lambda e: e.memset(idn[:, :], 0.0), writes=[("idn",)])
        OP("gpsimd", lambda e: e.affine_select(out=idn[:, :], in_=idn[:, :], compare_op=ALU.not_equal,
                                               fill=1.0, base=0, pattern=[[-1, 128]], channel_multiplier=1),
           reads=[("idn",)], writes=[("idn",)])
        OP("vector", lambda e: e.memset(ones_bf[:, :], 1.0), writes=[("ones_bf",)])
        OP("vector", lambda e: e.memset(epsln[:, :], EPS_LN), writes=[("epsln",)])
        OP("vector", lambda e: e.memset(zr_tok[:, :], 0.0), writes=[("zr",)])
        OP("vector", lambda e: e.memset(ones_col[:, :], 1.0), writes=[("ones_col",)])
        OP("vector", lambda e: e.memset(onesr_f[:, :], 1.0), writes=[("onesr_f",)])
        OP("vector", lambda e: e.memset(onesr_bf[:, :], 1.0), writes=[("onesr_bf",)])

        OP("scalar", lambda e: e.activation(cT[:, :], cT[:, :], AF.Silu), reads=[("cT",)], writes=[("cT",)])
        def wmod_dma(k):
            par = k % 2
            buf = arbC_f(par * 2048, 1536)
            OP("sync", (lambda k, buf: lambda e: e.dma_start(out=buf, in_=wmod_sh[k * 128:(k + 1) * 128, :]))(k, buf),
               writes=tC(2 * par, 2 * par + 1), dma=("qs", k))
        wmod_dma(0)
        for k in range(KC):
            par = k % 2
            if k + 1 < KC:
                wmod_dma(k + 1)
            for j in range(3):
                OP("tensor", (lambda k, j, par: lambda e: e.matmul(
                    P[4 + j][0:1, :], cT[:, k:k + 1], arbC_f(par * 2048 + j * 512, 512),
                    start=(k == 0), stop=(k == KC - 1)))(k, j, par),
                   reads=[("cT",)] + tC(2 * par, 2 * par + 1),
                   wcheck=tP(4 + j) if k == 0 else (), wpub=tP(4 + j) if k == KC - 1 else ())
        mrowA = rows_f[0:1, 2048:3072]
        for j in range(2):
            OP("vector", (lambda j: lambda e: e.tensor_copy(mrowA[0:1, j * 512:(j + 1) * 512], P[4 + j][0:1, :]))(j),
               reads=tP(4 + j), writes=tR(4 + j))
        for j in range(8):
            OP("tensor", (lambda j: lambda e: e.transpose(P[7][:, j:j + 1], mrowA[0:1, j * 128:(j + 1) * 128],
                                                          idn[0:1, 0:1]))(j),
               reads=tR(4 + j // 4) + [("idn",)],
               wcheck=tP(7) if j == 0 else (), wpub=tP(7) if j == 7 else ())
        mrowB = rows_f[0:1, 2048:2560]
        OP("vector", lambda e: e.tensor_copy(mrowB, P[6][0:1, :]), reads=tP(6), writes=tR(4))
        for j in range(4):
            OP("tensor", (lambda j: lambda e: e.transpose(P[7][:, 8 + j:9 + j], mrowB[0:1, j * 128:(j + 1) * 128],
                                                          idn[0:1, 0:1]))(j),
               reads=tR(4) + [("idn",)],
               wcheck=(), wpub=tP(7) if j == 3 else ())
        OP("vector", lambda e: e.tensor_copy(modf[:, 0:12], P[7][:, 0:12]), reads=tP(7), writes=[("modloc",)])
        OP("scalar", lambda e: e.dma_start(out=mod_in[:, :], in_=modf[:, 0:12]),
           reads=[("modloc",)], writes=[("mod_in",)], dma=("qa", 0))
        OP("gpsimd", lambda e: e.collective_compute(
            "AllGather", ALU.bypass, replica_groups=[list(range(NCORES))],
            ins=[mod_in.opt()], outs=[mod_out.opt()]),
           reads=[("mod_in",)], writes=[("mod_out",)], comp_sem=("cc1", 1))
        OP("gpsimd", lambda e: e.dma_start(out=modf[:, :].rearrange("p (r j) -> p r j", r=8),
                                           in_=mod_out.rearrange("(r p) j -> p r j", p=128)),
           reads=[("mod_out",), ("modloc",)], writes=[("modf",)], dma=("qg", 0))

        sa_v = modf[:, 0:16]
        ca_v = modf[:, 16:32]
        ga_v = modf[:, 32:48]
        sb_v = modf[:, 48:64]
        cb_v = modf[:, 64:80]
        gb_v = modf[:, 80:96]
        OP("vector", lambda e: e.tensor_scalar(onca[:, :], ca_v, 1.0, None, ALU.add),
           reads=[("modf",)], writes=[("onca",)])
        OP("vector", lambda e: e.tensor_scalar(oncb[:, :], cb_v, 1.0, None, ALU.add),
           reads=[("modf",)], writes=[("oncb",)])
        for gv, gout, gtok in ((ga_v, ga_t, "ga"), (gb_v, gb_t, "gb")):
            OP("scalar", (lambda gv: lambda e: e.activation(gsc[:, :], gv, AF.Sign))(gv),
               reads=[("modf",)], writes=[("gsc",)])
            OP("scalar", (lambda gv, gout: lambda e: e.activation(gout[:, :], gv, AF.Abs))(gv, gout),
               reads=[("modf",)], writes=[(gtok,)])
            OP("vector", (lambda gout: lambda e: e.tensor_scalar_max(gout[:, :], gout[:, :], 0.25))(gout),
               reads=[(gtok,)], writes=[(gtok,)])
            OP("vector", (lambda gout: lambda e: e.tensor_mul(gout[:, :], gout[:, :], gsc[:, :]))(gout),
               reads=[(gtok,), ("gsc",)], writes=[(gtok,)])

        for j in range(NT):
            OP("tensor", (lambda j: lambda e: e.transpose(P[6][:, j:j + 1], idx_row[0:1, j * 128:(j + 1) * 128],
                                                          idn[0:1, 0:1]))(j),
               reads=tR(6, 7) + [("idn",)],
               wcheck=tP(6) if j == 0 else (), wpub=tP(6) if j == NT - 1 else ())
        OP("vector", lambda e: e.tensor_copy(idx_tok[:, :], P[6][:, 0:NT]), reads=tP(6), writes=[("idx_tok",)])
        OP("scalar", lambda e: e.activation(dec_tok[:, :], idx_tok[:, :], AF.Exp, scale=-1.0 / SEQ),
           reads=[("idx_tok",)], writes=[("dec_tok",)])
        for n in range(2):
            OP("tensor", (lambda n: lambda e: e.matmul(P[6][:, :], onesr_f[:, :],
                                                       idx_row[0:1, n * 512:(n + 1) * 512], start=True, stop=True))(n),
               reads=tR(6, 7) + [("onesr_f",)], writes=tP(6))
            OP("scalar", (lambda n: lambda e: e.activation(dec_fm[:, n * 512:(n + 1) * 512], P[6][:, :],
                                                           AF.Exp, scale=-1.0 / SEQ))(n),
               reads=tP(6), writes=[("dec_fm", n)])

        mean_b = sm_f[:, 0:T]
        rstd_b = sm_f[:, T:2 * T]

        def ln_stats_chunk(c, pb):
            par = c % 2
            xb = sm_bf[:, 8192 + par * T:8192 + (par + 1) * T]
            xq_ = sm_bf[:, 8192 + 2 * T + par * T:8192 + 2 * T + (par + 1) * T]
            OP("vector", (lambda c, xb: lambda e: e.tensor_copy(xb, ax_f[:, c * T:(c + 1) * T]))(c, xb),
               reads=tX(c), writes=tSC(par))
            OP("scalar", (lambda c, xq_: lambda e: e.activation(xq_, ax_f[:, c * T:(c + 1) * T], AF.Square))(c, xq_),
               reads=tX(c), writes=tSC(2 + par))
            for n in range(2):
                OP("tensor", (lambda c, n, xb, pb: lambda e: e.matmul(
                    P[pb + n][0:1, :], ones_bf[:, :], xb[:, n * 512:(n + 1) * 512],
                    start=(c == 0), stop=(c == KC - 1)))(c, n, xb, pb),
                   reads=tSC(par) + [("ones_bf",)],
                   wcheck=tP(pb + n) if c == 0 else (), wpub=tP(pb + n) if c == KC - 1 else ())
                OP("tensor", (lambda c, n, xq_, pb: lambda e: e.matmul(
                    P[pb + 2 + n][0:1, :], ones_bf[:, :], xq_[:, n * 512:(n + 1) * 512],
                    start=(c == 0), stop=(c == KC - 1)))(c, n, xq_, pb),
                   reads=tSC(2 + par) + [("ones_bf",)],
                   wcheck=tP(pb + 2 + n) if c == 0 else (), wpub=tP(pb + 2 + n) if c == KC - 1 else ())

        def ln_stats_rows(pb):
            mean_r = rows_f[0:1, 0:1024]
            var_r = rows_f[0:1, 1024:2048]
            m2_r = rows_f[0:1, 2048:2560]
            for n in range(2):
                OP("vector", (lambda n, mean_r, pb: lambda e: e.tensor_scalar(
                    mean_r[0:1, n * 512:(n + 1) * 512], P[pb + n][0:1, :], 1.0 / D, None, ALU.mult))(n, mean_r, pb),
                   reads=tP(pb + n), writes=tR(n))
                OP("vector", (lambda n, var_r, pb: lambda e: e.tensor_scalar(
                    var_r[0:1, n * 512:(n + 1) * 512], P[pb + 2 + n][0:1, :], 1.0 / D, None, ALU.mult))(n, var_r, pb),
                   reads=tP(pb + 2 + n), writes=tR(2 + n))
                OP("vector", (lambda n, mean_r, m2_r: lambda e: e.tensor_mul(
                    m2_r, mean_r[0:1, n * 512:(n + 1) * 512], mean_r[0:1, n * 512:(n + 1) * 512]))(n, mean_r, m2_r),
                   reads=tR(n), writes=tR(4))
                OP("vector", (lambda n, var_r, m2_r: lambda e: e.tensor_sub(
                    var_r[0:1, n * 512:(n + 1) * 512], var_r[0:1, n * 512:(n + 1) * 512], m2_r))(n, var_r, m2_r),
                   reads=tR(2 + n, 4), writes=tR(2 + n))
            OP("scalar", lambda e: e.activation(var_r, var_r, AF.Ln, bias=epsln[0:1, 0:1]),
               reads=tR(2, 3) + [("epsln",)], writes=tR(2, 3))
            OP("scalar", lambda e: e.activation(var_r, var_r, AF.Exp, scale=-0.5), reads=tR(2, 3), writes=tR(2, 3))
            for n in range(2):
                OP("tensor", (lambda n, mean_r, pb: lambda e: e.matmul(
                    P[pb][:, :], onesr_f[:, :], mean_r[0:1, n * 512:(n + 1) * 512], start=True, stop=True))(n, mean_r, pb),
                   reads=tR(n) + [("onesr_f",)], writes=tP(pb))
                OP("vector", (lambda n, pb: lambda e: e.tensor_copy(mean_b[:, n * 512:(n + 1) * 512], P[pb][:, :]))(n, pb),
                   reads=tP(pb), writes=tSA(0))
                OP("tensor", (lambda n, var_r, pb: lambda e: e.matmul(
                    P[pb + 1][:, :], onesr_f[:, :], var_r[0:1, n * 512:(n + 1) * 512], start=True, stop=True))(n, var_r, pb),
                   reads=tR(2 + n) + [("onesr_f",)], writes=tP(pb + 1))
                OP("vector", (lambda n, pb: lambda e: e.tensor_copy(rstd_b[:, n * 512:(n + 1) * 512], P[pb + 1][:, :]))(n, pb),
                   reads=tP(pb + 1), writes=tSA(1))

        def ln_apply_chunk(c, scale_v, shift_v, out_ap, out_tokens, out_dtype_pass=None):
            par = c % 2
            t1 = sm_f[:, 2048 + par * T:2048 + (par + 1) * T]
            OP("vector", (lambda c, t1: lambda e: e.tensor_sub(t1, ax_f[:, c * T:(c + 1) * T], mean_b))(c, t1),
               reads=tX(c) + tSA(0), writes=tSB(2 * par, 2 * par + 1))
            OP("vector", (lambda t1: lambda e: e.tensor_mul(t1, t1, rstd_b))(t1),
               reads=tSB(2 * par, 2 * par + 1) + tSA(1), writes=tSB(2 * par, 2 * par + 1))
            OP("scalar", (lambda c, t1, out_ap, scale_v, shift_v: lambda e: e.activation(
                out_ap, t1, AF.Identity, scale=scale_v[:, c:c + 1], bias=shift_v[:, c:c + 1]))(c, t1, out_ap, scale_v, shift_v),
               reads=tSB(2 * par, 2 * par + 1) + [("onca",), ("oncb",), ("modf",)], writes=out_tokens)

        def ternarize(shard_ap, rows, out_dram, ipp, base_tok):
            yield
            cols = shard_ap.shape[1]
            nr, ncb = rows // 128, cols // 1024
            def t_dma(i):
                rb, cb = divmod(i, ncb)
                par = i % 2
                buf = arbC_f((2 + par) * 1024, 1024)
                OP("sync", (lambda rb, cb, buf: lambda e: e.dma_start(
                    out=buf, in_=shard_ap[rb * 128:(rb + 1) * 128, cb * 1024:(cb + 1) * 1024]))(rb, cb, buf),
                   writes=tC(2 + par), dma=("qs", 2 + i % 2))
            t_dma(0)
            for i in range(nr * ncb):
                rb, cb = divmod(i, ncb)
                par = i % 2
                buf = arbC_f((2 + par) * 1024, 1024)
                i8b = arbC_i8((4 + par) * 4096, 1024)
                ob = arbC_i8((6 + par) * 4096, 1024)
                if i + 1 < nr * ncb:
                    t_dma(i + 1)
                OP("vector", (lambda buf, i8b, ipp: lambda e: e.tensor_scalar(
                    i8b, buf, ipp[:, 0:1], None, ALU.mult))(buf, i8b, ipp),
                   reads=tC(2 + par) + [("isgu",), ("iswd",)], writes=tC(4 + par))
                OP("scalar", (lambda i8b, ob: lambda e: e.activation(
                    ob, i8b, AF.Sign))(i8b, ob),
                   reads=tC(4 + par), writes=tC(6 + par))
                OP("scalar", (lambda rb, cb, ob, out_dram: lambda e: e.dma_start(
                    out=out_dram[rb * 128:(rb + 1) * 128, cb * 1024:(cb + 1) * 1024], in_=ob))(rb, cb, ob, out_dram),
                   reads=tC(6 + par), writes=[(base_tok, rb, cb)], dma=("qa", i))
                yield

        for c in range(KC):
            ln_stats_chunk(c, 0)
        ln_stats_rows(0)
        for c in range(KC):
            ln_apply_chunk(c, onca, sa_v, ha_c(c), tA(c))

        OP("gpsimd", lambda e: e.dma_start(out=bk_row, in_=bk_in[:, :]), writes=tR(0, 1), dma=("qg", 0))
        OP("gpsimd", lambda e: e.dma_start(out=bv_row, in_=bv_in[:, :]), writes=tR(2, 3), dma=("qg", 1))

        def scan_abs(shard_ap, rows, acc, accname):
            cols = shard_ap.shape[1]
            nr, ncb = rows // 128, cols // 2048

            def s_dma(i):
                rb, cb = divmod(i, ncb)
                par = i % 2
                buf = arbB_f(par * 2048, 2048)
                OP("scalar", (lambda rb, cb, buf: lambda e: e.dma_start(
                    out=buf, in_=shard_ap[rb * 128:(rb + 1) * 128, cb * 2048:(cb + 1) * 2048]))(rb, cb, buf),
                   reads=tX(KC - 1) if i == 0 else [],
                   writes=tB(4 * par, 4 * par + 1, 4 * par + 2, 4 * par + 3), dma=("qa", i))
            s_dma(0)
            for i in range(nr * ncb):
                par = i % 2
                buf = arbB_f(par * 2048, 2048)
                if i + 1 < nr * ncb:
                    s_dma(i + 1)
                OP("vector", (lambda buf: lambda e: e.tensor_reduce(
                    red_t[:, :], buf, AX, ALU.add, apply_absolute_value=True))(buf),
                   reads=tB(4 * par, 4 * par + 1, 4 * par + 2, 4 * par + 3), writes=[("red",)])
                if i == 0:
                    OP("vector", (lambda acc: lambda e: e.tensor_copy(acc[:, :], red_t[:, :]))(acc),
                       reads=[("red",)], writes=[(accname,)])
                else:
                    OP("vector", (lambda acc: lambda e: e.tensor_add(acc[:, :], acc[:, :], red_t[:, :]))(acc),
                       reads=[("red",), (accname,)], writes=[(accname,)])

        scan_abs(wguT_sh, 256, acc_gu, "agu")
        scan_abs(wdT_sh, 1024, acc_wd, "awd")
        iswd_pp = acc_gu


        k_sl = sm_bf[:, 8192:10240]
        v_sl = sm_bf[:, 10240:12288]
        kvloc = sm_f[:, 0:2048]
        ksum_row = rows_f[0:1, 2048:4096]
        wk3 = wkT.rearrange("(kc p) d -> p kc d", p=128)
        wv3 = wvT.rearrange("(kc p) d -> p kc d", p=128)
        def kv_dma(ds):
            par = ds % 2
            kblk = arbC_bf(par * 4096, 4096)
            vblk = arbC_bf(8192 + par * 4096, 4096)
            OP("gpsimd", (lambda ds, kblk: lambda e: e.dma_start(
                out=kblk.rearrange("p (kc d) -> p kc d", kc=KC),
                in_=wk3[:, :, ds * 256:(ds + 1) * 256]))(ds, kblk),
               writes=tC(2 * par, 2 * par + 1), dma=("qg", ds))
            OP("gpsimd", (lambda ds, vblk: lambda e: e.dma_start(
                out=vblk.rearrange("p (kc d) -> p kc d", kc=KC),
                in_=wv3[:, :, ds * 256:(ds + 1) * 256]))(ds, vblk),
               writes=tC(4 + 2 * par, 5 + 2 * par), dma=("qg", ds))
        kv_dma(0)
        for ds in range(8):
            par = ds % 2
            kblk = arbC_bf(par * 4096, 4096)
            vblk = arbC_bf(8192 + par * 4096, 4096)
            if ds + 1 < 8:
                kv_dma(ds + 1)
            for nt in range(NT):
                pk = P[nt % 2]
                pv = P[2 + nt % 2]
                for k in range(KC):
                    OP("tensor", (lambda k, nt, kblk, pk: lambda e: e.matmul(
                        pk[:, 0:256], ha_c(k)[:, nt * 128:(nt + 1) * 128],
                        kblk[:, k * 256:(k + 1) * 256], start=(k == 0), stop=False))(k, nt, kblk, pk),
                       reads=tA(k) + tC(2 * par, 2 * par + 1),
                       wcheck=tP(nt % 2) if k == 0 else ())
                for k in range(KC):
                    OP("tensor", (lambda k, nt, vblk, pv: lambda e: e.matmul(
                        pv[:, 0:256], ha_c(k)[:, nt * 128:(nt + 1) * 128],
                        vblk[:, k * 256:(k + 1) * 256], start=(k == 0), stop=False))(k, nt, vblk, pv),
                       reads=tA(k) + tC(4 + 2 * par, 5 + 2 * par),
                       wcheck=tP(2 + nt % 2) if k == 0 else ())
                OP("tensor", (lambda ds, pk: lambda e: e.matmul(
                    pk[:, 0:256], onesr_bf[:, :], bk_row[0:1, ds * 256:(ds + 1) * 256],
                    start=False, stop=True))(ds, pk),
                   reads=tR(0, 1) + [("onesr_bf",)], wpub=tP(nt % 2))
                OP("tensor", (lambda ds, pv: lambda e: e.matmul(
                    pv[:, 0:256], onesr_bf[:, :], bv_row[0:1, ds * 256:(ds + 1) * 256],
                    start=False, stop=True))(ds, pv),
                   reads=tR(2, 3) + [("onesr_bf",)], wpub=tP(2 + nt % 2))
                OP("scalar", (lambda nt, pk: lambda e: e.activation(
                    k_sl[:, nt * 256:(nt + 1) * 256], pk[:, 0:256], AF.Relu,
                    scale=dec_tok[:, nt:nt + 1]))(nt, pk),
                   reads=tP(nt % 2) + [("dec_tok",)], writes=[("ksl", nt)] + (tSC(0, 1) if nt == 0 and ds == 0 else []))
                OP("scalar", (lambda nt, pv: lambda e: e.activation(
                    v_sl[:, nt * 256:(nt + 1) * 256], pv[:, 0:256], AF.Copy))(nt, pv),
                   reads=tP(2 + nt % 2), writes=[("vsl", nt)] + (tSC(2, 3) if nt == 0 and ds == 0 else []))
            for hh in range(2):
                h = 2 * ds + hh
                for nt in range(NT):
                    OP("tensor", (lambda hh, nt: lambda e: e.matmul(
                        P[4 + hh][:, 0:128],
                        k_sl[:, nt * 256 + hh * 128:nt * 256 + hh * 128 + 128],
                        v_sl[:, nt * 256 + hh * 128:nt * 256 + hh * 128 + 128],
                        start=(nt == 0), stop=(nt == NT - 1)))(hh, nt),
                       reads=[("ksl", nt), ("vsl", nt)],
                       wcheck=tP(4 + hh) if nt == 0 else (), wpub=tP(4 + hh) if nt == NT - 1 else ())
                OP("vector", (lambda h, hh: lambda e: e.tensor_copy(
                    kvloc[:, h * 128:(h + 1) * 128], P[4 + hh][:, 0:128]))(h, hh),
                   reads=tP(4 + hh), writes=[("kvloc", h), ("SA", h // 8)])

            for nt in range(NT):
                OP("tensor", (lambda ds, nt: lambda e: e.matmul(
                    P[6][0:1, 0:256], ones_bf[:, :], k_sl[:, nt * 256:(nt + 1) * 256],
                    start=(nt == 0), stop=(nt == NT - 1)))(ds, nt),
                   reads=[("ksl", nt), ("ones_bf",)],
                   wcheck=tP(6) if nt == 0 else (), wpub=tP(6) if nt == NT - 1 else ())
            OP("vector", (lambda ds: lambda e: e.tensor_copy(
                ksum_row[0:1, ds * 256:(ds + 1) * 256], P[6][0:1, 0:256]))(ds),
               reads=tP(6), writes=[("ksum", ds)] + (tR(4, 5, 6, 7) if ds == 0 else []))

            if ds == 3:
                scr = zr_tok[0:1, 0:128]
                for i, (acc, accname) in enumerate(((acc_gu, "agu"), (acc_wd, "awd"))):
                    OP("tensor", (lambda i, acc: lambda e: e.matmul(
                        P[7][0:1, i:i + 1], acc[:, :], ones_col[:, :], start=True, stop=True))(i, acc),
                       reads=[(accname,), ("ones_col",)],
                       wcheck=tP(7) if i == 0 else (), wpub=tP(7) if i == 1 else ())
                OP("vector", lambda e: e.tensor_copy(scr[0:1, 0:2], P[7][0:1, 0:2]),
                   reads=tP(7) + [("zr",)], writes=[("zr",)])
                OP("scalar", lambda e: e.dma_start(out=ws_in[:, :], in_=scr),
                   reads=[("zr",)], writes=[("ws_in",)], dma=("qa", 1))
                OP("gpsimd", lambda e: e.collective_compute(
                    "AllReduce", ALU.add, replica_groups=[list(range(NCORES))],
                    ins=[ws_in.opt()], outs=[ws_out.opt()]),
                   reads=[("ws_in",)], writes=[("ws_out",)], comp_sem=("cc2", 1))
            if ds == 5:
                scr = zr_tok[0:1, 0:128]
                OP("gpsimd", lambda e: e.dma_start(out=scr, in_=ws_out[:, :]),
                   reads=[("ws_out",), ("zr",)], writes=[("zr",)], dma=("qg", 0))
            if ds == 6:
                scr = zr_tok[0:1, 0:128]
                OP("vector", lambda e: e.tensor_scalar(scr[0:1, 0:1], scr[0:1, 0:1], 1.0 / (2.0 * F * D), None, ALU.mult),
                   reads=[("zr",)], writes=[("zr",)])
                OP("vector", lambda e: e.tensor_scalar(scr[0:1, 1:2], scr[0:1, 1:2], 1.0 / (float(F) * D), None, ALU.mult),
                   reads=[("zr",)], writes=[("zr",)])
                OP("vector", lambda e: e.tensor_scalar_max(scr, scr, EPS_Q),
                   reads=[("zr",)], writes=[("zr",)])
                for i, dst, tk in ((0, sgu_pp, "sgu"), (1, swd_pp, "swd")):
                    OP("tensor", (lambda i: lambda e: e.matmul(
                        P[7][:, 0:1], onesr_f[:, :], scr[0:1, i:i + 1], start=True, stop=True))(i),
                       reads=[("zr",), ("onesr_f",)], writes=tP(7))
                    OP("vector", (lambda dst, tk: lambda e: e.tensor_copy(dst[:, :], P[7][:, 0:1]))(dst, tk),
                       reads=tP(7), writes=[(tk,)])
                OP("vector", lambda e: e.reciprocal(isgu_pp[:, :], sgu_pp[:, :]), reads=[("sgu",)], writes=[("isgu",)])
                OP("vector", lambda e: e.reciprocal(iswd_pp[:, :], swd_pp[:, :]), reads=[("swd",)],
                   writes=[("iswd",)], wcheck=[("agu",)])

        OP("scalar", lambda e: e.dma_start(
            out=kv_in[0:2048, :].rearrange("(h d) e -> d h e", d=128),
            in_=kvloc[:, :].rearrange("p (h e) -> p h e", h=16)),
           reads=[("kvloc", h) for h in range(H)], writes=[("kv_in", 0)], dma=("qa", 0))
        OP("scalar", lambda e: e.dma_start(
            out=kv_in[2048:2064, :].rearrange("h e -> (h e)").unsqueeze(0), in_=ksum_row),
           reads=[("ksum", ds) for ds in range(8)], writes=[("kv_in", 1)], dma=("qa", 0))
        OP("gpsimd", lambda e: e.collective_compute(
            "AllReduce", ALU.add, replica_groups=[list(range(NCORES))],
            ins=[kv_in.opt()], outs=[kv_out.opt()]),
           reads=[("kv_in", 0), ("kv_in", 1)], writes=[("kv_out",)], comp_sem=("cc3", 1))

        wq3 = wqT.rearrange("(kc p) d -> p kc d", p=128)

        import itertools as _it
        tern_drip = _it.chain(ternarize(wguT_sh, 256, tgu_in, isgu_pp, "tgin"),
                              ternarize(wdT_sh, 1024, twd_in, iswd_pp, "twin"))
        next(tern_drip, None)

        def wq_dma(dc):
            par = dc % 2
            blk = arbC_bf(par * 2048, 2048)
            OP("gpsimd", (lambda dc, blk: lambda e: e.dma_start(
                out=blk.rearrange("p (kc d) -> p kc d", kc=KC),
                in_=wq3[:, :, dc * 128:(dc + 1) * 128]))(dc, blk),
               writes=tC(par), dma=("qg", dc))
        wq_dma(0)
        for dc in range(KC):
            par = dc % 2
            blk = arbC_bf(par * 2048, 2048)
            if dc + 1 < KC:
                wq_dma(dc + 1)
            pb = 2 * (dc % 2)
            for n in range(2):
                for k in range(KC):
                    OP("tensor", (lambda n, k, blk, pb: lambda e: e.matmul(
                        P[pb + n][:, :], blk[:, k * 128:(k + 1) * 128], ha_c(k)[:, n * 512:(n + 1) * 512],
                        start=(k == 0), stop=(k == KC - 1)))(n, k, blk, pb),
                       reads=tA(k) + tC(par),
                       wcheck=tP(pb + n) if k == 0 else (), wpub=tP(pb + n) if k == KC - 1 else ())
            for n in range(2):
                pq = P[pb + n]
                t1 = sm_f[:, 2048 + n * 512:2048 + (n + 1) * 512]
                OP("scalar", (lambda dc, n, t1, pq: lambda e: e.activation(
                    t1, pq[:, :], AF.Relu, bias=bq_fm[:, dc:dc + 1]))(dc, n, t1, pq),
                   reads=tP(pb + n) + [("bq",)], writes=tSB(n))
                OP("vector", (lambda dc, n, t1: lambda e: e.tensor_mul(
                    q_h(dc)[:, n * 512:(n + 1) * 512], t1, dec_fm[:, n * 512:(n + 1) * 512]))(dc, n, t1),
                   reads=tSB(n) + [("dec_fm", n)], writes=tB(dc))
            for _ in range(3):
                next(tern_drip, None)
            if dc == 5:
                OP("gpsimd", lambda e: e.collective_compute(
                    "AllGather", ALU.bypass, replica_groups=[list(range(NCORES))],
                    ins=[tgu_in[0:128, :].opt()], outs=[tguA.opt()]),
                   reads=[("tgin", 0, cb) for cb in range(16)],
                   writes=[("tguA",)], comp_sem=("cc4", 1))
            if dc == 10:
                OP("gpsimd", lambda e: e.collective_compute(
                    "AllGather", ALU.bypass, replica_groups=[list(range(NCORES))],
                    ins=[tgu_in[128:256, :].opt()], outs=[tguB.opt()]),
                   reads=[("tgin", 1, cb) for cb in range(16)],
                   writes=[("tguB",)], comp_sem=("cc5", 1))
            if dc == 13:
                OP("gpsimd", lambda e: e.collective_compute(
                    "AllGather", ALU.bypass, replica_groups=[list(range(NCORES))],
                    ins=[twd_in[0:512, :].opt()], outs=[twdA.opt()]),
                   reads=[("twin", rb, cb) for rb in range(4) for cb in range(2)],
                   writes=[("twdA",)], comp_sem=("cc6", 1))
        for _ in tern_drip:
            pass
        OP("gpsimd", lambda e: e.collective_compute(
            "AllGather", ALU.bypass, replica_groups=[list(range(NCORES))],
            ins=[twd_in[512:1024, :].opt()], outs=[twdB.opt()]),
           reads=[("twin", rb, cb) for rb in range(4, 8) for cb in range(2)],
           writes=[("twdB",)], comp_sem=("cc7", 1))

        kvall = sm_bf[:, 0:2048]
        OP("gpsimd", lambda e: e.dma_start(
            out=kvall.rearrange("p (h e) -> p h e", h=16),
            in_=kv_out[0:2048, :].rearrange("(h d) e -> d h e", d=128)),
           reads=[("kv_out",)], writes=[("kvall",)] + tSA(0),
           wcheck=[("kvloc", h2) for h2 in range(H)], dma=("qg", 0))
        OP("gpsimd", lambda e: e.dma_start(out=ks16s[:, :], in_=kv_out[2048:2064, :]),
           reads=[("kv_out",)], writes=[("ks16",)], dma=("qg", 1))
        if dbg:
            OP("gpsimd", lambda e: e.dma_start(out=dbg_kv[:, :], in_=kv_out[:, :]),
               reads=[("kv_out",)], writes=[("dbgkv",)], dma=("qg", 0))
            OP("gpsimd", lambda e: e.dma_start(out=dbg_mod[:, :], in_=modf[:, :]),
               reads=[("modf",)], writes=[("dbgmod",)], dma=("qg", 0))
            for c in range(KC):
                OP("gpsimd", (lambda c: lambda e: e.dma_start(
                    out=dbg_ha.bitcast(BF16)[c * 128:(c + 1) * 128, :] if False else dbg_ha[c * 128:(c + 1) * 128, :],
                    in_=ha_c(c)))(c), reads=tA(c), writes=[("dbgha", c)], dma=("qg", 0))
                OP("gpsimd", (lambda c: lambda e: e.dma_start(
                    out=dbg_q[c * 128:(c + 1) * 128, :], in_=q_h(c)))(c),
                   reads=tB(c), writes=[("dbgq", c)], dma=("qg", 0))
        OP("tensor", lambda e: e.transpose(P[6][:, 0:16], ks16s[:, :], idn[0:16, 0:16]),
           reads=[("ks16",), ("idn",)], writes=tP(6))
        OP("vector", lambda e: e.tensor_copy(ksum_col[:, :], P[6][:, 0:16]), reads=tP(6), writes=[("kscol",)])

        for h in range(H):
            for nt in range(NT):
                OP("tensor", (lambda h, nt: lambda e: e.matmul(
                    P[7][:, h * 8 + nt:h * 8 + nt + 1], q_h(h)[:, nt * 128:(nt + 1) * 128],
                    ksum_col[:, h:h + 1], start=True, stop=True))(h, nt),
                   reads=tB(h) + [("kscol",)],
                   wcheck=tP(7) if (h == 0 and nt == 0) else (),
                   wpub=tP(7) if (h == H - 1 and nt == NT - 1) else ())
        OP("vector", lambda e: e.tensor_scalar(zr_tok[:, :], P[7][:, 0:128], EPS_ATT, None, ALU.add),
           reads=tP(7), writes=[("zr",)])
        OP("vector", lambda e: e.reciprocal(zr_tok[:, :], zr_tok[:, :]), reads=[("zr",)], writes=[("zr",)])

        zrow = rows_f[0:1, 2048:3072]
        for h in range(H):
            pa = 2 * (h % 2)
            for n in range(2):
                OP("tensor", (lambda h, n, pa: lambda e: e.matmul(
                    P[pa + n][:, :], kvall[:, h * 128:(h + 1) * 128], q_h(h)[:, n * 512:(n + 1) * 512],
                    start=True, stop=True))(h, n, pa),
                   reads=[("kvall",)] + tB(h), writes=tP(pa + n))
            for half in range(2):
                for j in range(4):
                    nt = half * 4 + j
                    OP("tensor", (lambda h, nt, half, j: lambda e: e.transpose(
                        P[6 + half][0:1, j * 128:(j + 1) * 128],
                        zr_tok[:, h * 8 + nt:h * 8 + nt + 1], idn[:, :]))(h, nt, half, j),
                       reads=[("zr",), ("idn",)],
                       wcheck=tP(6 + half) if j == 0 else (), wpub=tP(6 + half) if j == 3 else ())
                OP("scalar", (lambda half: lambda e: e.copy(
                    zrow[0:1, half * 512:(half + 1) * 512], P[6 + half][0:1, :]))(half),
                   reads=tP(6 + half), writes=tR(4 + half),
                   wcheck=[("ksum", ds2) for ds2 in range(8)] if (h == 0 and half == 0) else ())
            for n in range(2):
                OP("tensor", (lambda n: lambda e: e.matmul(
                    P[4 + n][:, :], onesr_f[:, :], zrow[0:1, n * 512:(n + 1) * 512],
                    start=True, stop=True))(n),
                   reads=tR(4 + n) + [("onesr_f",)], writes=tP(4 + n))
                zb = sm_f[:, 2048 + n * 512:2048 + (n + 1) * 512]
                OP("scalar", (lambda n, zb: lambda e: e.copy(zb, P[4 + n][:, :]))(n, zb),
                   reads=tP(4 + n), writes=tSB(n))
                OP("vector", (lambda h, n, zb, pa: lambda e: e.tensor_mul(
                    ha_c(h)[:, n * 512:(n + 1) * 512], P[pa + n][:, :], zb))(h, n, zb, pa),
                   reads=tP(pa + n) + tSB(n), writes=tA(h))

        wo3 = woT.rearrange("(kc p) d -> p kc d", p=128)

        def wo_dma(dc):
            par = dc % 2
            blk = arbC_bf(par * 2048, 2048)
            OP("gpsimd", (lambda dc, blk: lambda e: e.dma_start(
                out=blk.rearrange("p (kc d) -> p kc d", kc=KC),
                in_=wo3[:, :, dc * 128:(dc + 1) * 128]))(dc, blk),
               writes=tC(par), dma=("qg", dc))
        wo_dma(0)
        for dc in range(KC):
            par = dc % 2
            blk = arbC_bf(par * 2048, 2048)
            if dc + 1 < KC:
                wo_dma(dc + 1)
            pb = 2 * (dc % 2)
            for n in range(2):
                for k in range(KC):
                    OP("tensor", (lambda n, k, blk, pb: lambda e: e.matmul(
                        P[pb + n][:, :], blk[:, k * 128:(k + 1) * 128], ha_c(k)[:, n * 512:(n + 1) * 512],
                        start=(k == 0), stop=(k == KC - 1)))(n, k, blk, pb),
                       reads=tA(k) + tC(par),
                       wcheck=tP(pb + n) if k == 0 else (), wpub=tP(pb + n) if k == KC - 1 else ())
            for n in range(2):
                pq = P[pb + n]
                t1 = sm_f[:, 2048 + n * 512:2048 + (n + 1) * 512]
                OP("scalar", (lambda dc, n, t1, pq: lambda e: e.activation(
                    t1, pq[:, :], AF.Identity, bias=bo_fm[:, dc:dc + 1]))(dc, n, t1, pq),
                   reads=tP(pb + n) + [("bo",)], writes=tSB(n))
                OP("vector", (lambda dc, n, t1: lambda e: e.scalar_tensor_tensor(
                    ax_f[:, dc * T + n * 512:dc * T + (n + 1) * 512], t1, ga_t[:, dc:dc + 1],
                    ax_f[:, dc * T + n * 512:dc * T + (n + 1) * 512], ALU.mult, ALU.add))(dc, n, t1),
                   reads=tSB(n) + [("ga",)] + tX(dc), writes=tX(dc))
            ln_stats_chunk(dc, 4)

        if dbg:
            for c in range(KC):
                OP("gpsimd", (lambda c: lambda e: e.dma_start(
                    out=dbg_x1[c * 128:(c + 1) * 128, :], in_=ax_f[:, c * T:(c + 1) * T]))(c),
                   reads=tX(c), writes=[("dbgx1", c)], dma=("qg", 0))


        ln_stats_rows(4)
        mxb = sm_f[:, 4096:5120]
        ab = sm_f[:, 5120:6144]

        def hb_c(c):
            if c < 8:
                return arb_f[:, c * T:(c + 1) * T]
            return arbC_f((c - 8) * T, T)

        def hb_tok(c):
            if c < 8:
                return tA(2 * c, 2 * c + 1)
            return tC(c - 8)

        for c in range(KC):
            par = c % 2
            t1 = sm_f[:, 2048 + par * T:2048 + (par + 1) * T]
            hb = hb_c(c)
            OP("vector", (lambda c, t1: lambda e: e.tensor_sub(t1, ax_f[:, c * T:(c + 1) * T], mean_b))(c, t1),
               reads=tX(c) + tSA(0), writes=tSB(2 * par, 2 * par + 1))
            OP("vector", (lambda t1: lambda e: e.tensor_mul(t1, t1, rstd_b))(t1),
               reads=tSB(2 * par, 2 * par + 1) + tSA(1), writes=tSB(2 * par, 2 * par + 1))
            OP("scalar", (lambda c, t1, hb: lambda e: e.activation(
                hb, t1, AF.Identity, scale=oncb[:, c:c + 1], bias=sb_v[:, c:c + 1]))(c, t1, hb),
               reads=tSB(2 * par, 2 * par + 1) + [("oncb",), ("modf",)], writes=hb_tok(c))
            for n2 in range(2):
                abh = ab[:, n2 * 512:(n2 + 1) * 512]
                OP("scalar", (lambda hb, abh, n2: lambda e: e.activation(
                    abh, hb[:, n2 * 512:(n2 + 1) * 512], AF.Abs))(hb, abh, n2),
                   reads=hb_tok(c), writes=tSC(2 + n2),
                   wcheck=[("vsl", nt2) for nt2 in range(NT)] if c == 0 else ())
                if c == 0:
                    OP("vector", (lambda abh, n2: lambda e: e.tensor_copy(
                        mxb[:, n2 * 512:(n2 + 1) * 512], abh))(abh, n2),
                       reads=tSC(2 + n2), writes=tSC(n2),
                       wcheck=[("ksl", nt2) for nt2 in range(NT)])
                else:
                    OP("vector", (lambda abh, n2: lambda e: e.tensor_tensor(
                        mxb[:, n2 * 512:(n2 + 1) * 512], mxb[:, n2 * 512:(n2 + 1) * 512],
                        abh, ALU.max))(abh, n2),
                       reads=tSC(2 + n2) + tSC(n2), writes=tSC(n2))
        for nt in range(NT):
            OP("tensor", (lambda nt: lambda e: e.transpose(
                P[6][:, 0:128], mxb[:, nt * 128:(nt + 1) * 128], idn[:, :]))(nt),
               reads=tSC(0, 1) + [("idn",)], writes=tP(6))
            OP("vector", (lambda nt: lambda e: e.tensor_reduce(
                rm1_tok[:, nt:nt + 1], P[6][:, 0:128], AX, ALU.max))(nt),
               reads=tP(6), writes=[("rm1",)] if nt == NT - 1 else [("rm1",)])
        OP("vector", lambda e: e.tensor_scalar_max(rm1_tok[:, :], rm1_tok[:, :], EPS_Q),
           reads=[("rm1",)], writes=[("rm1",)])
        OP("vector", lambda e: e.reciprocal(s1_tok[:, :], rm1_tok[:, :]), reads=[("rm1",)], writes=[("s1",)])
        OP("vector", lambda e: e.tensor_scalar(s1_tok[:, :], s1_tok[:, :], 127.0, None, ALU.mult),
           reads=[("s1",)], writes=[("s1",)])
        s1row = rows_f[0:1, 2048:3072]
        for half in range(2):
            for j in range(4):
                nt = half * 4 + j
                OP("tensor", (lambda nt, half, j: lambda e: e.transpose(
                    P[4 + half][0:1, j * 128:(j + 1) * 128], s1_tok[:, nt:nt + 1], idn[:, :]))(nt, half, j),
                   reads=[("s1",), ("idn",)],
                   wcheck=tP(4 + half) if j == 0 else (), wpub=tP(4 + half) if j == 3 else ())
            OP("vector", (lambda half: lambda e: e.tensor_copy(
                s1row[0:1, half * 512:(half + 1) * 512], P[4 + half][0:1, :]))(half),
               reads=tP(4 + half), writes=tR(4 + half))
        S1_b = sm_f[:, 4096:5120]
        for half in range(2):
            OP("tensor", (lambda half: lambda e: e.matmul(
                P[4 + half][:, :], onesr_f[:, :], s1row[0:1, half * 512:(half + 1) * 512],
                start=True, stop=True))(half),
               reads=tR(4 + half) + [("onesr_f",)], writes=tP(4 + half))
            OP("vector", (lambda half: lambda e: e.tensor_copy(
                S1_b[:, half * 512:(half + 1) * 512], P[4 + half][:, :]))(half),
               reads=tP(4 + half), writes=tSC(half) + [("s1b",)])
        i8q = sm_i8[:, 20480:21504]
        for c in range(KC):
            hb = hb_c(c)
            par = c % 2
            i8qp = sm_i8[:, 20480 + par * 1024:21504 + par * 1024]
            OP("vector", (lambda hb, i8qp: lambda e: e.tensor_mul(i8qp, hb, S1_b))(hb, i8qp),
               reads=hb_tok(c) + [("s1b",)], writes=[("i8q", par)] + tSC(2 + par))
            OP("scalar", (lambda c, i8qp: lambda e: e.activation(q_h(c), i8qp, AF.Copy))(c, i8qp),
               reads=[("i8q", par)], writes=tB(c))

        OP("vector", lambda e: e.tensor_scalar(u_tok[:, :], rm1_tok[:, :], sgu_pp[:, 0:1], None, ALU.mult),
           reads=[("rm1",), ("sgu",)], writes=[("sk_tok",)])
        OP("vector", lambda e: e.tensor_scalar(u_tok[:, :], u_tok[:, :], 1.0 / 127.0, None, ALU.mult),
           reads=[("sk_tok",)], writes=[("sk_tok",)])
        skrow = rows_f[0:1, 3072:4096]
        for half in range(2):
            for j in range(4):
                nt = half * 4 + j
                OP("tensor", (lambda nt, half, j: lambda e: e.transpose(
                    P[4 + half][0:1, j * 128:(j + 1) * 128], u_tok[:, nt:nt + 1], idn[:, :]))(nt, half, j),
                   reads=[("sk_tok",), ("idn",)],
                   wcheck=tP(4 + half) if j == 0 else (), wpub=tP(4 + half) if j == 3 else ())
            OP("vector", (lambda half: lambda e: e.tensor_copy(
                skrow[0:1, half * 512:(half + 1) * 512], P[4 + half][0:1, :]))(half),
               reads=tP(4 + half), writes=tR(6 + half))
        sk_b = sm_f[:, 0:T]
        for half in range(2):
            OP("tensor", (lambda half: lambda e: e.matmul(
                P[4 + half][:, :], onesr_f[:, :], skrow[0:1, half * 512:(half + 1) * 512],
                start=True, stop=True))(half),
               reads=tR(6 + half) + [("onesr_f",)], writes=tP(4 + half))
            OP("vector", (lambda half: lambda e: e.tensor_copy(
                sk_b[:, half * 512:(half + 1) * 512], P[4 + half][:, :]))(half),
               reads=tP(4 + half), writes=tSA(0) + [("skb",)])

        if dbg:
            for c in range(KC):
                OP("gpsimd", (lambda c: lambda e: e.dma_start(
                    out=dbg_xq[c * 128:(c + 1) * 128, :], in_=q_h(c)))(c),
                   reads=tB(c), writes=[("dbgxq", c)], dma=("qg", 0))

        def x1t(nt):
            if nt < 4:
                return arb_f[:, nt * 2048:(nt + 1) * 2048]
            return arbC_f((nt - 4) * 2048, 2048)

        def x1t_tok(nt):
            if nt < 4:
                return tA(4 * nt, 4 * nt + 1, 4 * nt + 2, 4 * nt + 3)
            return tC(2 * (nt - 4), 2 * (nt - 4) + 1)

        for c in range(KC):
            for nt in range(NT):
                pidx = nt % 4
                OP("tensor", (lambda c, nt, pidx: lambda e: e.transpose(
                    P[pidx][:, 0:128], ax_f[:, c * T + nt * 128:c * T + (nt + 1) * 128], idn[:, :]))(c, nt, pidx),
                   reads=tX(c) + [("idn",)], writes=tP(pidx))
                if nt % 2 == 0:
                    OP("scalar", (lambda c, nt, pidx: lambda e: e.copy(
                        x1t(nt)[:, c * 128:(c + 1) * 128], P[pidx][:, 0:128]))(c, nt, pidx),
                       reads=tP(pidx), writes=x1t_tok(nt) if c == 0 else [x1t_tok(nt)[c % len(x1t_tok(nt))]])
                else:
                    OP("vector", (lambda c, nt, pidx: lambda e: e.tensor_copy(
                        x1t(nt)[:, c * 128:(c + 1) * 128], P[pidx][:, 0:128]))(c, nt, pidx),
                       reads=tP(pidx), writes=x1t_tok(nt) if c == 0 else [x1t_tok(nt)[c % len(x1t_tok(nt))]])

        tguA3 = tguA.rearrange("(c p) f -> p c f", p=128)
        tguB3 = tguB.rearrange("(c p) f -> p c f", p=128)
        mxt = sm_f[:, 1024:2048]
        abt = ax_f[:, 11264:12288]
        def gu_dma(g2):
            par = g2 % 2
            ablk = ax_bf[:, par * 4096:par * 4096 + 4096]
            bblk = ax_bf[:, 8192 + par * 4096:8192 + par * 4096 + 4096]
            for hf, (src, tok) in enumerate(((tguA3, "tguA"), (tguB3, "tguB"))):
                OP("gpsimd", (lambda g2, ablk, hf, src: lambda e: e.dma_start(
                    out=ablk[:, hf * 2048:(hf + 1) * 2048].rearrange("p (c f) -> p c f", c=8),
                    in_=src[:, :, g2 * 256:(g2 + 1) * 256]))(g2, ablk, hf, src),
                   reads=[(tok,)], writes=tX(2 * par + hf), dma=("qg", g2))
                OP("gpsimd", (lambda g2, bblk, hf, src: lambda e: e.dma_start(
                    out=bblk[:, hf * 2048:(hf + 1) * 2048].rearrange("p (c f) -> p c f", c=8),
                    in_=src[:, :, F + g2 * 256:F + (g2 + 1) * 256]))(g2, bblk, hf, src),
                   reads=[(tok,)], writes=tX(4 + 2 * par + hf), dma=("qg", g2))
        gu_dma(0)
        for g2 in range(32):
            par = g2 % 2
            ablk = ax_bf[:, par * 4096:par * 4096 + 4096]
            bblk = ax_bf[:, 8192 + par * 4096:8192 + par * 4096 + 4096]
            if g2 + 1 < 32:
                gu_dma(g2 + 1)
            for sub in range(2):
                fc = 2 * g2 + sub
                pa = (P[0], P[1]) if sub == 0 else (P[2], P[3])
                pb = (P[4], P[5]) if sub == 0 else (P[6], P[7])
                for n in range(2):
                    for k in range(KC):
                        off = (k % 2) * 2048 + (k // 2) * 256 + sub * 128
                        OP("tensor", (lambda sub, n, k, off, ablk, pa: lambda e: e.matmul(
                            pa[n][:, :], ablk[:, off:off + 128],
                            q_h(k)[:, n * 512:(n + 1) * 512], start=(k == 0), stop=(k == KC - 1)))(sub, n, k, off, ablk, pa),
                           reads=tX(2 * par, 2 * par + 1) + tB(k),
                           wcheck=tP(P.index(pa[n])) if k == 0 else (),
                           wpub=tP(P.index(pa[n])) if k == KC - 1 else ())
                for n in range(2):
                    for k in range(KC):
                        off = (k % 2) * 2048 + (k // 2) * 256 + sub * 128
                        OP("tensor", (lambda sub, n, k, off, bblk, pb: lambda e: e.matmul(
                            pb[n][:, :], bblk[:, off:off + 128],
                            q_h(k)[:, n * 512:(n + 1) * 512], start=(k == 0), stop=(k == KC - 1)))(sub, n, k, off, bblk, pb),
                           reads=tX(4 + 2 * par, 5 + 2 * par) + tB(k),
                           wcheck=tP(P.index(pb[n])) if k == 0 else (),
                           wpub=tP(P.index(pb[n])) if k == KC - 1 else ())
                tpar = fc % 2
                tst = ax_f[:, 8192 + tpar * T:8192 + (tpar + 1) * T]
                sil = ax_f[:, 10240:11264]
                for n in range(2):
                    OP("vector", (lambda n, tst, pa: lambda e: e.tensor_mul(
                        tst[:, n * 512:(n + 1) * 512], pa[n][:, :], sk_b[:, n * 512:(n + 1) * 512]))(n, tst, pa),
                       reads=tP(P.index(pa[n])) + [("skb",)], writes=tX(8 + tpar))
                OP("scalar", (lambda tst, sil: lambda e: e.activation(sil, tst, AF.Silu))(tst, sil),
                   reads=tX(8 + tpar), writes=tX(10))
                for n in range(2):
                    OP("vector", (lambda n, tst, sil, pb: lambda e: e.tensor_mul(
                        tst[:, n * 512:(n + 1) * 512], sil[:, n * 512:(n + 1) * 512], pb[n][:, :]))(n, tst, sil, pb),
                       reads=tX(10) + tP(P.index(pb[n])), writes=tX(8 + tpar))
                OP("scalar", (lambda tst: lambda e: e.activation(abt, tst, AF.Abs))(tst),
                   reads=tX(8 + tpar), writes=tX(11))
                if g2 == 0 and sub == 0:
                    OP("vector", lambda e: e.tensor_copy(mxt, abt),
                       reads=tX(11), writes=tSA(1))
                else:
                    OP("vector", lambda e: e.tensor_tensor(mxt, mxt, abt, ALU.max),
                       reads=tX(11) + tSA(1), writes=tSA(1))
                OP("scalar", (lambda fc, tst: lambda e: e.dma_start(
                    out=t_buf[fc * 128:(fc + 1) * 128, :], in_=tst))(fc, tst),
                   reads=tX(8 + tpar), writes=[("tb", fc)], dma=("qa", fc))

        if dbg:
            OP("gpsimd", lambda e: e.dma_start(out=dbg_t[:, :], in_=t_buf[:, :]),
               reads=[("tb", fc2) for fc2 in range(64)], writes=[("dbgt",)], dma=("qg", 0))
            OP("gpsimd", lambda e: e.dma_start(out=dbg_tgin[:, :], in_=tgu_in[:, :]),
               reads=[("tgin", rb, cb) for rb in range(2) for cb in range(16)],
               writes=[("dbgtgin",)], dma=("qg", 0))
        for nt in range(NT):
            OP("tensor", (lambda nt: lambda e: e.transpose(
                P[6][:, 0:128], mxt[:, nt * 128:(nt + 1) * 128], idn[:, :]))(nt),
               reads=tSA(1) + [("idn",)], writes=tP(6))
            OP("vector", (lambda nt: lambda e: e.tensor_reduce(
                mx_tok[:, nt:nt + 1], P[6][:, 0:128], AX, ALU.max))(nt),
               reads=tP(6), writes=[("mxt",)])
        OP("vector", lambda e: e.tensor_mul(fs_tok[:, :], u_tok[:, :], mx_tok[:, :]),
           reads=[("sk_tok",), ("mxt",)], writes=[("rm2",)])
        OP("vector", lambda e: e.tensor_scalar_max(fs_tok[:, :], fs_tok[:, :], EPS_Q),
           reads=[("rm2",)], writes=[("rm2",)])
        OP("vector", lambda e: e.reciprocal(mx_tok[:, :], fs_tok[:, :]), reads=[("rm2",)], writes=[("irm2",)])
        OP("vector", lambda e: e.tensor_mul(mx_tok[:, :], mx_tok[:, :], u_tok[:, :]),
           reads=[("irm2",), ("sk_tok",)], writes=[("irm2",)])
        OP("vector", lambda e: e.tensor_scalar(mx_tok[:, :], mx_tok[:, :], 127.0, None, ALU.mult),
           reads=[("irm2",)], writes=[("uq",)])
        OP("vector", lambda e: e.tensor_scalar(fs_tok[:, :], fs_tok[:, :], swd_pp[:, 0:1], None, ALU.mult),
           reads=[("rm2",), ("swd",)], writes=[("fs",)])
        OP("vector", lambda e: e.tensor_scalar(fs_tok[:, :], fs_tok[:, :], 1.0 / 127.0, None, ALU.mult),
           reads=[("fs",)], writes=[("fs",)])
        urow = rows_f[0:1, 2048:3072]
        for half in range(2):
            for j in range(4):
                nt = half * 4 + j
                OP("tensor", (lambda nt, half, j: lambda e: e.transpose(
                    P[4 + half][0:1, j * 128:(j + 1) * 128], mx_tok[:, nt:nt + 1], idn[:, :]))(nt, half, j),
                   reads=[("uq",), ("idn",)],
                   wcheck=tP(4 + half) if j == 0 else (), wpub=tP(4 + half) if j == 3 else ())
            OP("vector", (lambda half: lambda e: e.tensor_copy(
                urow[0:1, half * 512:(half + 1) * 512], P[4 + half][0:1, :]))(half),
               reads=tP(4 + half), writes=tR(4 + half))
        c2_b = sm_f[:, 0:T]
        for half in range(2):
            OP("tensor", (lambda half: lambda e: e.matmul(
                P[4 + half][:, :], onesr_f[:, :], urow[0:1, half * 512:(half + 1) * 512],
                start=True, stop=True))(half),
               reads=tR(4 + half) + [("onesr_f",)], writes=tP(4 + half))
            OP("vector", (lambda half: lambda e: e.tensor_copy(
                c2_b[:, half * 512:(half + 1) * 512], P[4 + half][:, :]))(half),
               reads=tP(4 + half), writes=tSA(0) + [("c2b",)], wcheck=[("skb",)] if half == 0 else ())

        gbrow = rows_f[0:1, 0:2048]
        gb_b = sm_f[:, 4096:6144]
        for q4 in range(4):
            for j in range(4):
                OP("tensor", (lambda q4, j: lambda e: e.transpose(
                    P[4 + (q4 % 2)][0:1, j * 128:(j + 1) * 128],
                    gb_t[:, q4 * 4 + j:q4 * 4 + j + 1], idn[:, :]))(q4, j),
                   reads=[("gb",), ("idn",)],
                   wcheck=tP(4 + (q4 % 2)) if j == 0 else (), wpub=tP(4 + (q4 % 2)) if j == 3 else ())
            OP("vector", (lambda q4: lambda e: e.tensor_copy(
                gbrow[0:1, q4 * 512:(q4 + 1) * 512], P[4 + (q4 % 2)][0:1, :]))(q4),
               reads=tP(4 + (q4 % 2)), writes=tR(q4))
        for q4 in range(4):
            OP("tensor", (lambda q4: lambda e: e.matmul(
                P[4 + (q4 % 2)][:, :], onesr_f[:, :], gbrow[0:1, q4 * 512:(q4 + 1) * 512],
                start=True, stop=True))(q4),
               reads=tR(q4) + [("onesr_f",)], writes=tP(4 + (q4 % 2)))
            OP("vector", (lambda q4: lambda e: e.tensor_copy(
                gb_b[:, q4 * 512:(q4 + 1) * 512], P[4 + (q4 % 2)][:, :]))(q4),
               reads=tP(4 + (q4 % 2)), writes=tSC(q4) + [("gbb",)])

        i8d = sm_i8[:, 13312:14336]
        twdA3 = twdA.rearrange("(cb p) d -> p cb d", p=128)
        twdB3 = twdB.rearrange("(cb p) d -> p cb d", p=128)
        for st2 in range(2):
            def tq_dma(k, st2=st2):
                par = k % 2
                tq = sm_f[:, 2048 + par * 512:2048 + (par + 1) * 512]
                OP("sync", (lambda k, st2, tq: lambda e: e.dma_start(
                    out=tq, in_=t_buf[k * 128:(k + 1) * 128, st2 * 512:(st2 + 1) * 512]))(k, st2, tq),
                   reads=[("tb", k)], writes=tSB(par), dma=("qs", 2 + k % 2))
            tq_dma(0)
            for k in range(F // 128):
                par = k % 2
                tq = sm_f[:, 2048 + par * 512:2048 + (par + 1) * 512]
                if k + 1 < F // 128:
                    tq_dma(k + 1)
                OP("vector", (lambda k, tq, st2: lambda e: e.tensor_mul(
                    i8d[:, 0:512], tq, c2_b[:, st2 * 512:(st2 + 1) * 512]))(k, tq, st2),
                   reads=tSB(par) + [("c2b",)], writes=tSB(2))
                OP("scalar", (lambda k, st2: lambda e: e.activation(
                    ax_bf[:, k * 512:(k + 1) * 512], i8d[:, 0:512], AF.Copy))(k, st2),
                   reads=tSB(2), writes=tX(k // 4))
            def wd_dma(j, st2=st2):
                dp, kb = divmod(j, 8)
                par = j % 2
                wblk3 = arbB_bf(par * 8192, 8192).rearrange("p (kb d) -> p kb d", kb=8)
                for hf, src in enumerate((twdA3, twdB3)):
                    OP("gpsimd", (lambda dp, kb, wblk3, hf, src: lambda e: e.dma_start(
                        out=wblk3[:, hf * 4:(hf + 1) * 4, :],
                        in_=src[:, kb * 4:(kb + 1) * 4, dp * 1024:(dp + 1) * 1024]))(dp, kb, wblk3, hf, src),
                       reads=[("twdA",) if hf == 0 else ("twdB",)],
                       writes=tB(*range(8 * par + 4 * hf, 8 * par + 4 * hf + 4)), dma=("qg", j))
            wd_dma(0)
            for j in range(16):
                dp, kb = divmod(j, 8)
                par = j % 2
                wblk = arbB_bf(par * 8192, 8192)
                if j + 1 < 16:
                    wd_dma(j + 1)
                for nt in range(4):
                    for d2 in range(2):
                        for kk in range(8):
                            k = kb * 8 + kk
                            OP("tensor", (lambda st2, nt, k, kk, d2, wblk: lambda e: e.matmul(
                                P[nt * 2 + d2][:, :], ax_bf[:, k * 512 + nt * 128:k * 512 + nt * 128 + 128],
                                wblk[:, kk * 1024 + d2 * 512:kk * 1024 + (d2 + 1) * 512],
                                start=(k == 0), stop=(k == 63)))(st2, nt, k, kk, d2, wblk),
                               reads=tX(k // 4) + tB(*range(8 * par, 8 * par + 8)),
                               wcheck=tP(nt * 2 + d2) if k == 0 else (),
                               wpub=tP(nt * 2 + d2) if k == 63 else ())
                if kb == 7:
                    dtmp = sm_f[:, 3584:4096]
                    for nt in range(4):
                        ntg = st2 * 4 + nt
                        for d2 in range(2):
                            dsl = dp * 2 + d2
                            OP("vector", (lambda nt, ntg, dsl, d2: lambda e: e.scalar_tensor_tensor(
                                dtmp, P[nt * 2 + d2][:, :], fs_tok[:, ntg:ntg + 1],
                                gb_b[:, dsl * 512:(dsl + 1) * 512], ALU.mult, ALU.mult))(nt, ntg, dsl, d2),
                               reads=tP(nt * 2 + d2) + [("fs",), ("gbb",)], writes=tSB(3))
                            OP("vector", (lambda ntg, dsl: lambda e: e.tensor_add(
                                x1t(ntg)[:, dsl * 512:(dsl + 1) * 512],
                                x1t(ntg)[:, dsl * 512:(dsl + 1) * 512], dtmp))(ntg, dsl),
                               reads=tSB(3) + x1t_tok(ntg), writes=x1t_tok(ntg))

        for ntg in range(NT):
            OP("scalar", (lambda ntg: lambda e: e.dma_start(
                out=y_out[ntg * 128:(ntg + 1) * 128, :], in_=x1t(ntg)))(ntg),
               reads=x1t_tok(ntg), writes=[("y", ntg)], dma=("qa", ntg))
        b.barrier("sync", [("y", ntg) for ntg in range(NT)])

        b.finalize()
        block.sync(b.body("sync", sems))
        block.vector(b.body("vector", sems))
        block.scalar(b.body("scalar", sems))
        block.tensor(b.body("tensor", sems))
        block.gpsimd(b.body("gpsimd", sems))

    return nc


_CACHE = {}


def kernel(L0_attn_x, L0_attn_Wq, L0_attn_bq, L0_attn_Wk, L0_attn_bk, L0_attn_Wv,
           L0_attn_bv, L0_attn_Wo, L0_attn_bo, L0_ffn_w_gate_up, L0_ffn_w_down,
           L0_c, L0_wmod):
    x = np.ascontiguousarray(np.asarray(L0_attn_x, dtype=np.float32).reshape(SEQ, D))
    wqT = np.ascontiguousarray(np.asarray(L0_attn_Wq, dtype=np.float32).T)
    wkT = np.ascontiguousarray(np.asarray(L0_attn_Wk, dtype=np.float32).T)
    wvT = np.ascontiguousarray(np.asarray(L0_attn_Wv, dtype=np.float32).T)
    woT = np.ascontiguousarray(np.asarray(L0_attn_Wo, dtype=np.float32).T)
    wguT = np.ascontiguousarray(np.asarray(L0_ffn_w_gate_up, dtype=np.float32).T)
    wdT = np.ascontiguousarray(np.asarray(L0_ffn_w_down, dtype=np.float32).T)
    wmod = np.ascontiguousarray(np.asarray(L0_wmod, dtype=np.float32))
    cvec = np.asarray(L0_c, dtype=np.float32).reshape(D)
    bq = np.asarray(L0_attn_bq, dtype=np.float32).reshape(D)
    bk = np.asarray(L0_attn_bk, dtype=np.float32).reshape(D)
    bv = np.asarray(L0_attn_bv, dtype=np.float32).reshape(D)
    bo = np.asarray(L0_attn_bo, dtype=np.float32).reshape(D)

    cT = np.ascontiguousarray(cvec.reshape(KC, 128).T)
    bq_fm = np.ascontiguousarray(bq.reshape(KC, 128).T)
    bo_fm = np.ascontiguousarray(bo.reshape(KC, 128).T)
    bk_row = bk.reshape(1, D)
    bv_row = bv.reshape(1, D)

    if "nc" not in _CACHE:
        _CACHE["nc"] = build_nc()
    nc = _CACHE["nc"]

    in_maps = []
    for c in range(NCORES):
        xs = np.ascontiguousarray(x[c * T:(c + 1) * T, :].T)
        in_maps.append({
            "x_fm": xs,
            "wqT": wqT, "wkT": wkT, "wvT": wvT, "woT": woT,
            "wguT_sh": np.ascontiguousarray(wguT[c * 256:(c + 1) * 256, :]),
            "wdT_sh": np.ascontiguousarray(wdT[c * 1024:(c + 1) * 1024, :]),
            "wmod_sh": np.ascontiguousarray(wmod[:, c * 1536:(c + 1) * 1536]),
            "cT": cT, "bq_fm": bq_fm, "bo_fm": bo_fm,
            "bk_row": bk_row, "bv_row": bv_row,
            "idx": np.arange(c * T, (c + 1) * T, dtype=np.float32).reshape(1, T),
        })
    res = run_bass_kernel_spmd(nc, in_maps, core_ids=list(range(NCORES)))
    out = np.concatenate([np.asarray(r["y"]) for r in res.results], axis=0)
    return out.reshape(1, SEQ, D).astype(np.float32)
